# version 09
# baseline (speedup 1.0000x reference)
import math
import numpy as np
import ml_dtypes

import concourse.bass as bass
import concourse.mybir as mybir
from concourse.bass_utils import run_bass_kernel_spmd

F32 = mybir.dt.float32
BF16 = mybir.dt.bfloat16
I32 = mybir.dt.int32
AF = mybir.ActivationFunctionType
OP = mybir.AluOpType
AX = mybir.AxisListType
BF = ml_dtypes.bfloat16

NEG = -1.0e30
SC0 = float(1.0 / np.float32(np.sqrt(128.0)))
SC1 = float(np.float64(192.0) ** -0.5)
INV127 = float(1.0 / 127.0)

G_ROWS, G_COLS = 256, 5504
D_ROWS, D_COLS = 688, 2048
TU_ROWS, TU_COLS = 256, 4096
TD_ROWS, TD_COLS = 512, 2048
OFF_G = 0
OFF_U = OFF_G + G_ROWS * G_COLS
OFF_D = OFF_U + G_ROWS * G_COLS
OFF_TU = OFF_D + D_ROWS * D_COLS
OFF_TD = OFF_TU + TU_ROWS * TU_COLS
PACK = OFF_TD + TD_ROWS * TD_COLS


class SemC:
    def __init__(self, nc, name):
        self.h = nc.semaphore(name).__enter__()
        self.n = 0

    def inc(self, instr, k=1):
        instr.then_inc(self.h, k)
        self.n += k
        return instr

    def wait(self, eng):
        if self.n:
            eng.wait_ge(self.h, self.n)

    def wait_at(self, eng, val):
        if val:
            eng.wait_ge(self.h, val)


class Psum:

    def __init__(self, buf):
        self.buf = buf
        self.dep = None

    def acq(self, eng):
        if self.dep is not None:
            self.dep[0].wait_at(eng, self.dep[1])
            self.dep = None

    def rel(self, sem):
        self.dep = (sem, sem.n)


def build_program(stop=None):
    nc = bass.Bass("TRN2", target_bir_lowering=False, debug=False)

    def din(name, shape, dt=F32):
        return nc.dram_tensor(name, shape, dt, kind="ExternalInput")

    io = {}
    io["w_qc"] = din("w_qc", [2048, 512])
    io["w_qup"] = din("w_qup", [512, 2048])
    io["kvc_w"] = din("kvc_w", [2048, 128])
    io["kvc_wz"] = din("kvc_wz", [2048, 128])
    io["k_proj_w"] = din("k_proj_w", [2048, 128])
    io["v_proj_w"] = din("v_proj_w", [2048, 128])
    io["idx_c_w"] = din("idx_c_w", [2048, 64])
    io["idx_c_wz"] = din("idx_c_wz", [2048, 64])
    io["w_dqT"] = din("w_dqT", [64, 2048])
    io["w_iuq"] = din("w_iuq", [64, 256])
    io["w_w"] = din("w_w", [2048, 4])
    io["group_w"] = din("group_w", [2048, 512])
    io["final_w"] = din("final_w", [2048, 2048])
    io["Wqa"] = din("Wqa", [2048, 1536])
    io["Wqb"] = din("Wqb", [1536, 3072])
    io["Wkva"] = din("Wkva", [2048, 576])
    io["Wkvb_n"] = din("Wkvb_n", [512, 2048])
    io["Wkvb_v"] = din("Wkvb_v", [512, 2048])
    io["Wo"] = din("Wo", [2048, 2048])
    io["w_up"] = din("w_up", [4096, 2048])
    io["w_down"] = din("w_down", [2048, 4096])
    io["gidxT"] = din("gidxT", [256, 5504], I32)
    io["uidxT"] = din("uidxT", [256, 5504], I32)
    io["didxT"] = din("didxT", [688, 2048], I32)
    io["wupT"] = din("wupT", [256, 4096])
    io["wdnT"] = din("wdnT", [512, 2048])
    io["lutcols"] = din("lutcols", [128, 16])
    io["eye_f"] = din("eye_f", [128, 128])
    io["had_f"] = din("had_f", [128, 128])
    io["qnw_rep"] = din("qnw_rep", [128, 2048])
    io["knw_rep"] = din("knw_rep", [128, 128])
    io["knw_col"] = din("knw_col", [128, 1])
    io["qalnw_rep"] = din("qalnw_rep", [128, 1536])
    io["kvalnw_rep"] = din("kvalnw_rep", [128, 512])
    io["kvcb_rep"] = din("kvcb_rep", [128, 128])
    io["idxb_rep"] = din("idxb_rep", [64, 128])
    io["cs0own"] = din("cs0own", [128, 1024])
    io["cs0prv"] = din("cs0prv", [128, 1024])
    io["cs1q"] = din("cs1q", [128, 1024])
    io["cs1k"] = din("cs1k", [4096, 64])
    io["spmask"] = din("spmask", [128, 4096])
    io["spflag"] = din("spflag", [128, 16])
    io["wm0"] = din("wm0", [128, 384])
    io["wmN"] = din("wmN", [128, 384])
    io["l1mA"] = din("l1mA", [128, 512])
    io["l1mB"] = din("l1mB", [128, 512])
    io["x_own"] = din("x_own", [2048, 2048])
    io["x_prev"] = din("x_prev", [2048, 2048])
    io["x_full"] = din("x_full", [4096, 2048])

    out = nc.dram_tensor("out", [2048, 2048], F32, kind="ExternalOutput")

    d_pack_src = nc.dram_tensor("d_pack_src", [1, PACK], BF16)
    d_pack = nc.dram_tensor("d_pack", [8, PACK], BF16)
    d_ckv_src = nc.dram_tensor("d_ckv_src", [2048, 576], F32)
    d_ckv = nc.dram_tensor("d_ckv", [2, 2048, 576], F32)
    d_x1 = nc.dram_tensor("d_x1", [2048, 2048], F32)
    d_oT = nc.dram_tensor("d_oT", [16, 128, 2048], BF16)
    d_qaT = nc.dram_tensor("d_qaT", [12, 128, 2048], BF16)
    d_kT = nc.dram_tensor("d_kT", [16, 128, 4096], BF16)
    d_V = nc.dram_tensor("d_V", [16, 4096, 128], BF16)
    d_qT = nc.dram_tensor("d_qT", [16, 192, 2048], BF16)
    d_xn0T = nc.dram_tensor("d_xn0T", [16, 128, 2048], BF16)
    d_xq1 = nc.dram_tensor("d_xq1", [16, 128, 2048], BF16)
    d_h2T = nc.dram_tensor("d_h2T", [32, 128, 2048], BF16)
    d_xq2 = nc.dram_tensor("d_xq2", [32, 128, 2048], BF16)

    t_done = SemC(nc, "t_done")
    v_done = SemC(nc, "v_done")
    s_done = SemC(nc, "s_done")
    g_done = SemC(nc, "g_done")
    q_dma = SemC(nc, "q_dma")
    g_dma = SemC(nc, "g_dma")
    st_dma = SemC(nc, "st_dma")
    cc_sem = SemC(nc, "cc_sem")
    cc_sem2 = SemC(nc, "cc_sem2")

    ALL = {}

    def sb(name, shape, dt):
        h = nc.sbuf_tensor(name, shape, dt).__enter__()
        ALL[name] = h
        return h

    c_eye_f = sb("c_eye_f", [128, 128], F32)
    c_eye_b = sb("c_eye_b", [128, 128], BF16)
    c_had_b = sb("c_had_b", [128, 128], BF16)
    c_lut = sb("c_lut", [128, 16], F32)
    c_scales = sb("c_scales", [128, 2], F32)
    c_knwc = sb("c_knwc", [128, 1], F32)
    c_ones = sb("c_ones", [128, 512], F32)

    PA = Psum(nc.psum_tensor("PA", [128, 512], F32).__enter__())
    PB = Psum(nc.psum_tensor("PB", [128, 512], F32).__enter__())
    PC = Psum(nc.psum_tensor("PC", [128, 512], F32).__enter__())
    PD = Psum(nc.psum_tensor("PD", [128, 512], F32).__enter__())
    PE2 = Psum(nc.psum_tensor("PE2", [128, 512], F32).__enter__())
    PF = Psum(nc.psum_tensor("PF", [128, 512], F32).__enter__())
    PTa = Psum(nc.psum_tensor("PTa", [128, 1024], BF16).__enter__())
    PTb = Psum(nc.psum_tensor("PTb", [128, 1024], BF16).__enter__())

    block = nc.Block().__enter__()

    def barrier():
        @block.vector
        def _(v):
            v.drain()
            v_done.inc(v.nop())
        @block.scalar
        def _(s):
            s.drain()
            s_done.inc(s.nop())
        @block.gpsimd
        def _(g):
            g_done.inc(g.nop())
        tv, sv, gv, ttv = v_done.n, s_done.n, g_done.n, t_done.n
        qv, gdv, stv = q_dma.n, g_dma.n, st_dma.n

        def wait_all(eng, skip):
            for sem, val in ((t_done, ttv), (v_done, tv), (s_done, sv),
                             (g_done, gv), (q_dma, qv), (g_dma, gdv),
                             (st_dma, stv)):
                if sem is not skip:
                    sem.wait_at(eng, val)
        @block.tensor
        def _(t):
            wait_all(t, t_done)
        @block.vector
        def _(v):
            wait_all(v, v_done)
        @block.scalar
        def _(s):
            wait_all(s, s_done)
        @block.gpsimd
        def _(g):
            wait_all(g, g_done)
        @block.sync
        def _(q):
            wait_all(q, None)

    def loadq(dst_ap, src_ap, wait=()):
        @block.sync
        def _(q):
            for sem, val in wait:
                sem.wait_at(q, val)
            q_dma.inc(q.dma_start(out=dst_ap, in_=src_ap), 16)

    def loadg(dst_ap, src_ap, wait=()):
        @block.gpsimd
        def _(g):
            for sem, val in wait:
                sem.wait_at(g, val)
            g_dma.inc(g.dma_start(out=dst_ap, in_=src_ap), 16)

    def storeg(dst_ap, src_ap, wait=()):
        @block.gpsimd
        def _(g):
            for sem, val in wait:
                sem.wait_at(g, val)
            st_dma.inc(g.dma_start(out=dst_ap, in_=src_ap), 16)

    def ap2(buf, off, dims):
        return bass.AP(buf[:].tensor, off, dims)

    def dump_and_stop(src_dram, nrows, ncols):
        @block.gpsimd
        def _(g):
            st_dma.inc(g.dma_start(out=out.ap()[0:nrows, 0:ncols],
                                   in_=src_dram.ap()), 16)
        barrier()

    def dump_and_stop2(src_dram, dims):
        n = 1
        for _, c in dims:
            n = max(n, c)
        @block.gpsimd
        def _(g):
            st_dma.inc(g.dma_start(out=ap2(out, 0, [[1, n]]),
                                   in_=ap2(src_dram, 0, dims)), 16)
        barrier()

    def layernorm(xbuf, D, outb, tmp, small, eps=1e-5, wait_dma=None):
        @block.vector
        def _(v):
            if wait_dma is not None:
                wait_dma[0].wait_at(v, wait_dma[1])
            v.reduce_sum(small[:, 0:1], xbuf[:, 0:D], axis=AX.X)
            v.tensor_mul(tmp[:, 0:D], xbuf[:, 0:D], xbuf[:, 0:D])
            v.drain()
            v.reduce_sum(small[:, 1:2], tmp[:, 0:D], axis=AX.X)
            v.tensor_scalar(out=small[:, 0:1], in0=small[:, 0:1],
                            scalar1=1.0 / D, scalar2=None, op0=OP.mult)
            v.drain()
            v.tensor_scalar(out=small[:, 2:3], in0=small[:, 1:2],
                            scalar1=1.0 / D, scalar2=None, op0=OP.mult)
            v.drain()
            v.tensor_mul(small[:, 3:4], small[:, 0:1], small[:, 0:1])
            v.drain()
            v.tensor_sub(small[:, 1:2], small[:, 2:3], small[:, 3:4])
            v.drain()
            v.tensor_scalar(out=small[:, 1:2], in0=small[:, 1:2],
                            scalar1=0.0, scalar2=eps, op0=OP.max, op1=OP.add)
            v.drain()
            v_done.inc(v.nop())
        vv = v_done.n
        @block.scalar
        def _(s):
            v_done.wait_at(s, vv)
            s.activation(out=small[:, 2:3], in_=small[:, 1:2], func=AF.Ln)
            s.drain()
            s.activation(out=small[:, 1:2], in_=small[:, 2:3], func=AF.Exp,
                         scale=-0.5)
            s.drain()
            s_done.inc(s.nop())
        svv = s_done.n
        @block.vector
        def _(v):
            s_done.wait_at(v, svv)
            v.tensor_scalar(out=outb[:, 0:D], in0=xbuf[:, 0:D],
                            scalar1=small[:, 0:1], scalar2=small[:, 1:2],
                            op0=OP.subtract, op1=OP.mult)
            v.drain()
            v_done.inc(v.nop())
        return v_done.n

    def rmsnorm_rows(xbuf, D, outb, tmp, small, wrep=None, eps=1e-6,
                     out_slice=None, x_slice=None):
        xa = xbuf if x_slice is None else x_slice
        oa = outb if out_slice is None else out_slice
        @block.vector
        def _(v):
            v.tensor_mul(tmp[:, 0:D], xa, xa)
            v.drain()
            v.reduce_sum(small[:, 0:1], tmp[:, 0:D], axis=AX.X)
            v.drain()
            v.tensor_scalar(out=small[:, 0:1], in0=small[:, 0:1],
                            scalar1=1.0 / D, scalar2=eps, op0=OP.mult,
                            op1=OP.add)
            v.drain()
            v_done.inc(v.nop())
        vv = v_done.n
        @block.scalar
        def _(s):
            v_done.wait_at(s, vv)
            s.activation(out=small[:, 1:2], in_=small[:, 0:1], func=AF.Ln)
            s.drain()
            s.activation(out=small[:, 0:1], in_=small[:, 1:2], func=AF.Exp,
                         scale=-0.5)
            s.drain()
            s_done.inc(s.nop())
        svv = s_done.n
        @block.vector
        def _(v):
            s_done.wait_at(v, svv)
            if wrep is None:
                v.tensor_scalar(out=oa, in0=xa, scalar1=small[:, 0:1],
                                scalar2=None, op0=OP.mult)
            else:
                v.scalar_tensor_tensor(out=oa, in0=xa, scalar=small[:, 0:1],
                                       in1=wrep, op0=OP.mult, op1=OP.mult)
            v.drain()
            v_done.inc(v.nop())
        return v_done.n

    def transp(items, PT):
        assert len(items) <= 8
        vw = v_done.n
        @block.tensor
        def _(t):
            v_done.wait_at(t, vw)
            PT.acq(t)
            for i, (src, _) in enumerate(items):
                ins = t.transpose(PT.buf[:, i * 128:(i + 1) * 128], src,
                                  c_eye_b[:])
            t_done.inc(ins)
        tv = t_done.n
        sw = st_dma.n
        @block.vector
        def _(v):
            t_done.wait_at(v, tv)
            st_dma.wait_at(v, sw)
            for i, (_, dst) in enumerate(items):
                v.tensor_copy(out=dst, in_=PT.buf[:, i * 128:(i + 1) * 128])
            v.drain()
            v_done.inc(v.nop())
        PT.rel(v_done)
        return v_done.n

    loadq(c_eye_f[:], io["eye_f"].ap())
    loadg(c_eye_b[:], io["eye_f"].ap())
    loadg(c_had_b[:], io["had_f"].ap())
    loadq(c_lut[:], io["lutcols"].ap())
    loadq(c_knwc[:], io["knw_col"].ap())

    @block.vector
    def _(v):
        v.memset(c_ones[:], 1.0)
        v.drain()
        v_done.inc(v.nop())

    with (
        nc.sbuf_tensor("p0_idx", [128, 5504], BF16) as p0_idx,
        nc.sbuf_tensor("p0_acc", [128, 5504], BF16) as p0_acc,
        nc.sbuf_tensor("p0_tmp", [128, 5504], BF16) as p0_tmp,
        nc.sbuf_tensor("p0_w", [128, 4096], F32) as p0_w,
        nc.sbuf_tensor("p0_wt", [128, 4096], BF16) as p0_wt,
        nc.sbuf_tensor("p0_t1", [128, 4096], F32) as p0_t1,
        nc.sbuf_tensor("p0_abs", [128, 48], F32) as p0_abs,
        nc.sbuf_tensor("p0_sml", [128, 8], F32) as p0_sml,
    ):
        regions = [
            (io["gidxT"], 256, 5504, OFF_G),
            (io["uidxT"], 256, 5504, OFF_U),
            (io["didxT"], 688, 2048, OFF_D),
        ]
        for (src, nr, ncol, poff) in regions:
            rt = 0
            while rt * 128 < nr:
                rows = min(128, nr - rt * 128)
                loadg(p0_idx[0:rows, 0:ncol],
                      src.ap()[rt * 128:rt * 128 + rows, :])
                gw = g_dma.n
                @block.vector
                def _(v, rows=rows, ncol=ncol, gw=gw, qw=q_dma.n,
                      sw=st_dma.n):
                    g_dma.wait_at(v, gw)
                    q_dma.wait_at(v, qw)
                    st_dma.wait_at(v, sw)
                    v.tensor_scalar(out=p0_acc[0:rows, 0:ncol],
                                    in0=p0_idx[0:rows, 0:ncol],
                                    scalar1=0.0, scalar2=c_lut[0:rows, 0:1],
                                    op0=OP.is_equal, op1=OP.mult)
                    for vv in range(1, 16):
                        v.tensor_scalar(out=p0_tmp[0:rows, 0:ncol],
                                        in0=p0_idx[0:rows, 0:ncol],
                                        scalar1=float(vv),
                                        scalar2=c_lut[0:rows, vv:vv + 1],
                                        op0=OP.is_equal, op1=OP.mult)
                        v.drain()
                        v.tensor_add(p0_acc[0:rows, 0:ncol],
                                     p0_acc[0:rows, 0:ncol],
                                     p0_tmp[0:rows, 0:ncol])
                        v.drain()
                    v_done.inc(v.nop())
                vv_ = v_done.n
                storeg(
                    ap2(d_pack_src, poff + rt * 128 * ncol,
                        [[ncol, rows], [1, ncol]]),
                    p0_acc[0:rows, 0:ncol],
                    wait=[(v_done, vv_)])
                @block.gpsimd
                def _(g, vv_=vv_):
                    v_done.wait_at(g, vv_)
                rt += 1

        for t in range(32):
            loadq(p0_w[:, 0:2048], io["w_up"].ap()[t * 128:(t + 1) * 128, :],
                  wait=[(v_done, v_done.n)])
            qv = q_dma.n
            @block.vector
            def _(v, t=t, qv=qv):
                q_dma.wait_at(v, qv)
                v.reduce_sum(p0_abs[:, t:t + 1], p0_w[:, 0:2048], axis=AX.X,
                             apply_absolute_value=True)
                v.drain()
                v_done.inc(v.nop())
        for t in range(16):
            loadq(p0_w[:, 0:4096], io["w_down"].ap()[t * 128:(t + 1) * 128, :],
                  wait=[(v_done, v_done.n)])
            qv = q_dma.n
            @block.vector
            def _(v, t=t, qv=qv):
                q_dma.wait_at(v, qv)
                v.reduce_sum(p0_abs[:, 32 + t:33 + t], p0_w[:, 0:4096],
                             axis=AX.X, apply_absolute_value=True)
                v.drain()
                v_done.inc(v.nop())
        @block.vector
        def _(v):
            v.reduce_sum(p0_sml[:, 0:1], p0_abs[:, 0:32], axis=AX.X)
            v.reduce_sum(p0_sml[:, 1:2], p0_abs[:, 32:48], axis=AX.X)
            v.drain()
            v.tensor_copy(out=p0_w[:, 0:2], in_=p0_sml[:, 0:2])
            v.drain()
            v_done.inc(v.nop())
        vv_ = v_done.n
        @block.tensor
        def _(t):
            v_done.wait_at(t, vv_)
            PA.acq(t)
            ins = t.matmul(PA.buf[0:1, 0:2], c_ones[:, 0:1],
                           p0_w[:, 0:2], start=True, stop=True)
            t_done.inc(ins)
        tv = t_done.n
        @block.vector
        def _(v):
            t_done.wait_at(v, tv)
            v.tensor_scalar(out=p0_sml[0:1, 0:2], in0=PA.buf[0:1, 0:2],
                            scalar1=1.0 / (4096.0 * 2048.0), scalar2=1e-5,
                            op0=OP.mult, op1=OP.max)
            v.drain()
            v_done.inc(v.nop())
        PA.rel(v_done)
        vv_ = v_done.n
        @block.tensor
        def _(t):
            v_done.wait_at(t, vv_)
            PB.acq(t)
            ins = t.matmul(PB.buf[:, 0:2], c_ones[0:1, 0:128],
                           p0_sml[0:1, 0:2], start=True, stop=True)
            t_done.inc(ins)
        tv = t_done.n
        @block.vector
        def _(v):
            t_done.wait_at(v, tv)
            v.tensor_copy(out=c_scales[:, 0:2], in_=PB.buf[:, 0:2])
            v.drain()
            v.tensor_scalar(out=p0_sml[:, 4:5], in0=c_scales[:, 0:1],
                            scalar1=0.5, scalar2=None, op0=OP.mult)
            v.tensor_scalar(out=p0_sml[:, 5:6], in0=c_scales[:, 0:1],
                            scalar1=-0.5, scalar2=None, op0=OP.mult)
            v.tensor_scalar(out=p0_sml[:, 6:7], in0=c_scales[:, 1:2],
                            scalar1=0.5, scalar2=None, op0=OP.mult)
            v.tensor_scalar(out=p0_sml[:, 7:8], in0=c_scales[:, 1:2],
                            scalar1=-0.5, scalar2=None, op0=OP.mult)
            v.drain()
            v_done.inc(v.nop())
        PB.rel(v_done)

        tern_jobs = [
            (io["wupT"], 2, 4096, OFF_TU, 4, 5),
            (io["wdnT"], 4, 2048, OFF_TD, 6, 7),
        ]
        for (src, ntile, ncol, poff, hi, lo) in tern_jobs:
            for t in range(ntile):
                loadq(p0_w[:, 0:ncol], src.ap()[t * 128:(t + 1) * 128, :],
                      wait=[(v_done, v_done.n)])
                qv = q_dma.n
                @block.vector
                def _(v, ncol=ncol, hi=hi, lo=lo, qv=qv, sw=st_dma.n):
                    q_dma.wait_at(v, qv)
                    st_dma.wait_at(v, sw)
                    v.tensor_scalar(out=p0_t1[:, 0:ncol], in0=p0_w[:, 0:ncol],
                                    scalar1=p0_sml[:, hi:hi + 1], scalar2=None,
                                    op0=OP.is_gt)
                    v.drain()
                    v.tensor_scalar(out=p0_w[:, 0:ncol], in0=p0_w[:, 0:ncol],
                                    scalar1=p0_sml[:, lo:lo + 1], scalar2=None,
                                    op0=OP.is_lt)
                    v.drain()
                    v.tensor_sub(p0_wt[:, 0:ncol], p0_t1[:, 0:ncol],
                                 p0_w[:, 0:ncol])
                    v.drain()
                    v_done.inc(v.nop())
                vv_ = v_done.n
                storeg(
                    ap2(d_pack_src, poff + t * 128 * ncol,
                        [[ncol, 128], [1, ncol]]),
                    p0_wt[:, 0:ncol], wait=[(v_done, vv_)])
                @block.gpsimd
                def _(g, vv_=vv_):
                    v_done.wait_at(g, vv_)

        stv = st_dma.n
        @block.gpsimd
        def _(g):
            st_dma.wait_at(g, stv)
            ins = g.collective_compute(
                "AllGather", OP.bypass, replica_groups=[list(range(8))],
                ins=[d_pack_src.ap().opt()], outs=[d_pack.ap().opt()])
            cc_sem.inc(ins)

    barrier()

    if stop == "ptu":
        @block.gpsimd
        def _(g):
            st_dma.inc(g.dma_start(
                out=ap2(out, 0, [[1, 1048576]]),
                in_=ap2(d_pack_src, OFF_TU, [[1, 1048576]])), 16)
            st_dma.inc(g.dma_start(
                out=ap2(out, 1048576, [[1, 1048576]]),
                in_=ap2(d_pack_src, OFF_TD, [[1, 1048576]])), 16)
        barrier()
        return nc, io, out

    CMS = {}
    CMS_ORDER = []

    def sba(name, shape, dt):
        cm = nc.sbuf_tensor(name, shape, dt)
        h = cm.__enter__()
        CMS[name] = cm
        CMS_ORDER.append(name)
        return h

    def sbfree(*names):
        for n in sorted(names, key=lambda x: -CMS_ORDER.index(x)):
            assert CMS_ORDER[-1] == n, (n, CMS_ORDER[-5:])
            CMS_ORDER.pop()
            CMS.pop(n).__exit__(None, None, None)

    xt = sba("xt", [128, 2048], F32)
    xsq = sba("xsq", [128, 2048], F32)
    xnb = sba("xnb", [128, 2048], BF16)
    xnTtmp = sba("xnTtmp", [128, 16 * 128], BF16)
    sml = sba("sml", [128, 16], F32)
    ev1 = sba("ev1", [128, 512], F32)
    ev2 = sba("ev2", [128, 512], F32)
    evb = sba("evb", [128, 512], BF16)
    pac = sba("pac", [128, 8], F32)
    xnTd = sba("xnTd", [128, 2 * 2048], BF16)

    def pacer(buf_slice, p=128):
        vv_ = v_done.n
        @block.scalar
        def _(s, vv_=vv_):
            v_done.wait_at(s, vv_)
            s.activation(out=pac[0:p, 0:8], in_=buf_slice, func=AF.Exp,
                         scale=0.0)
            s.drain()
            s_done.inc(s.nop())
        return s_done.n

    swkT_o = sba("swkT_o", [128, 17 * 128], BF16)
    swv_o = sba("swv_o", [128, 17 * 128], BF16)
    swkT_p = sba("swkT_p", [128, 16 * 128], BF16)
    swv_p = sba("swv_p", [128, 16 * 128], BF16)
    kcompT = sba("kcompT", [128, 256], F32)
    icompT = sba("icompT", [64, 256], F32)
    icompT_b = sba("icompT_b", [64, 256], BF16)
    skvT_b = sba("skvT_b", [128, 256], BF16)
    skv_b = sba("skv_b", [128, 256], BF16)
    tau = sba("tau", [128, 16], F32)
    cs0own = sba("b_cs0own", [128, 1024], F32)
    spflag = sba("b_spflag", [128, 16], F32)
    spm = sba("spm", [128, 256], F32)
    selmask = sba("selmask", [128, 4096], F32)

    cs0prv = sba("b_cs0prv", [128, 1024], F32)
    knw_rep = sba("b_knw_rep", [128, 128], F32)
    kvcb = sba("kvcb", [128, 128], F32)
    idxb = sba("idxb", [64, 128], F32)
    w_kvc = sba("w_kvc", [128, 16 * 128], BF16)
    w_kvcz = sba("w_kvcz", [128, 16 * 128], BF16)
    w_kp = sba("w_kp", [128, 16 * 128], BF16)
    w_vp = sba("w_vp", [128, 16 * 128], BF16)
    w_idx = sba("w_idx", [128, 16 * 64], BF16)
    w_idxz = sba("w_idxz", [128, 16 * 64], BF16)

    def ld16(dst, src, ncols, cast=True, col0=0):
        for kt in range(16):
            (loadg if cast else loadq)(
                dst[:, kt * ncols:(kt + 1) * ncols],
                src.ap()[kt * 128:(kt + 1) * 128, col0:col0 + ncols])

    ld16(w_kvc, io["kvc_w"], 128)
    ld16(w_kvcz, io["kvc_wz"], 128)
    ld16(w_kp, io["k_proj_w"], 128)
    ld16(w_vp, io["v_proj_w"], 128)
    ld16(w_idx, io["idx_c_w"], 64)
    ld16(w_idxz, io["idx_c_wz"], 64)
    loadq(cs0own[:], io["cs0own"].ap())
    loadq(cs0prv[:], io["cs0prv"].ap())
    loadq(spflag[:], io["spflag"].ap())
    loadq(knw_rep[:], io["knw_rep"].ap())
    loadq(kvcb[:], io["kvcb_rep"].ap())
    loadq(idxb[:], io["idxb_rep"].ap())

    @block.vector
    def _(v):
        v.memset(swkT_o[:, 0:128], 0.0)
        v.memset(swv_o[:, 0:128], 0.0)
        v.drain()
        v_done.inc(v.nop())

    barrier()

    def rope0(qdst, cs, csoff, nh, stride, tmpf, base=0):
        @block.vector
        def _(v):
            for hh in range(nh):
                o = base + hh * stride
                x1 = qdst[:, o:o + 32]
                x2 = qdst[:, o + 32:o + 64]
                cos = cs[:, csoff:csoff + 32]
                sin = cs[:, csoff + 32:csoff + 64]
                v.tensor_mul(tmpf[:, 0:32], x1, cos)
                v.tensor_mul(tmpf[:, 32:64], x2, sin)
                v.tensor_mul(tmpf[:, 64:96], x1, sin)
                v.tensor_mul(tmpf[:, 96:128], x2, cos)
                v.drain()
                v.tensor_sub(x1, tmpf[:, 0:32], tmpf[:, 32:64])
                v.tensor_add(x2, tmpf[:, 64:96], tmpf[:, 96:128])
                v.drain()
            v_done.inc(v.nop())
        return v_done.n

    def kv_proj_tile(kdstT_slice, vdst_slice, cs, csoff):
        vw = v_done.n
        @block.tensor
        def _(t, vw=vw):
            v_done.wait_at(t, vw)
            PA.acq(t)
            for kt in range(16):
                ins = t.matmul(PA.buf[:, 0:128],
                               xnTtmp[:, kt * 128:(kt + 1) * 128],
                               w_kp[:, kt * 128:(kt + 1) * 128],
                               start=(kt == 0), stop=(kt == 15))
            t_done.inc(ins)
            PB.acq(t)
            for kt in range(16):
                ins = t.matmul(PB.buf[:, 0:128],
                               xnTtmp[:, kt * 128:(kt + 1) * 128],
                               w_vp[:, kt * 128:(kt + 1) * 128],
                               start=(kt == 0), stop=(kt == 15))
            t_done.inc(ins)
        tv = t_done.n
        @block.vector
        def _(v):
            t_done.wait_at(v, tv)
            v.tensor_copy(out=ev1[:, 0:128], in_=PA.buf[:, 0:128])
            v.tensor_copy(out=vdst_slice, in_=PB.buf[:, 0:128])
            v.drain()
            v_done.inc(v.nop())
        PA.rel(v_done)
        PB.rel(v_done)
        rmsnorm_rows(ev1, 128, ev1, ev2, sml, wrep=knw_rep[:],
                     out_slice=ev1[:, 0:128], x_slice=ev1[:, 0:128])
        rope0(ev1, cs, csoff, 1, 128, ev2)
        @block.vector
        def _(v):
            v.tensor_copy(out=evb[:, 0:128], in_=ev1[:, 0:128])
            v.drain()
            v_done.inc(v.nop())
        transp([(evb[:, 0:128], kdstT_slice)], PTa)

    def matmul_wait(t, xn_wait):
        pass

    def ln_transpose_tile(src_dram_ap, dst16=None, store_tile=None):
        loadq(xt[:], src_dram_ap, wait=[(v_done, v_done.n)])
        layernorm(xt, 2048, xnb, xsq, sml, wait_dma=(q_dma, q_dma.n))
        for half in range(2):
            items = [(xnb[:, (half * 8 + j) * 128:(half * 8 + j + 1) * 128],
                      xnTtmp[:, (half * 8 + j) * 128:(half * 8 + j + 1) * 128])
                     for j in range(8)]
            transp(items, PTa if half == 0 else PTb)
        if store_tile is not None:
            tl = store_tile
            vv = v_done.n
            storeg(ap2(d_xn0T, tl * 128,
                       [[2048, 128], [128 * 2048, 16], [1, 128]]),
                   ap2(xnTtmp, 0, [[2048, 128], [128, 16], [1, 128]]),
                   wait=[(v_done, vv)])

    for tl in range(16):
        ln_transpose_tile(io["x_own"].ap()[tl * 128:(tl + 1) * 128, :],
                          store_tile=tl)
        kv_proj_tile(swkT_o[:, (tl + 1) * 128:(tl + 2) * 128],
                     swv_o[:, (tl + 1) * 128:(tl + 2) * 128],
                     cs0own, tl * 64)

    for tl in range(16):
        ln_transpose_tile(io["x_prev"].ap()[tl * 128:(tl + 1) * 128, :])
        kv_proj_tile(swkT_p[:, tl * 128:(tl + 1) * 128],
                     swv_p[:, tl * 128:(tl + 1) * 128],
                     cs0prv, tl * 64)

    def compress_tile(gt):
        ln_transpose_tile(io["x_full"].ap()[gt * 128:(gt + 1) * 128, :])
        vwc = v_done.n
        @block.tensor
        def _(t, vwc=vwc):
            v_done.wait_at(t, vwc)
            for (P, wmat, dpr, ncw) in ((PA, w_kvc, 128, 128),
                                        (PB, w_kvcz, 128, 128),
                                        (PC, w_idx, 64, 64),
                                        (PD, w_idxz, 64, 64)):
                P.acq(t)
                for kt in range(16):
                    ins = t.matmul(P.buf[0:dpr, 0:128],
                                   wmat[:, kt * ncw:(kt + 1) * ncw],
                                   xnTtmp[:, kt * 128:(kt + 1) * 128],
                                   start=(kt == 0), stop=(kt == 15))
                t_done.inc(ins)
        tv = t_done.n
        for (Pc, Pz, dpr, bias, dstT) in (
            (PA, PB, 128, kvcb, kcompT),
            (PC, PD, 64, idxb, icompT),
        ):
            @block.vector
            def _(v, Pz=Pz, dpr=dpr, bias=bias, tv=tv):
                t_done.wait_at(v, tv)
                v.tensor_add(ev1[0:dpr, 0:128], Pz.buf[0:dpr, 0:128],
                             bias[0:dpr, 0:128])
                v.drain()
                v_done.inc(v.nop())
            vv = v_done.n
            @block.scalar
            def _(s, dpr=dpr, vv=vv):
                v_done.wait_at(s, vv)
                s.activation(out=ev1[0:dpr, 0:128], in_=ev1[0:dpr, 0:128],
                             func=AF.Exp)
                s.drain()
                s_done.inc(s.nop())
            sv = s_done.n
            @block.vector
            def _(v, Pc=Pc, dpr=dpr, sv=sv, dstT=dstT, gt=gt):
                s_done.wait_at(v, sv)
                v.tensor_mul(ev2[0:dpr, 0:128], ev1[0:dpr, 0:128],
                             Pc.buf[0:dpr, 0:128])
                v.drain()
                v.reduce_sum(ap2(xsq, 0, [[2048, dpr], [1, 8]]),
                             ap2(ev2, 0, [[512, dpr], [16, 8], [1, 16]]),
                             axis=AX.X)
                v.reduce_sum(ap2(xsq, 8, [[2048, dpr], [1, 8]]),
                             ap2(ev1, 0, [[512, dpr], [16, 8], [1, 16]]),
                             axis=AX.X)
                v.drain()
                v.reciprocal(ap2(xsq, 16, [[2048, dpr], [1, 8]]),
                             ap2(xsq, 8, [[2048, dpr], [1, 8]]))
                v.drain()
                v.tensor_mul(dstT[0:dpr, gt * 8:(gt + 1) * 8],
                             ap2(xsq, 0, [[2048, dpr], [1, 8]]),
                             ap2(xsq, 16, [[2048, dpr], [1, 8]]))
                v.drain()
                v_done.inc(v.nop())
            Pc.rel(v_done)
            Pz.rel(v_done)

    for gt in range(32):
        compress_tile(gt)

    @block.vector
    def _(v):
        v.tensor_mul(ev1[:, 0:256], kcompT[:], kcompT[:])
        v.drain()
        v_done.inc(v.nop())
    vv = v_done.n
    @block.tensor
    def _(t):
        v_done.wait_at(t, vv)
        PA.acq(t)
        ins = t.matmul(PA.buf[0:1, 0:256], c_ones[:, 0:1], ev1[:, 0:256],
                       start=True, stop=True)
        t_done.inc(ins)
    tv = t_done.n
    @block.vector
    def _(v):
        t_done.wait_at(v, tv)
        v.tensor_scalar(out=ev2[0:1, 0:256], in0=PA.buf[0:1, 0:256],
                        scalar1=1.0 / 128.0, scalar2=1e-6, op0=OP.mult,
                        op1=OP.add)
        v.drain()
        v_done.inc(v.nop())
    PA.rel(v_done)
    vv = v_done.n
    @block.scalar
    def _(s):
        v_done.wait_at(s, vv)
        s.activation(out=ev2[0:1, 256:512], in_=ev2[0:1, 0:256], func=AF.Ln)
        s.drain()
        s.activation(out=ev2[0:1, 0:256], in_=ev2[0:1, 256:512], func=AF.Exp,
                     scale=-0.5)
        s.drain()
        s_done.inc(s.nop())
    sv = s_done.n
    @block.tensor
    def _(t):
        s_done.wait_at(t, sv)
        PB.acq(t)
        ins = t.matmul(PB.buf[:, 0:256], c_ones[0:1, 0:128],
                       ev2[0:1, 0:256], start=True, stop=True)
        t_done.inc(ins)
    tv = t_done.n
    @block.vector
    def _(v):
        t_done.wait_at(v, tv)
        v.scalar_tensor_tensor(out=ev1[:, 0:256], in0=kcompT[:],
                               scalar=c_knwc[:, 0:1], in1=PB.buf[:, 0:256],
                               op0=OP.mult, op1=OP.mult)
        v.drain()
        v.tensor_copy(out=skvT_b[:], in_=ev1[:, 0:256])
        v.tensor_copy(out=icompT_b[0:64, :], in_=icompT[0:64, :])
        v.drain()
        v_done.inc(v.nop())
    PB.rel(v_done)
    transp([(skvT_b[:, 0:128], skv_b[:, 0:128]),
            (skvT_b[:, 128:256], skv_b[:, 128:256])], PTa)

    barrier()
    sbfree("b_cs0prv", "b_knw_rep", "kvcb", "idxb", "w_kvc", "w_kvcz",
           "w_kp", "w_vp", "w_idx", "w_idxz")

    if stop == "xn0":
        dump_and_stop2(d_xn0T, [[1, 2048 * 2048]])
        return nc, io, out

    iscall = sba("iscall", [128, 4096], F32)
    xn_dma = SemC(nc, "xn_dma")
    w_dqTb = sba("w_dqTb", [64, 2048], BF16)
    w_iuqb = sba("w_iuqb", [64, 256], BF16)
    w_wb = sba("w_wb", [128, 16 * 4], BF16)
    weff = sba("weff", [128, 16 * 256], BF16)
    qiT = sba("qiT", [64, 512], BF16)
    wihb = sba("wihb", [128, 4], F32)
    dbg = sba("dbg", [128, 16 * 516 + 4096], F32) if stop == "p2dbg" else None

    loadg(w_dqTb[:], io["w_dqT"].ap())
    ld16(w_wb, io["w_w"], 4)
    loadg(w_iuqb[:], io["w_iuq"].ap())
    barrier()

    for kt in range(16):
        @block.tensor
        def _(t, kt=kt):
            PA.acq(t)
            t.matmul(PA.buf[0:128, 0:256],
                     w_dqTb[0:64, kt * 128:(kt + 1) * 128],
                     w_iuqb[0:64, 0:256], start=True, stop=True)
            ins2 = t.matmul(PA.buf[0:1, 256:257], c_ones[0:1, 0:1],
                            c_ones[0:1, 0:1], start=True, stop=True)
            t_done.inc(ins2)
        tv = t_done.n
        @block.vector
        def _(v, kt=kt, tv=tv):
            t_done.wait_at(v, tv)
            v.tensor_copy(out=ev1[:, 481:482], in_=PA.buf[0:128, 0:1])
            v.drain()
            v.tensor_copy(out=weff[:, kt * 256:(kt + 1) * 256],
                          in_=PA.buf[0:128, 0:256])
            v.drain()
            v_done.inc(v.nop())
        PA.rel(v_done)
    barrier()

    def xn_load(tl, bi):
        tw_ = t_done.n
        @block.sync
        def _(q, tl=tl, bi=bi, tw_=tw_):
            t_done.wait_at(q, tw_)
            xn_dma.inc(q.dma_start(
                out=xnTd[:, bi * 2048:(bi + 1) * 2048],
                in_=ap2(d_xn0T, tl * 128,
                        [[2048, 128], [128 * 2048, 16], [1, 128]])), 16)
        return xn_dma.n

    xn_rdy = [xn_load(0, 0)]
    for tl in range(16):
        bo = (tl % 2) * 2048
        if tl + 1 < 16:
            xn_rdy.append(xn_load(tl + 1, (tl + 1) % 2))
        qv = xn_rdy[tl]
        @block.tensor
        def _(t, bo=bo, qv=qv):
            xn_dma.wait_at(t, qv)
            PA.acq(t)
            for ih in range(4):
                for kt in range(16):
                    t.matmul(PA.buf[0:64, ih * 128:(ih + 1) * 128],
                             weff[:, kt * 256 + ih * 64:
                                  kt * 256 + (ih + 1) * 64],
                             xnTd[:, bo + kt * 128:bo + (kt + 1) * 128],
                             start=(kt == 0), stop=(kt == 15))
            PB.acq(t)
            for kt in range(16):
                t.matmul(PB.buf[0:128, 0:4],
                         xnTd[:, bo + kt * 128:bo + (kt + 1) * 128],
                         w_wb[:, kt * 4:(kt + 1) * 4],
                         start=(kt == 0), stop=(kt == 15))
            ins2 = t.matmul(PB.buf[0:1, 8:9], c_ones[0:1, 0:1],
                            c_ones[0:1, 0:1], start=True, stop=True)
            t_done.inc(ins2)
        tv = t_done.n
        @block.vector
        def _(v, tv=tv):
            t_done.wait_at(v, tv)
            v.tensor_copy(out=ev1[:, 481:482], in_=PB.buf[0:128, 0:1])
            v.drain()
            v.tensor_copy(out=qiT[0:64, :], in_=PA.buf[0:64, 0:512])
            v.tensor_copy(out=wihb[:], in_=PB.buf[0:128, 0:4])
            v.drain()
            v_done.inc(v.nop())
        PA.rel(v_done)
        PB.rel(v_done)
        if stop == "p2dbg":
            @block.vector
            def _(v, tl=tl, bo=bo):
                v.tensor_copy(out=dbg[:, tl * 516:tl * 516 + 4], in_=wihb[:])
                v.tensor_copy(out=dbg[0:64, tl * 516 + 4:tl * 516 + 516],
                              in_=qiT[0:64, :])
                if tl in (0, 8):
                    off = 16 * 516 + (tl // 8) * 2048
                    v.tensor_copy(out=dbg[:, off:off + 2048],
                                  in_=xnTd[:, bo:bo + 2048])
                v.drain()
                v_done.inc(v.nop())
        loadq(spm[:], io["spmask"].ap()[:, tl * 256:(tl + 1) * 256],
              wait=[(v_done, v_done.n)])
        qv = q_dma.n
        svq = pacer(qiT[0:64, 0:8], p=64)
        for ih in range(4):
            @block.tensor
            def _(t, ih=ih, svq=svq):
                s_done.wait_at(t, svq)
                PD.acq(t)
                ins = t.matmul(PD.buf[0:128, 0:256],
                               qiT[0:64, ih * 128:(ih + 1) * 128],
                               icompT_b[0:64, 0:256], start=True, stop=True)
                t_done.inc(ins)
            tv = t_done.n
            @block.vector
            def _(v, ih=ih, tv=tv, qv=qv, tl=tl):
                t_done.wait_at(v, tv)
                v.tensor_copy(out=ev1[:, 481:482], in_=PD.buf[0:128, 0:1])
                v.drain()
                v.tensor_relu(ev1[:, 0:256], PD.buf[0:128, 0:256])
                v.drain()
                if ih == 0:
                    v.tensor_scalar(out=ev2[:, 0:256], in0=ev1[:, 0:256],
                                    scalar1=wihb[:, 0:1], scalar2=None,
                                    op0=OP.mult)
                else:
                    v.scalar_tensor_tensor(out=ev2[:, 0:256],
                                           in0=ev1[:, 0:256],
                                           scalar=wihb[:, ih:ih + 1],
                                           in1=ev2[:, 0:256],
                                           op0=OP.mult, op1=OP.add)
                v.drain()
                if ih == 3:
                    q_dma.wait_at(v, qv)
                    v.tensor_add(iscall[:, tl * 256:(tl + 1) * 256],
                                 ev2[:, 0:256], spm[:])
                    v.drain()
                v_done.inc(v.nop())
            PD.rel(v_done)

    if stop == "p2dbg":
        barrier()
        NDBG = 16 * 516 + 4096
        storeg(ap2(d_x1, 0, [[NDBG, 128], [1, NDBG]]), dbg[:])
        dump_and_stop2(d_x1, [[1, 128 * NDBG]])
        return nc, io, out

    barrier()
    sbfree("w_dqTb", "w_iuqb", "w_wb", "weff", "qiT", "wihb")

    srtA = sba("srtA", [128, 2048], F32)
    srtB = sba("srtB", [128, 2048], F32)

    def sort_view(buf, k, j, bitk, bitj, nt):
        hi_cnt = max(256 // (2 * k), 1)
        mid_cnt = max(k // (2 * j), 1)
        off = bitk * k + bitj * j
        dims = [buf[:].ap[0], [256, nt]]
        if hi_cnt > 1:
            dims.append([2 * k, hi_cnt])
        if mid_cnt > 1:
            dims.append([2 * j, mid_cnt])
        dims.append([1, j])
        return bass.AP(buf[:].tensor, off, dims)

    for half in range(2):
        @block.vector
        def _(v, half=half):
            v.tensor_copy(out=srtA[:], in_=iscall[:, half * 2048:(half + 1) * 2048])
            v.drain()
            cur, oth = srtA, srtB
            k = 2
            while k <= 256:
                j = k // 2
                while j >= 1:
                    for bitk in (0, 1):
                        if k == 256 and bitk == 1:
                            continue
                        lo0 = sort_view(cur, k, j, bitk, 0, 8)
                        hi0 = sort_view(cur, k, j, bitk, 1, 8)
                        olo = sort_view(oth, k, j, bitk, 0, 8)
                        ohi = sort_view(oth, k, j, bitk, 1, 8)
                        if bitk == 0:
                            v.tensor_tensor(out=olo, in0=lo0, in1=hi0, op=OP.max)
                            v.tensor_tensor(out=ohi, in0=lo0, in1=hi0, op=OP.min)
                        else:
                            v.tensor_tensor(out=olo, in0=lo0, in1=hi0, op=OP.min)
                            v.tensor_tensor(out=ohi, in0=lo0, in1=hi0, op=OP.max)
                    v.drain()
                    cur, oth = oth, cur
                    j //= 2
                k *= 2
            assert cur is srtA
            v.tensor_copy(out=tau[:, half * 8:(half + 1) * 8],
                          in_=ap2(srtA, 63, [[2048, 128], [256, 8]]))
            v.drain()
            v_done.inc(v.nop())

    for tl in range(16):
        loadq(spm[:], io["spmask"].ap()[:, tl * 256:(tl + 1) * 256],
              wait=[(v_done, v_done.n)])
        qv = q_dma.n
        @block.vector
        def _(v, tl=tl, qv=qv):
            q_dma.wait_at(v, qv)
            isl = iscall[:, tl * 256:(tl + 1) * 256]
            gt_ = srtA[:, 0:256]
            eq_ = srtA[:, 256:512]
            incl = srtA[:, 512:768]
            ind = srtA[:, 768:1024]
            v.tensor_scalar(out=gt_, in0=isl, scalar1=tau[:, tl:tl + 1],
                            scalar2=None, op0=OP.is_gt)
            v.tensor_scalar(out=eq_, in0=isl, scalar1=tau[:, tl:tl + 1],
                            scalar2=None, op0=OP.is_equal)
            v.drain()
            v.reduce_sum(sml[:, 0:1], gt_, axis=AX.X)
            v.tensor_tensor_scan(out=incl, data0=eq_, data1=c_ones[:, 0:256],
                                 initial=0.0, op0=OP.add, op1=OP.mult)
            v.drain()
            v.tensor_scalar(out=sml[:, 1:2], in0=sml[:, 0:1], scalar1=-1.0,
                            scalar2=64.0, op0=OP.mult, op1=OP.add)
            v.tensor_sub(incl, incl, eq_)
            v.drain()
            v.tensor_scalar(out=ind, in0=incl, scalar1=sml[:, 1:2],
                            scalar2=None, op0=OP.is_lt)
            v.drain()
            v.tensor_mul(eq_, eq_, ind)
            v.drain()
            v.tensor_add(gt_, gt_, eq_)
            v.drain()
            v.tensor_scalar(out=gt_, in0=gt_, scalar1=1.0, scalar2=1.0e30,
                            op0=OP.subtract, op1=OP.mult)
            v.drain()
            v.tensor_add(selmask[:, tl * 256:(tl + 1) * 256], gt_, spm[:])
            v.drain()
            v_done.inc(v.nop())

    if stop in ("isc", "sel"):
        buf = iscall if stop == "isc" else selmask
        vvx = v_done.n
        storeg(ap2(d_x1, 0, [[4096, 128], [1, 4096]]), buf[:],
               wait=[(v_done, vvx)])
        dump_and_stop2(d_x1, [[1, 2048 * 2048]])
        return nc, io, out

    barrier()
    sbfree("srtA", "srtB", "iscall")

    w_qcb = sba("w_qcb", [128, 16 * 512], BF16)
    w_qupb = sba("w_qupb", [128, 4 * 2048], BF16)
    w_gwb = sba("w_gwb", [128, 16 * 512], BF16)
    b_qnw = sba("b_qnw", [128, 2048], F32)
    b_wm0 = sba("b_wm0", [128, 384], F32)
    b_wmN = sba("b_wmN", [128, 384], F32)
    qT = sba("qT", [128, 16 * 128], BF16)
    qcT = sba("qcT", [128, 4 * 128], BF16)
    ptb = sba("ptb", [128, 384], BF16)
    attnac = sba("attnac", [128, 2048], F32)

    ld16(w_qcb, io["w_qc"], 512)
    for j in range(4):
        loadg(w_qupb[:, j * 2048:(j + 1) * 2048],
              io["w_qup"].ap()[j * 128:(j + 1) * 128, :])
    ld16(w_gwb, io["group_w"], 512)
    loadq(b_qnw[:], io["qnw_rep"].ap())
    loadq(b_wm0[:], io["wm0"].ap())
    loadq(b_wmN[:], io["wmN"].ap())
    barrier()

    xn_rdy = [xn_load(0, 0)]
    for tl in range(16):
        bo = (tl % 2) * 2048
        if tl + 1 < 16:
            xn_rdy.append(xn_load(tl + 1, (tl + 1) % 2))
        qv = xn_rdy[tl]
        @block.tensor
        def _(t, bo=bo, qv=qv):
            xn_dma.wait_at(t, qv)
            PA.acq(t)
            for kt in range(16):
                ins = t.matmul(PA.buf[:, 0:512],
                               xnTd[:, bo + kt * 128:bo + (kt + 1) * 128],
                               w_qcb[:, kt * 512:(kt + 1) * 512],
                               start=(kt == 0), stop=(kt == 15))
            t_done.inc(ins)
        tv = t_done.n
        @block.vector
        def _(v, tv=tv):
            t_done.wait_at(v, tv)
            v.tensor_copy(out=evb[:, 0:512], in_=PA.buf[:, 0:512])
            v.drain()
            v_done.inc(v.nop())
        PA.rel(v_done)
        transp([(evb[:, j * 128:(j + 1) * 128], qcT[:, j * 128:(j + 1) * 128])
                for j in range(4)], PTa)
        vwq = v_done.n
        for ch in range(4):
            @block.tensor
            def _(t, ch=ch, vwq=vwq):
                v_done.wait_at(t, vwq)
                PB.acq(t)
                for j in range(4):
                    ins = t.matmul(PB.buf[:, 0:512],
                                   qcT[:, j * 128:(j + 1) * 128],
                                   w_qupb[:, j * 2048 + ch * 512:
                                          j * 2048 + (ch + 1) * 512],
                                   start=(j == 0), stop=(j == 3))
                t_done.inc(ins)
            tv = t_done.n
            @block.vector
            def _(v, ch=ch, tv=tv):
                t_done.wait_at(v, tv)
                v.tensor_copy(out=xt[:, ch * 512:(ch + 1) * 512],
                              in_=PB.buf[:, 0:512])
                v.drain()
                v_done.inc(v.nop())
            PB.rel(v_done)
        rope0(xt, cs0own, tl * 64, 16, 128, ev1)
        @block.vector
        def _(v):
            v.tensor_mul(xsq[:], xt[:], xt[:])
            v.drain()
            v.reduce_sum(ap2(sml, 0, [[16, 128], [1, 16]]),
                         ap2(xsq, 0, [[2048, 128], [128, 16], [1, 128]]),
                         axis=AX.X)
            v.drain()
            v.tensor_scalar(out=sml[:, 0:16], in0=sml[:, 0:16],
                            scalar1=1.0 / 128.0, scalar2=1e-6,
                            op0=OP.mult, op1=OP.add)
            v.drain()
            v_done.inc(v.nop())
        vv = v_done.n
        @block.scalar
        def _(s, vv=vv):
            v_done.wait_at(s, vv)
            s.activation(out=sml[:, 0:16], in_=sml[:, 0:16], func=AF.Ln)
            s.drain()
            s.activation(out=sml[:, 0:16], in_=sml[:, 0:16], func=AF.Exp,
                         scale=-0.5)
            s.drain()
            s_done.inc(s.nop())
        sv = s_done.n
        @block.vector
        def _(v, sv=sv):
            s_done.wait_at(v, sv)
            for hh in range(16):
                v.tensor_scalar(out=xsq[:, hh * 128:(hh + 1) * 128],
                                in0=xt[:, hh * 128:(hh + 1) * 128],
                                scalar1=sml[:, hh:hh + 1], scalar2=None,
                                op0=OP.mult)
            v.drain()
            v.tensor_mul(xnb[:], xsq[:], b_qnw[:])
            v.drain()
            v_done.inc(v.nop())
        for half in range(2):
            transp([(xnb[:, (half * 8 + j) * 128:(half * 8 + j + 1) * 128],
                     qT[:, (half * 8 + j) * 128:(half * 8 + j + 1) * 128])
                    for j in range(8)], PTa if half == 0 else PTb)
        vwh = v_done.n
        for hh in range(16):
            qTh = qT[:, hh * 128:(hh + 1) * 128]
            @block.tensor
            def _(t, qTh=qTh, vwh=vwh):
                v_done.wait_at(t, vwh)
                PC.acq(t)
                ins = t.matmul(PC.buf[:, 0:256], qTh, skvT_b[:, 0:256],
                               start=True, stop=True)
                t_done.inc(ins)
            tv = t_done.n
            @block.vector
            def _(v, tv=tv, tl=tl):
                t_done.wait_at(v, tv)
                v.scalar_tensor_tensor(out=ev1[:, 0:256],
                                       in0=PC.buf[:, 0:256], scalar=SC0,
                                       in1=selmask[:, tl * 256:(tl + 1) * 256],
                                       op0=OP.mult, op1=OP.add)
                v.drain()
                v.reduce_max(sml[:, 0:1], ev1[:, 0:256], axis=AX.X,
                             negate=True)
                v.drain()
                v_done.inc(v.nop())
            PC.rel(v_done)
            vv = v_done.n
            @block.scalar
            def _(s, vv=vv):
                v_done.wait_at(s, vv)
                s.activation(out=evb[:, 0:256], in_=ev1[:, 0:256], func=AF.Exp,
                             bias=sml[:, 0:1], accum_out=sml[:, 1:2])
                s.drain()
                s_done.inc(s.nop())
            sv = s_done.n
            @block.tensor
            def _(t, sv=sv):
                s_done.wait_at(t, sv)
                PTa.acq(t)
                ins = t.transpose(PTa.buf[:, 0:128], evb[:, 0:128], c_eye_b[:])
                ins = t.transpose(PTa.buf[:, 128:256], evb[:, 128:256],
                                  c_eye_b[:])
                t_done.inc(ins)
            tv = t_done.n
            @block.vector
            def _(v, tv=tv, sv=sv):
                t_done.wait_at(v, tv)
                s_done.wait_at(v, sv)
                v.tensor_copy(out=ptb[:, 0:256], in_=PTa.buf[:, 0:256])
                v.drain()
                v.reciprocal(sml[:, 2:3], sml[:, 1:2])
                v.drain()
                v.tensor_mul(sml[:, 3:4], sml[:, 2:3], spflag[:, tl:tl + 1])
                v.drain()
                v_done.inc(v.nop())
            PTa.rel(v_done)
            vv = v_done.n
            @block.tensor
            def _(t, vv=vv):
                v_done.wait_at(t, vv)
                PD.acq(t)
                ins = t.matmul(PD.buf[:, 0:128], ptb[:, 0:128],
                               skv_b[:, 0:128], start=True, stop=False)
                ins = t.matmul(PD.buf[:, 0:128], ptb[:, 128:256],
                               skv_b[:, 128:256], start=False, stop=True)
                t_done.inc(ins)
            tv = t_done.n
            @block.vector
            def _(v, tv=tv, hh=hh):
                t_done.wait_at(v, tv)
                v.tensor_scalar(out=attnac[:, hh * 128:(hh + 1) * 128],
                                in0=PD.buf[:, 0:128], scalar1=sml[:, 3:4],
                                scalar2=None, op0=OP.mult)
                v.drain()
                v_done.inc(v.nop())
            PD.rel(v_done)
            @block.tensor
            def _(t, qTh=qTh, tl=tl):
                PE2.acq(t)
                ins = t.matmul(PE2.buf[:, 0:128], qTh,
                               swkT_o[:, tl * 128:(tl + 1) * 128],
                               start=True, stop=True)
                ins = t.matmul(PE2.buf[:, 128:256], qTh,
                               swkT_p[:, tl * 128:(tl + 1) * 128],
                               start=True, stop=True)
                ins = t.matmul(PE2.buf[:, 256:384], qTh,
                               swkT_o[:, (tl + 1) * 128:(tl + 2) * 128],
                               start=True, stop=True)
                t_done.inc(ins)
            tv = t_done.n
            wmask = b_wm0 if tl == 0 else b_wmN
            @block.vector
            def _(v, tv=tv, wmask=wmask):
                t_done.wait_at(v, tv)
                v.scalar_tensor_tensor(out=ev1[:, 0:384],
                                       in0=PE2.buf[:, 0:384], scalar=SC0,
                                       in1=wmask[:, 0:384],
                                       op0=OP.mult, op1=OP.add)
                v.drain()
                v.reduce_max(sml[:, 4:5], ev1[:, 0:384], axis=AX.X,
                             negate=True)
                v.drain()
                v_done.inc(v.nop())
            PE2.rel(v_done)
            vv = v_done.n
            @block.scalar
            def _(s, vv=vv):
                v_done.wait_at(s, vv)
                s.activation(out=evb[:, 0:384], in_=ev1[:, 0:384], func=AF.Exp,
                             bias=sml[:, 4:5], accum_out=sml[:, 5:6])
                s.drain()
                s_done.inc(s.nop())
            sv = s_done.n
            @block.tensor
            def _(t, sv=sv):
                s_done.wait_at(t, sv)
                PTb.acq(t)
                for j in range(3):
                    ins = t.transpose(PTb.buf[:, j * 128:(j + 1) * 128],
                                      evb[:, j * 128:(j + 1) * 128], c_eye_b[:])
                t_done.inc(ins)
            tv = t_done.n
            @block.vector
            def _(v, tv=tv):
                t_done.wait_at(v, tv)
                v.tensor_copy(out=ptb[:, 0:384], in_=PTb.buf[:, 0:384])
                v.drain()
                v.reciprocal(sml[:, 6:7], sml[:, 5:6])
                v.drain()
                v_done.inc(v.nop())
            PTb.rel(v_done)
            vv = v_done.n
            @block.tensor
            def _(t, vv=vv, tl=tl):
                v_done.wait_at(t, vv)
                PF.acq(t)
                ins = t.matmul(PF.buf[:, 0:128], ptb[:, 0:128],
                               swv_o[:, tl * 128:(tl + 1) * 128],
                               start=True, stop=False)
                ins = t.matmul(PF.buf[:, 0:128], ptb[:, 128:256],
                               swv_p[:, tl * 128:(tl + 1) * 128],
                               start=False, stop=False)
                ins = t.matmul(PF.buf[:, 0:128], ptb[:, 256:384],
                               swv_o[:, (tl + 1) * 128:(tl + 2) * 128],
                               start=False, stop=True)
                t_done.inc(ins)
            tv = t_done.n
            @block.vector
            def _(v, tv=tv, hh=hh):
                t_done.wait_at(v, tv)
                v.scalar_tensor_tensor(
                    out=attnac[:, hh * 128:(hh + 1) * 128],
                    in0=PF.buf[:, 0:128], scalar=sml[:, 6:7],
                    in1=attnac[:, hh * 128:(hh + 1) * 128],
                    op0=OP.mult, op1=OP.add)
                v.drain()
                v_done.inc(v.nop())
            PF.rel(v_done)
        for gg in range(4):
            @block.vector
            def _(v, gg=gg):
                v.tensor_copy(out=evb[:, 0:512],
                              in_=attnac[:, gg * 512:(gg + 1) * 512])
                v.drain()
                v_done.inc(v.nop())
            transp([(evb[:, j * 128:(j + 1) * 128],
                     qcT[:, j * 128:(j + 1) * 128]) for j in range(4)], PTa)
            vv = v_done.n
            @block.tensor
            def _(t, gg=gg, vv=vv):
                v_done.wait_at(t, vv)
                PA.acq(t)
                for j in range(4):
                    ins = t.matmul(PA.buf[:, 0:512],
                                   qcT[:, j * 128:(j + 1) * 128],
                                   w_gwb[:, (4 * gg + j) * 512:
                                          (4 * gg + j + 1) * 512],
                                   start=(j == 0), stop=(j == 3))
                t_done.inc(ins)
            tv = t_done.n
            @block.vector
            def _(v, tv=tv):
                t_done.wait_at(v, tv)
                v.tensor_copy(out=evb[:, 0:512], in_=PA.buf[:, 0:512])
                v.drain()
                v_done.inc(v.nop())
            PA.rel(v_done)
            vv = v_done.n
            @block.tensor
            def _(t, vv=vv):
                v_done.wait_at(t, vv)
                PTb.acq(t)
                for j in range(4):
                    ins = t.transpose(PTb.buf[:, j * 128:(j + 1) * 128],
                                      evb[:, j * 128:(j + 1) * 128],
                                      c_eye_b[:])
                t_done.inc(ins)
            tv = t_done.n
            @block.vector
            def _(v, tv=tv, sw=st_dma.n):
                t_done.wait_at(v, tv)
                st_dma.wait_at(v, sw)
                v.tensor_copy(out=xnb[:, 0:512], in_=PTb.buf[:, 0:512])
                v.drain()
                v_done.inc(v.nop())
            PTb.rel(v_done)
            vv = v_done.n
            for j in range(4):
                storeg(ap2(d_oT, (gg * 4 + j) * 128 * 2048 + tl * 128,
                           [[2048, 128], [1, 128]]),
                       xnb[:, j * 128:(j + 1) * 128], wait=[(v_done, vv)])

    barrier()
    sbfree("attnac", "ptb", "qcT", "qT", "b_wmN", "b_wm0", "b_qnw",
           "w_gwb", "w_qupb", "w_qcb")
    sbfree("spm", "selmask", "b_spflag", "b_cs0own", "tau", "skv_b",
           "skvT_b", "icompT_b", "icompT", "kcompT", "swv_p", "swkT_p",
           "swv_o", "swkT_o")

    oTsb = sba("oTsb", [128, 16 * 2048], BF16)
    fwb = sba("fwb", [128, 16 * 512], BF16)

    for ft in range(16):
        loadq(oTsb[:, ft * 2048:(ft + 1) * 2048],
              ap2(d_oT, ft * 128 * 2048, [[2048, 128], [1, 2048]]))
    barrier()

    for ch in range(4):
        for kt in range(16):
            loadg(fwb[:, kt * 512:(kt + 1) * 512],
                  io["final_w"].ap()[kt * 128:(kt + 1) * 128,
                                     ch * 512:(ch + 1) * 512],
                  wait=[(t_done, t_done.n)])
        gv = g_dma.n
        for tl in range(16):
            @block.tensor
            def _(t, tl=tl, gv=gv):
                g_dma.wait_at(t, gv)
                PA.acq(t)
                for kt in range(16):
                    ins = t.matmul(PA.buf[:, 0:512],
                                   oTsb[:, kt * 2048 + tl * 128:
                                        kt * 2048 + (tl + 1) * 128],
                                   fwb[:, kt * 512:(kt + 1) * 512],
                                   start=(kt == 0), stop=(kt == 15))
                t_done.inc(ins)
            tv = t_done.n
            loadq(ev1[:, 0:512],
                  io["x_own"].ap()[tl * 128:(tl + 1) * 128,
                                   ch * 512:(ch + 1) * 512],
              wait=[(v_done, v_done.n)])
            qv = q_dma.n
            @block.vector
            def _(v, tv=tv, qv=qv, sw=st_dma.n):
                t_done.wait_at(v, tv)
                q_dma.wait_at(v, qv)
                st_dma.wait_at(v, sw)
                v.tensor_add(ev2[:, 0:512], PA.buf[:, 0:512], ev1[:, 0:512])
                v.drain()
                v_done.inc(v.nop())
            PA.rel(v_done)
            vv = v_done.n
            storeg(ap2(d_x1, tl * 128 * 2048 + ch * 512,
                       [[2048, 128], [1, 512]]),
                   ev2[:, 0:512], wait=[(v_done, vv)])

    barrier()
    sbfree("fwb", "oTsb")

    if stop == "x0a":
        dump_and_stop(d_x1, 2048, 2048)
        return nc, io, out

    xn0Th = sba("xn0Th", [128, 16 * 1024], BF16)
    hT = sba("hT", [128, 43 * 1024], BF16)

    @block.gpsimd
    def _(g):
        cc_sem.wait_at(g, 1)
        g_done.inc(g.nop())
    gv0 = g_done.n
    @block.sync
    def _(q):
        g_done.wait_at(q, gv0)

    def pack_g(shard, off, row0, nrows, col0, ncols, rowstride):
        return ap2(d_pack, shard * PACK + off + row0 * rowstride + col0,
                   [[rowstride, nrows], [1, ncols]])

    for half in range(2):
        for kt in range(16):
            loadq(xn0Th[:, kt * 1024:(kt + 1) * 1024],
                  ap2(d_xn0T, kt * 128 * 2048 + half * 1024,
                      [[2048, 128], [1, 1024]]),
                  wait=[(t_done, t_done.n)])
        with (
            nc.sbuf_tensor(f"wg_c{half}", [128, 16 * 512], BF16) as wg_c,
            nc.sbuf_tensor(f"wu_c{half}", [128, 16 * 512], BF16) as wu_c,
        ):
            for oc in range(11):
                ocn = 512 if oc < 10 else 384
                for s8 in range(8):
                    for r2 in range(2):
                        kt = s8 * 2 + r2
                        loadq(wg_c[:, kt * 512:kt * 512 + ocn],
                              pack_g(s8, OFF_G, r2 * 128, 128, oc * 512,
                                     ocn, 5504),
                              wait=[(t_done, t_done.n)])
                        loadq(wu_c[:, kt * 512:kt * 512 + ocn],
                              pack_g(s8, OFF_U, r2 * 128, 128, oc * 512,
                                     ocn, 5504))
                qv = q_dma.n
                for tt in range(8):
                    @block.tensor
                    def _(t, tt=tt, ocn=ocn, qv=qv):
                        q_dma.wait_at(t, qv)
                        PA.acq(t)
                        for kt in range(16):
                            ins = t.matmul(
                                PA.buf[:, 0:ocn],
                                xn0Th[:, kt * 1024 + tt * 128:
                                      kt * 1024 + (tt + 1) * 128],
                                wg_c[:, kt * 512:kt * 512 + ocn],
                                start=(kt == 0), stop=(kt == 15))
                        t_done.inc(ins)
                        PB.acq(t)
                        for kt in range(16):
                            ins = t.matmul(
                                PB.buf[:, 0:ocn],
                                xn0Th[:, kt * 1024 + tt * 128:
                                      kt * 1024 + (tt + 1) * 128],
                                wu_c[:, kt * 512:kt * 512 + ocn],
                                start=(kt == 0), stop=(kt == 15))
                        t_done.inc(ins)
                    tv = t_done.n
                    @block.scalar
                    def _(s, tv=tv, ocn=ocn):
                        t_done.wait_at(s, tv)
                        s.activation(out=ev1[:, 0:ocn], in_=PA.buf[:, 0:ocn],
                                     func=AF.Silu)
                        s.drain()
                        s_done.inc(s.nop())
                    sv = s_done.n
                    @block.vector
                    def _(v, sv=sv, ocn=ocn):
                        s_done.wait_at(v, sv)
                        v.tensor_mul(evb[:, 0:ocn], ev1[:, 0:ocn],
                                     PB.buf[:, 0:ocn])
                        v.drain()
                        v_done.inc(v.nop())
                    PA.rel(v_done)
                    PB.rel(v_done)
                    nj = ocn // 128
                    transp([(evb[:, j * 128:(j + 1) * 128],
                             hT[:, (oc * 4 + j) * 1024 + tt * 128:
                                (oc * 4 + j) * 1024 + (tt + 1) * 128])
                            for j in range(nj)], PTa)
        with nc.sbuf_tensor(f"wd_c{half}", [128, 43 * 512], BF16) as wd_c:
            for oc2 in range(4):
                for rt in range(43):
                    for s2 in range(2):
                        s8 = oc2 * 2 + s2
                        loadq(wd_c[:, rt * 512 + s2 * 256:
                                   rt * 512 + (s2 + 1) * 256],
                              pack_g(s8, OFF_D, rt * 128, 128, 0, 256, 256),
                              wait=[(t_done, t_done.n)])
                qv = q_dma.n
                vwd = v_done.n
                for tt in range(8):
                    @block.tensor
                    def _(t, tt=tt, qv=qv, vwd=vwd):
                        q_dma.wait_at(t, qv)
                        v_done.wait_at(t, vwd)
                        PC.acq(t)
                        for rt in range(43):
                            ins = t.matmul(
                                PC.buf[:, 0:512],
                                hT[:, rt * 1024 + tt * 128:
                                   rt * 1024 + (tt + 1) * 128],
                                wd_c[:, rt * 512:(rt + 1) * 512],
                                start=(rt == 0), stop=(rt == 42))
                        t_done.inc(ins)
                    tv = t_done.n
                    tl = half * 8 + tt
                    loadq(ev1[:, 0:512],
                          ap2(d_x1, tl * 128 * 2048 + oc2 * 512,
                              [[2048, 128], [1, 512]]),
                          wait=[(v_done, v_done.n)])
                    qv2 = q_dma.n
                    @block.vector
                    def _(v, tv=tv, qv2=qv2, sw=st_dma.n):
                        t_done.wait_at(v, tv)
                        q_dma.wait_at(v, qv2)
                        st_dma.wait_at(v, sw)
                        v.tensor_add(ev2[:, 0:512], PC.buf[:, 0:512],
                                     ev1[:, 0:512])
                        v.drain()
                        v_done.inc(v.nop())
                    PC.rel(v_done)
                    vv = v_done.n
                    storeg(ap2(d_x1, tl * 128 * 2048 + oc2 * 512,
                               [[2048, 128], [1, 512]]),
                           ev2[:, 0:512], wait=[(v_done, vv)])

    barrier()
    sbfree("hT", "xn0Th")


    if stop == "x1":
        dump_and_stop(d_x1, 2048, 2048)
        return nc, io, out

    w_kva = sba("w_kva", [128, 16 * 576], BF16)
    ckvb = sba("ckvb", [128, 576], F32)
    for kt in range(16):
        loadg(w_kva[:, kt * 576:(kt + 1) * 576],
              io["Wkva"].ap()[kt * 128:(kt + 1) * 128, :])
    barrier()

    for tl in range(16):
        loadq(xt[:], ap2(d_x1, tl * 128 * 2048, [[2048, 128], [1, 2048]]),
              wait=[(v_done, v_done.n)])
        layernorm(xt, 2048, xnb, xsq, sml, wait_dma=(q_dma, q_dma.n))
        for half in range(2):
            items = [(xnb[:, (half * 8 + j) * 128:(half * 8 + j + 1) * 128],
                      xnTtmp[:, (half * 8 + j) * 128:(half * 8 + j + 1) * 128])
                     for j in range(8)]
            transp(items, PTa if half == 0 else PTb)
        vv = v_done.n
        storeg(ap2(d_xn0T, tl * 128,
                   [[2048, 128], [128 * 2048, 16], [1, 128]]),
               ap2(xnTtmp, 0, [[2048, 128], [128, 16], [1, 128]]),
               wait=[(v_done, vv)])
        @block.tensor
        def _(t, vv=vv):
            v_done.wait_at(t, vv)
            PA.acq(t)
            for kt in range(16):
                ins = t.matmul(PA.buf[:, 0:512],
                               xnTtmp[:, kt * 128:(kt + 1) * 128],
                               w_kva[:, kt * 576:kt * 576 + 512],
                               start=(kt == 0), stop=(kt == 15))
            t_done.inc(ins)
            PB.acq(t)
            for kt in range(16):
                ins = t.matmul(PB.buf[:, 0:64],
                               xnTtmp[:, kt * 128:(kt + 1) * 128],
                               w_kva[:, kt * 576 + 512:(kt + 1) * 576],
                               start=(kt == 0), stop=(kt == 15))
            t_done.inc(ins)
        tv = t_done.n
        @block.vector
        def _(v, tv=tv, sw=st_dma.n):
            t_done.wait_at(v, tv)
            st_dma.wait_at(v, sw)
            v.tensor_copy(out=ckvb[:, 0:512], in_=PA.buf[:, 0:512])
            v.tensor_copy(out=ckvb[:, 512:576], in_=PB.buf[:, 0:64])
            v.drain()
            v_done.inc(v.nop())
        PA.rel(v_done)
        PB.rel(v_done)
        vv = v_done.n
        storeg(ap2(d_ckv_src, tl * 128 * 576, [[576, 128], [1, 576]]),
               ckvb[:, 0:576], wait=[(v_done, vv)])

    stv = st_dma.n
    @block.gpsimd
    def _(g):
        st_dma.wait_at(g, stv)
        ins = g.collective_compute(
            "AllGather", OP.bypass,
            replica_groups=[[0, 1], [2, 3], [4, 5], [6, 7]],
            ins=[d_ckv_src.ap().opt()], outs=[d_ckv.ap().opt()])
        cc_sem2.inc(ins)

    barrier()
    sbfree("w_kva", "ckvb")

    if stop == "ckv":
        dump_and_stop(d_ckv_src, 2048, 576)
        return nc, io, out

    w_qa = sba("w_qa", [128, 16 * 1536], BF16)
    qaln = sba("qaln", [128, 1536], F32)
    for kt in range(16):
        loadg(w_qa[:, kt * 1536:(kt + 1) * 1536],
              io["Wqa"].ap()[kt * 128:(kt + 1) * 128, :])
    loadq(qaln[:], io["qalnw_rep"].ap())
    barrier()

    xn_rdy = [xn_load(0, 0)]
    for tl in range(16):
        bo = (tl % 2) * 2048
        if tl + 1 < 16:
            xn_rdy.append(xn_load(tl + 1, (tl + 1) % 2))
        qv = xn_rdy[tl]
        for ch in range(3):
            P = (PA, PB, PC)[ch]
            @block.tensor
            def _(t, ch=ch, P=P, bo=bo, qv=qv):
                xn_dma.wait_at(t, qv)
                P.acq(t)
                for kt in range(16):
                    ins = t.matmul(P.buf[:, 0:512],
                                   xnTd[:, bo + kt * 128:
                                        bo + (kt + 1) * 128],
                                   w_qa[:, kt * 1536 + ch * 512:
                                        kt * 1536 + (ch + 1) * 512],
                                   start=(kt == 0), stop=(kt == 15))
                t_done.inc(ins)
            tv = t_done.n
            @block.vector
            def _(v, ch=ch, P=P, tv=tv):
                t_done.wait_at(v, tv)
                v.tensor_copy(out=xt[:, ch * 512:(ch + 1) * 512],
                              in_=P.buf[:, 0:512])
                v.drain()
                v_done.inc(v.nop())
            P.rel(v_done)
        rmsnorm_rows(xt, 1536, xnb, xsq, sml, wrep=qaln[:],
                     out_slice=xnb[:, 0:1536], x_slice=xt[:, 0:1536])
        transp([(xnb[:, j * 128:(j + 1) * 128],
                 xnTtmp[:, j * 128:(j + 1) * 128]) for j in range(8)], PTa)
        transp([(xnb[:, (8 + j) * 128:(9 + j) * 128],
                 xnTtmp[:, (8 + j) * 128:(9 + j) * 128]) for j in range(4)],
               PTb)
        vv = v_done.n
        storeg(ap2(d_qaT, tl * 128,
                   [[2048, 128], [128 * 2048, 12], [1, 128]]),
               ap2(xnTtmp, 0, [[2048, 128], [128, 12], [1, 128]]),
               wait=[(v_done, vv)])

    barrier()
    sbfree("w_qa", "qaln")

    w_qb = sba("w_qb", [128, 12 * 3072], BF16)
    cs1qb = sba("cs1qb", [128, 1024], F32)
    q3 = sba("q3", [128, 3072], F32)
    qb3 = sba("qb3", [128, 3072], BF16)
    qTn = sba("qTn", [128, 2048], BF16)
    qTp = sba("qTp", [64, 2048], BF16)
    for kt in range(12):
        loadg(w_qb[:, kt * 3072:(kt + 1) * 3072],
              io["Wqb"].ap()[kt * 128:(kt + 1) * 128, :])
    loadq(cs1qb[:], io["cs1q"].ap())
    barrier()

    def qa_load(tl, bi):
        tw_ = t_done.n
        @block.sync
        def _(q, tl=tl, bi=bi, tw_=tw_):
            t_done.wait_at(q, tw_)
            xn_dma.inc(q.dma_start(
                out=xnTd[:, bi * 2048:bi * 2048 + 1536],
                in_=ap2(d_qaT, tl * 128,
                        [[2048, 128], [128 * 2048, 12], [1, 128]])), 16)
        return xn_dma.n

    xn_rdy = [qa_load(0, 0)]
    for tl in range(16):
        bo = (tl % 2) * 2048
        if tl + 1 < 16:
            xn_rdy.append(qa_load(tl + 1, (tl + 1) % 2))
        qv = xn_rdy[tl]
        for ch in range(6):
            P = PA if ch % 2 == 0 else PB
            @block.tensor
            def _(t, ch=ch, P=P, bo=bo, qv=qv):
                xn_dma.wait_at(t, qv)
                P.acq(t)
                for j in range(12):
                    ins = t.matmul(P.buf[:, 0:512],
                                   xnTd[:, bo + j * 128:bo + (j + 1) * 128],
                                   w_qb[:, j * 3072 + ch * 512:
                                        j * 3072 + (ch + 1) * 512],
                                   start=(j == 0), stop=(j == 11))
                t_done.inc(ins)
            tv = t_done.n
            @block.vector
            def _(v, ch=ch, P=P, tv=tv):
                t_done.wait_at(v, tv)
                v.tensor_copy(out=q3[:, ch * 512:(ch + 1) * 512],
                              in_=P.buf[:, 0:512])
                v.drain()
                v_done.inc(v.nop())
            P.rel(v_done)
        rope0(q3, cs1qb, tl * 64, 16, 192, ev1, base=128)
        @block.vector
        def _(v):
            v.tensor_copy(out=qb3[:], in_=q3[:])
            v.drain()
            v_done.inc(v.nop())
        for half in range(2):
            transp([(qb3[:, (half * 8 + j) * 192:(half * 8 + j) * 192 + 128],
                     qTn[:, (half * 8 + j) * 128:(half * 8 + j + 1) * 128])
                    for j in range(8)], PTa)
            vvh = v_done.n
            @block.tensor
            def _(t, half=half, vvh=vvh):
                v_done.wait_at(t, vvh)
                PTb.acq(t)
                for j in range(8):
                    hh = half * 8 + j
                    ins = t.transpose(PTb.buf[0:64, j * 128:(j + 1) * 128],
                                      qb3[:, hh * 192 + 128:hh * 192 + 192],
                                      c_eye_b[:])
                t_done.inc(ins)
            tv = t_done.n
            @block.vector
            def _(v, half=half, tv=tv, sw=st_dma.n):
                t_done.wait_at(v, tv)
                st_dma.wait_at(v, sw)
                v.tensor_copy(out=qTp[:, half * 1024:(half + 1) * 1024],
                              in_=PTb.buf[0:64, 0:1024])
                v.drain()
                v_done.inc(v.nop())
            PTb.rel(v_done)
        vv = v_done.n
        for hh in range(16):
            storeg(ap2(d_qT, hh * 192 * 2048 + tl * 128,
                       [[2048, 128], [1, 128]]),
                   qTn[:, hh * 128:(hh + 1) * 128], wait=[(v_done, vv)])
            storeg(ap2(d_qT, hh * 192 * 2048 + 128 * 2048 + tl * 128,
                       [[2048, 64], [1, 128]]),
                   qTp[:, hh * 128:(hh + 1) * 128], wait=[(v_done, vv)])

    barrier()
    sbfree("w_qb", "cs1qb", "q3", "qb3", "qTn", "qTp")

    kpeT = sba("kpeT", [64, 4096], BF16)
    w_kvbn = sba("w_kvbn", [128, 4 * 2048], BF16)
    w_kvbv = sba("w_kvbv", [128, 4 * 2048], BF16)
    kvaln = sba("kvaln", [128, 512], F32)
    ckvnT = sba("ckvnT", [128, 4 * 512], BF16)
    csb = sba("csb", [128, 64], F32)
    for cc in range(4):
        loadg(w_kvbn[:, cc * 2048:(cc + 1) * 2048],
              io["Wkvb_n"].ap()[cc * 128:(cc + 1) * 128, :])
        loadg(w_kvbv[:, cc * 2048:(cc + 1) * 2048],
              io["Wkvb_v"].ap()[cc * 128:(cc + 1) * 128, :])
    loadq(kvaln[:], io["kvalnw_rep"].ap())
    @block.gpsimd
    def _(g):
        g.wait_ge(cc_sem2.h, 1)
        g_done.inc(g.nop())
    gv_ckv = g_done.n
    barrier()

    for g_grp in range(8):
        for gi in range(4):
            gg = g_grp * 4 + gi
            par, t_loc = gg % 2, gg // 2
            loadq(xt[:, 0:576],
                  ap2(d_ckv, par * 2048 * 576 + t_loc * 128 * 576,
                      [[576, 128], [1, 576]]),
                  wait=[(g_done, gv_ckv), (v_done, v_done.n)])
            loadq(csb[:], io["cs1k"].ap()[gg * 128:(gg + 1) * 128, :],
                  wait=[(v_done, v_done.n)])
            qv = q_dma.n
            @block.vector
            def _(v, qv=qv):
                q_dma.wait_at(v, qv)
            rope0(xt, csb, 0, 1, 0, ev1, base=512)
            @block.vector
            def _(v):
                v.tensor_copy(out=evb[:, 0:64], in_=xt[:, 512:576])
                v.drain()
                v_done.inc(v.nop())
            vv = v_done.n
            @block.tensor
            def _(t, vv=vv):
                v_done.wait_at(t, vv)
                PTa.acq(t)
                ins = t.transpose(PTa.buf[0:64, 0:128], evb[:, 0:64],
                                  c_eye_b[:])
                t_done.inc(ins)
            tv = t_done.n
            @block.vector
            def _(v, tv=tv, gg=gg):
                t_done.wait_at(v, tv)
                v.tensor_copy(out=kpeT[:, gg * 128:(gg + 1) * 128],
                              in_=PTa.buf[0:64, 0:128])
                v.drain()
                v_done.inc(v.nop())
            PTa.rel(v_done)
            rmsnorm_rows(xt, 512, xnb, xsq, sml, wrep=kvaln[:],
                         out_slice=xnb[:, 0:512], x_slice=xt[:, 0:512])
            transp([(xnb[:, c * 128:(c + 1) * 128],
                     ckvnT[:, c * 512 + gi * 128:c * 512 + gi * 128 + 128])
                    for c in range(4)], PTb)
        vwk = v_done.n
        for hh in range(16):
            @block.tensor
            def _(t, hh=hh, vwk=vwk):
                v_done.wait_at(t, vwk)
                PA.acq(t)
                for cc in range(4):
                    ins = t.matmul(PA.buf[:, 0:512],
                                   w_kvbn[:, cc * 2048 + hh * 128:
                                          cc * 2048 + (hh + 1) * 128],
                                   ckvnT[:, cc * 512:(cc + 1) * 512],
                                   start=(cc == 0), stop=(cc == 3))
                t_done.inc(ins)
            tv = t_done.n
            slot = hh % 2
            @block.scalar
            def _(s, tv=tv, slot=slot, sw=st_dma.n):
                t_done.wait_at(s, tv)
                st_dma.wait_at(s, sw)
                s.activation(out=xnb[:, slot * 512:(slot + 1) * 512],
                             in_=PA.buf[:, 0:512], func=AF.Copy)
                s.drain()
                s_done.inc(s.nop())
            PA.rel(s_done)
            sv = s_done.n
            storeg(ap2(d_kT, hh * 128 * 4096 + g_grp * 512,
                       [[4096, 128], [1, 512]]),
                   xnb[:, slot * 512:(slot + 1) * 512],
                   wait=[(s_done, sv)])
        for gi in range(4):
            for vg in range(4):
                @block.tensor
                def _(t, gi=gi, vg=vg, vwk=vwk):
                    v_done.wait_at(t, vwk)
                    PB.acq(t)
                    for cc in range(4):
                        ins = t.matmul(
                            PB.buf[:, 0:512],
                            ckvnT[:, cc * 512 + gi * 128:
                                  cc * 512 + gi * 128 + 128],
                            w_kvbv[:, cc * 2048 + vg * 512:
                                   cc * 2048 + (vg + 1) * 512],
                            start=(cc == 0), stop=(cc == 3))
                    t_done.inc(ins)
                tv = t_done.n
                slot = vg % 2
                @block.scalar
                def _(s, tv=tv, slot=slot, sw=st_dma.n):
                    t_done.wait_at(s, tv)
                    st_dma.wait_at(s, sw)
                    s.activation(out=xnb[:, 1024 + slot * 512:
                                         1024 + (slot + 1) * 512],
                                 in_=PB.buf[:, 0:512], func=AF.Copy)
                    s.drain()
                    s_done.inc(s.nop())
                PB.rel(s_done)
                sv = s_done.n
                gg = g_grp * 4 + gi
                for j in range(4):
                    hh = vg * 4 + j
                    storeg(ap2(d_V, hh * 4096 * 128 + gg * 128 * 128,
                               [[128, 128], [1, 128]]),
                           xnb[:, 1024 + slot * 512 + j * 128:
                               1024 + slot * 512 + (j + 1) * 128],
                           wait=[(s_done, sv)])

    barrier()
    sbfree("w_kvbn", "w_kvbv", "kvaln", "ckvnT", "csb")

    if stop == "kv":
        dump_and_stop(d_kT, 2048, 2048)
        return nc, io, out

    l1ma = sba("l1ma", [128, 512], F32)
    l1mb = sba("l1mb", [128, 512], F32)
    KTg = sba("KTg", [128, 4 * 4096], BF16)
    Vg = sba("Vg", [128, 4 * 4096], BF16)
    QTn10 = sba("QTn10", [128, 4 * 2048], BF16)
    QTp10 = sba("QTp10", [64, 4 * 2048], BF16)
    scrs = sba("scrs", [128, 4096], F32)
    pexp = sba("pexp", [128, 4096], BF16)
    pts = sba("pts", [128, 256], BF16)
    loadq(l1ma[:], io["l1mA"].ap())
    loadq(l1mb[:], io["l1mB"].ap())
    barrier()

    for hg in range(4):
        for hl in range(4):
            hh = hg * 4 + hl
            w8 = [(t_done, t_done.n)]
            loadq(KTg[:, hl * 4096:(hl + 1) * 4096],
                  ap2(d_kT, hh * 128 * 4096, [[4096, 128], [1, 4096]]),
                  wait=w8)
            loadq(Vg[:, hl * 4096:(hl + 1) * 4096],
                  ap2(d_V, hh * 4096 * 128,
                      [[128, 128], [16384, 32], [1, 128]]))
            loadq(QTn10[:, hl * 2048:(hl + 1) * 2048],
                  ap2(d_qT, hh * 192 * 2048, [[2048, 128], [1, 2048]]))
            loadq(QTp10[:, hl * 2048:(hl + 1) * 2048],
                  ap2(d_qT, hh * 192 * 2048 + 128 * 2048,
                      [[2048, 64], [1, 2048]]))
        qv_grp = q_dma.n
        for tl in range(16):
            nch = (2 * tl + 1) // 4 + 1
            NK = nch * 512
            for hl in range(4):
                hh = hg * 4 + hl
                for ch in range(nch):
                    P = PA if ch % 2 == 0 else PB
                    @block.tensor
                    def _(t, P=P, ch=ch, hl=hl, tl=tl, qv_grp=qv_grp):
                        q_dma.wait_at(t, qv_grp)
                        P.acq(t)
                        ins = t.matmul(P.buf[:, 0:512],
                                       QTn10[:, hl * 2048 + tl * 128:
                                             hl * 2048 + (tl + 1) * 128],
                                       KTg[:, hl * 4096 + ch * 512:
                                           hl * 4096 + (ch + 1) * 512],
                                       start=True, stop=False)
                        ins = t.matmul(P.buf[:, 0:512],
                                       QTp10[:, hl * 2048 + tl * 128:
                                             hl * 2048 + (tl + 1) * 128],
                                       kpeT[:, ch * 512:(ch + 1) * 512],
                                       start=False, stop=True)
                        t_done.inc(ins)
                    tv = t_done.n
                    if ch < nch - 1:
                        @block.scalar
                        def _(s, P=P, ch=ch, tv=tv):
                            t_done.wait_at(s, tv)
                            s.activation(out=scrs[:, ch * 512:(ch + 1) * 512],
                                         in_=P.buf[:, 0:512], func=AF.Copy,
                                         scale=SC1)
                            s.drain()
                            s_done.inc(s.nop())
                        P.rel(s_done)
                    else:
                        msk = l1ma if tl % 2 == 0 else l1mb
                        @block.vector
                        def _(v, P=P, ch=ch, tv=tv, msk=msk, sv=s_done.n):
                            t_done.wait_at(v, tv)
                            s_done.wait_at(v, sv)
                            v.scalar_tensor_tensor(
                                out=scrs[:, ch * 512:(ch + 1) * 512],
                                in0=P.buf[:, 0:512], scalar=SC1, in1=msk[:],
                                op0=OP.mult, op1=OP.add)
                            v.drain()
                            v_done.inc(v.nop())
                        P.rel(v_done)
                @block.vector
                def _(v, NK=NK, sv=s_done.n):
                    s_done.wait_at(v, sv)
                    v.reduce_max(sml[:, 0:1], scrs[:, 0:NK], axis=AX.X,
                                 negate=True)
                    v.drain()
                    v_done.inc(v.nop())
                vv = v_done.n
                @block.scalar
                def _(s, NK=NK, vv=vv, tw=t_done.n):
                    v_done.wait_at(s, vv)
                    t_done.wait_at(s, tw)
                    s.activation(out=pexp[:, 0:NK], in_=scrs[:, 0:NK],
                                 func=AF.Exp, bias=sml[:, 0:1],
                                 accum_out=sml[:, 1:2])
                    s.drain()
                    s_done.inc(s.nop())
                sv_exp = s_done.n
                @block.vector
                def _(v, sv_exp=sv_exp):
                    s_done.wait_at(v, sv_exp)
                    v.reciprocal(sml[:, 2:3], sml[:, 1:2])
                    v.drain()
                    v_done.inc(v.nop())
                vv_rcp = v_done.n
                nkt = 4 * nch
                pv_t = []
                for kt in range(nkt):
                    PT = PTa if kt % 2 == 0 else PTb
                    @block.tensor
                    def _(t, PT=PT, kt=kt, sv_exp=sv_exp):
                        if kt == 0:
                            s_done.wait_at(t, sv_exp)
                        PT.acq(t)
                        ins = t.transpose(PT.buf[:, 0:128],
                                          pexp[:, kt * 128:(kt + 1) * 128],
                                          c_eye_b[:])
                        t_done.inc(ins)
                    tv_tr = t_done.n
                    @block.scalar
                    def _(s, kt=kt, PT=PT, tv_tr=tv_tr,
                          pvw=(pv_t[kt - 2] if kt >= 2 else 0)):
                        t_done.wait_at(s, tv_tr)
                        if pvw:
                            t_done.wait_at(s, pvw)
                        s.activation(out=pts[:, (kt % 2) * 128:
                                             (kt % 2 + 1) * 128],
                                     in_=PT.buf[:, 0:128], func=AF.Copy)
                        s.drain()
                        s_done.inc(s.nop())
                    PT.rel(s_done)
                    sv_c = s_done.n
                    @block.tensor
                    def _(t, kt=kt, hl=hl, sv_c=sv_c, nkt=nkt):
                        s_done.wait_at(t, sv_c)
                        if kt == 0:
                            PC.acq(t)
                        ins = t.matmul(PC.buf[0:128, 0:128],
                                       pts[:, (kt % 2) * 128:
                                           (kt % 2 + 1) * 128],
                                       Vg[:, hl * 4096 + kt * 128:
                                          hl * 4096 + (kt + 1) * 128],
                                       start=(kt == 0), stop=(kt == nkt - 1))
                        t_done.inc(ins)
                    pv_t.append(t_done.n)
                @block.vector
                def _(v, tv=pv_t[-1], vv_rcp=vv_rcp):
                    t_done.wait_at(v, tv)
                    v.tensor_scalar(out=evb[:, 0:128],
                                    in0=PC.buf[0:128, 0:128],
                                    scalar1=sml[:, 2:3], scalar2=None,
                                    op0=OP.mult)
                    v.drain()
                    v_done.inc(v.nop())
                PC.rel(v_done)
                vv = v_done.n
                @block.tensor
                def _(t, vv=vv):
                    v_done.wait_at(t, vv)
                    PTa.acq(t)
                    ins = t.transpose(PTa.buf[:, 128:256], evb[:, 0:128],
                                      c_eye_b[:])
                    t_done.inc(ins)
                tv = t_done.n
                @block.vector
                def _(v, tv=tv, sw=st_dma.n):
                    t_done.wait_at(v, tv)
                    st_dma.wait_at(v, sw)
                    v.tensor_copy(out=evb[:, 128:256],
                                  in_=PTa.buf[:, 128:256])
                    v.drain()
                    v_done.inc(v.nop())
                PTa.rel(v_done)
                vv = v_done.n
                storeg(ap2(d_oT, hh * 128 * 2048 + tl * 128,
                           [[2048, 128], [1, 128]]),
                       evb[:, 128:256], wait=[(v_done, vv)])

    barrier()
    sbfree("l1ma", "l1mb", "KTg", "Vg", "QTn10", "QTp10", "scrs", "pexp",
           "pts")
    sbfree("kpeT")

    w_o = sba("w_o", [128, 16 * 2048], BF16)
    oTt = sba("oTt", [128, 2048], BF16)
    for kt in range(16):
        loadg(w_o[:, kt * 2048:(kt + 1) * 2048],
              io["Wo"].ap()[kt * 128:(kt + 1) * 128, :])
    barrier()

    for tl in range(16):
        for hh in range(16):
            loadq(oTt[:, hh * 128:(hh + 1) * 128],
                  ap2(d_oT, hh * 128 * 2048 + tl * 128,
                      [[2048, 128], [1, 128]]),
                  wait=([(t_done, t_done.n)] if hh == 0 else ()))
        qv = q_dma.n
        for oc in range(4):
            P = PA if oc % 2 == 0 else PB
            @block.tensor
            def _(t, P=P, oc=oc, qv=qv):
                q_dma.wait_at(t, qv)
                P.acq(t)
                for hh in range(16):
                    ins = t.matmul(P.buf[:, 0:512],
                                   oTt[:, hh * 128:(hh + 1) * 128],
                                   w_o[:, hh * 2048 + oc * 512:
                                       hh * 2048 + (oc + 1) * 512],
                                   start=(hh == 0), stop=(hh == 15))
                t_done.inc(ins)
            tv = t_done.n
            loadq(ev1[:, 0:512],
                  ap2(d_x1, tl * 128 * 2048 + oc * 512,
                      [[2048, 128], [1, 512]]),
                  wait=[(v_done, v_done.n)])
            qv2 = q_dma.n
            @block.vector
            def _(v, P=P, tv=tv, qv2=qv2, sw=st_dma.n):
                t_done.wait_at(v, tv)
                q_dma.wait_at(v, qv2)
                st_dma.wait_at(v, sw)
                v.tensor_add(ev2[:, 0:512], P.buf[:, 0:512], ev1[:, 0:512])
                v.drain()
                v_done.inc(v.nop())
            P.rel(v_done)
            vv = v_done.n
            storeg(ap2(d_x1, tl * 128 * 2048 + oc * 512,
                       [[2048, 128], [1, 512]]),
                   ev2[:, 0:512], wait=[(v_done, vv)])

    barrier()
    sbfree("w_o", "oTt")
    sbfree("xnTd")

    if stop == "xattn":
        dump_and_stop(d_x1, 2048, 2048)
        return nc, io, out

    SQ2048 = float(np.sqrt(2048.0))
    FLR2 = float(1e-5 * np.sqrt(4096.0) * 127.0 * 127.0)
    KAL = float(1.0 / (64.0 * 127.0 ** 3))
    mstat = sba("mstat", [128, 64], F32)
    y1 = sba("y1", [128, 8192], F32)
    y2 = sba("y2", [128, 8192], F32)
    qi32 = sba("qi32", [128, 8192], I32)
    sbc = sba("sbc", [128, 256], F32)
    xng = sba("xng", [128, 4096], BF16)
    xqg = sba("xqg", [128, 8192], BF16)

    def butterflies(nc_blocks, stages, width):
        @block.vector
        def _(v, sv=s_done.n):
            s_done.wait_at(v, sv)
            cur, oth = y1, y2
            for hstep in stages:
                for b0 in range(0, nc_blocks, 2 * hstep):
                    for k in range(hstep):
                        cA, cB = b0 + k, b0 + k + hstep
                        v.tensor_add(oth[:, cA * width:(cA + 1) * width],
                                     cur[:, cA * width:(cA + 1) * width],
                                     cur[:, cB * width:(cB + 1) * width])
                        v.tensor_sub(oth[:, cB * width:(cB + 1) * width],
                                     cur[:, cA * width:(cA + 1) * width],
                                     cur[:, cB * width:(cB + 1) * width])
                v.drain()
                cur, oth = oth, cur
            v_done.inc(v.nop())
        return y1 if len(stages) % 2 == 0 else y2

    def quant_group(g, nblk, ybuf, scr, flr_mode, width=256):
        @block.vector
        def _(v, g=g, nblk=nblk):
            half = nblk // 2

            def tree(op):
                for k in range(half):
                    v.tensor_tensor(out=scr[:, k * width:(k + 1) * width],
                                    in0=ybuf[:, k * width:(k + 1) * width],
                                    in1=ybuf[:, (k + half) * width:
                                             (k + half + 1) * width],
                                    op=op)
                v.drain()
                base, n = 0, half
                while n > 1:
                    for k in range(n // 2):
                        v.tensor_tensor(
                            out=scr[:, (base + n + k) * width:
                                    (base + n + k + 1) * width],
                            in0=scr[:, (base + k) * width:
                                    (base + k + 1) * width],
                            in1=scr[:, (base + k + n // 2) * width:
                                    (base + k + n // 2 + 1) * width],
                            op=op)
                    v.drain()
                    base += n
                    n //= 2
                return base
            v.drain()
            maxb = tree(OP.max)
            v.tensor_copy(out=ev2[:, 0:width],
                          in_=scr[:, maxb * width:(maxb + 1) * width])
            v.drain()
            minb = tree(OP.min)
            v.tensor_scalar(out=ev1[:, 0:width],
                            in0=scr[:, minb * width:(minb + 1) * width],
                            scalar1=-1.0, scalar2=None, op0=OP.mult)
            v.drain()
            v.tensor_tensor(out=ev2[:, 0:width], in0=ev1[:, 0:width],
                            in1=ev2[:, 0:width], op=OP.max)
            v.drain()
            v_done.inc(v.nop())
        vv = v_done.n
        @block.tensor
        def _(t, vv=vv):
            v_done.wait_at(t, vv)
            PC.acq(t)
            ins = t.transpose(PC.buf[:, 0:128], ev2[:, 0:128], c_eye_f[:])
            ins = t.transpose(PC.buf[:, 128:256], ev2[:, 128:256],
                              c_eye_f[:])
            t_done.inc(ins)
        tv = t_done.n
        @block.vector
        def _(v, g=g, tv=tv, flr_mode=flr_mode):
            t_done.wait_at(v, tv)
            for b in range(2):
                v.reduce_max(sml[:, 8 + b:9 + b],
                             PC.buf[:, b * 128:(b + 1) * 128], axis=AX.X)
            v.drain()
            for b in range(2):
                tile = 2 * g + b
                if flr_mode == 1:
                    v.tensor_scalar(out=mstat[:, tile:tile + 1],
                                    in0=sml[:, 8 + b:9 + b],
                                    scalar1=1.0 / SQ2048, scalar2=1e-5,
                                    op0=OP.mult, op1=OP.max)
                    v.drain()
                    v.reciprocal(sml[:, 10 + b:11 + b],
                                 mstat[:, tile:tile + 1])
                    v.drain()
                    v.tensor_scalar(out=sml[:, 12 + b:13 + b],
                                    in0=sml[:, 10 + b:11 + b],
                                    scalar1=127.0 / SQ2048, scalar2=None,
                                    op0=OP.mult)
                else:
                    v.tensor_scalar(out=sml[:, 10 + b:11 + b],
                                    in0=mstat[:, tile:tile + 1],
                                    scalar1=c_scales[:, 0:1], scalar2=None,
                                    op0=OP.mult)
                    v.drain()
                    v.tensor_mul(sml[:, 12 + b:13 + b],
                                 sml[:, 10 + b:11 + b],
                                 sml[:, 10 + b:11 + b])
                    v.drain()
                    v.reciprocal(sml[:, 10 + b:11 + b],
                                 sml[:, 12 + b:13 + b])
                    v.drain()
                    v.tensor_scalar(out=sml[:, 12 + b:13 + b],
                                    in0=sml[:, 10 + b:11 + b],
                                    scalar1=FLR2, scalar2=None, op0=OP.mult)
                    v.drain()
                    v.tensor_tensor(out=mstat[:, 16 + tile:17 + tile],
                                    in0=sml[:, 8 + b:9 + b],
                                    in1=sml[:, 12 + b:13 + b], op=OP.max)
                    v.drain()
                    v.reciprocal(sml[:, 10 + b:11 + b],
                                 mstat[:, 16 + tile:17 + tile])
                    v.drain()
                    v.tensor_scalar(out=sml[:, 12 + b:13 + b],
                                    in0=sml[:, 10 + b:11 + b],
                                    scalar1=127.0, scalar2=None, op0=OP.mult)
            v.drain()
            v_done.inc(v.nop())
        PC.rel(v_done)
        vv = v_done.n
        @block.tensor
        def _(t, vv=vv):
            v_done.wait_at(t, vv)
            PD.acq(t)
            ins = t.transpose(PD.buf[0:1, 0:128], sml[:, 12:13], c_eye_f[:])
            ins = t.transpose(PD.buf[0:1, 128:256], sml[:, 13:14],
                              c_eye_f[:])
            t_done.inc(ins)
        tv = t_done.n
        @block.vector
        def _(v, tv=tv):
            t_done.wait_at(v, tv)
            v.tensor_copy(out=ev1[0:1, 0:256], in_=PD.buf[0:1, 0:256])
            v.drain()
            v_done.inc(v.nop())
        PD.rel(v_done)
        vv = v_done.n
        @block.tensor
        def _(t, vv=vv):
            v_done.wait_at(t, vv)
            PE2.acq(t)
            ins = t.matmul(PE2.buf[:, 0:256], c_ones[0:1, 0:128],
                           ev1[0:1, 0:256], start=True, stop=True)
            t_done.inc(ins)
        tv = t_done.n
        @block.vector
        def _(v, tv=tv):
            t_done.wait_at(v, tv)
            v.tensor_copy(out=sbc[:], in_=PE2.buf[:, 0:256])
            v.drain()
            v_done.inc(v.nop())
        PE2.rel(v_done)

    def rne_to_bf16(nblk, ybuf, width=256):
        @block.vector
        def _(v, nblk=nblk, sw=st_dma.n):
            st_dma.wait_at(v, sw)
            for c in range(nblk):
                v.tensor_tensor(out=qi32[:, c * width:(c + 1) * width],
                                in0=ybuf[:, c * width:(c + 1) * width],
                                in1=sbc[:], op=OP.mult)
            v.drain()
            v.tensor_copy(out=y2[:, 0:nblk * width],
                          in_=qi32[:, 0:nblk * width])
            v.drain()
            v.tensor_copy(out=xqg[:, 0:nblk * width],
                          in_=y2[:, 0:nblk * width])
            v.drain()
            v_done.inc(v.nop())
        return v_done.n

    for g in range(8):
        loadq(xng[:], ap2(d_xn0T, g * 256,
                          [[2048, 128], [128 * 2048, 16], [1, 256]]),
              wait=[(t_done, t_done.n)])
        qv = q_dma.n
        for c in range(16):
            P = PA if c % 2 == 0 else PB
            @block.tensor
            def _(t, c=c, P=P, qv=qv):
                if c == 0:
                    q_dma.wait_at(t, qv)
                P.acq(t)
                ins = t.matmul(P.buf[:, 0:256], c_had_b[:],
                               xng[:, c * 256:(c + 1) * 256],
                               start=True, stop=True)
                t_done.inc(ins)
            tv = t_done.n
            @block.scalar
            def _(s, c=c, P=P, tv=tv, vw=v_done.n):
                t_done.wait_at(s, tv)
                if c == 0:
                    v_done.wait_at(s, vw)
                s.activation(out=y1[:, c * 256:(c + 1) * 256],
                             in_=P.buf[:, 0:256], func=AF.Copy)
                s.drain()
                s_done.inc(s.nop())
            P.rel(s_done)
        fin = butterflies(16, [1, 2, 4, 8], 256)
        assert fin is y1
        quant_group(g, 16, y1, y2, flr_mode=1)
        vv = rne_to_bf16(16, y1)
        storeg(ap2(d_xq1, g * 256,
                   [[2048, 128], [128 * 2048, 16], [1, 256]]),
               ap2(xqg, 0, [[8192, 128], [256, 16], [1, 256]]),
               wait=[(v_done, vv)])

    barrier()

    if stop == "xq1":
        dump_and_stop2(d_xq1, [[1, 2048 * 2048]])
        return nc, io, out

    wupb = sba("wupb", [128, 2 * 8192], BF16)
    xqs = sba("xqs", [128, 16 * 512], BF16)
    h2b = sba("h2b", [128, 1024], BF16)
    for tg in range(4):
        loadq(xqs[:], ap2(d_xq1, tg * 512,
                          [[2048, 128], [128 * 2048, 16], [1, 512]]),
              wait=[(t_done, t_done.n)])
        qv_x = q_dma.n
        for iblk in range(8):
            dslot = iblk % 2
            for kt in range(16):
                s8, r2 = kt // 2, kt % 2
                loadq(wupb[:, dslot * 8192 + kt * 512:
                           dslot * 8192 + (kt + 1) * 512],
                      pack_g(s8, OFF_TU, r2 * 128, 128, iblk * 512, 512,
                             4096),
                      wait=([(t_done, t_done.n)] if kt == 0 else ()))
            qv = q_dma.n
            for isub in range(4):
                P = PA if isub % 2 == 0 else PB
                @block.tensor
                def _(t, P=P, isub=isub, dslot=dslot, qv=qv, qv_x=qv_x):
                    q_dma.wait_at(t, qv)
                    q_dma.wait_at(t, qv_x)
                    P.acq(t)
                    for kt in range(16):
                        ins = t.matmul(
                            P.buf[:, 0:512],
                            wupb[:, dslot * 8192 + kt * 512 + isub * 128:
                                 dslot * 8192 + kt * 512 + (isub + 1) * 128],
                            xqs[:, kt * 512:(kt + 1) * 512],
                            start=(kt == 0), stop=(kt == 15))
                    t_done.inc(ins)
                tv = t_done.n
                @block.scalar
                def _(s, P=P, tv=tv, vw=v_done.n):
                    t_done.wait_at(s, tv)
                    v_done.wait_at(s, vw)
                    s.activation(out=ev1[:, 0:512], in_=P.buf[:, 0:512],
                                 func=AF.Relu)
                    s.drain()
                    s_done.inc(s.nop())
                P.rel(s_done)
                sv = s_done.n
                slot = isub % 2
                @block.vector
                def _(v, sv=sv, slot=slot, sw=st_dma.n):
                    s_done.wait_at(v, sv)
                    st_dma.wait_at(v, sw)
                    v.tensor_mul(h2b[:, slot * 512:(slot + 1) * 512],
                                 ev1[:, 0:512], ev1[:, 0:512])
                    v.drain()
                    v_done.inc(v.nop())
                vv = v_done.n
                ic = iblk * 4 + isub
                storeg(ap2(d_h2T, ic * 128 * 2048 + tg * 512,
                           [[2048, 128], [1, 512]]),
                       h2b[:, slot * 512:(slot + 1) * 512],
                       wait=[(v_done, vv)])

    barrier()
    sbfree("wupb", "xqs", "h2b")

    if stop in ("h2a", "h2b"):
        off = 0 if stop == "h2a" else 2048 * 2048
        @block.gpsimd
        def _(g, off=off):
            st_dma.inc(g.dma_start(out=ap2(out, 0, [[1, 2048 * 2048]]),
                                   in_=ap2(d_h2T, off, [[1, 2048 * 2048]])),
                       16)
        barrier()
        return nc, io, out

    h2g = sba("h2g", [128, 8192], BF16)
    for g in range(8):
        loadq(h2g[:], ap2(d_h2T, g * 256,
                          [[2048, 128], [128 * 2048, 32], [1, 256]]),
              wait=[(t_done, t_done.n)])
        qv = q_dma.n
        for c in range(32):
            P = PA if c % 2 == 0 else PB
            @block.tensor
            def _(t, c=c, P=P, qv=qv):
                if c == 0:
                    q_dma.wait_at(t, qv)
                P.acq(t)
                ins = t.matmul(P.buf[:, 0:256], c_had_b[:],
                               h2g[:, c * 256:(c + 1) * 256],
                               start=True, stop=True)
                t_done.inc(ins)
            tv = t_done.n
            @block.scalar
            def _(s, c=c, P=P, tv=tv, vw=v_done.n):
                t_done.wait_at(s, tv)
                if c == 0:
                    v_done.wait_at(s, vw)
                s.activation(out=y1[:, c * 256:(c + 1) * 256],
                             in_=P.buf[:, 0:256], func=AF.Copy)
                s.drain()
                s_done.inc(s.nop())
            P.rel(s_done)
        fin = butterflies(32, [1, 2, 4, 8, 16], 256)
        assert fin is y2
        quant_group(g, 32, y2, y1, flr_mode=2)
        vv = rne_to_bf16(32, y2)
        storeg(ap2(d_xq2, g * 256,
                   [[2048, 128], [128 * 2048, 32], [1, 256]]),
               ap2(xqg, 0, [[8192, 128], [256, 32], [1, 256]]),
               wait=[(v_done, vv)])

    barrier()
    sbfree("y1", "y2", "qi32", "sbc", "xng", "xqg", "h2g")

    if stop in ("xq2a", "xq2b"):
        off = 0 if stop == "xq2a" else 2048 * 2048
        @block.gpsimd
        def _(g, off=off):
            st_dma.inc(g.dma_start(out=ap2(out, 0, [[1, 2048 * 2048]]),
                                   in_=ap2(d_xq2, off, [[1, 2048 * 2048]])),
                       16)
        barrier()
        return nc, io, out

    wdnb = sba("wdnb", [128, 32 * 512], BF16)
    xq2t = sba("xq2t", [128, 32 * 128], BF16)
    @block.vector
    def _(v):
        v.tensor_scalar(out=mstat[:, 32:48], in0=mstat[:, 0:16],
                        scalar1=c_scales[:, 0:1], scalar2=None, op0=OP.mult)
        v.drain()
        v.tensor_mul(mstat[:, 48:64], mstat[:, 32:48], mstat[:, 32:48])
        v.drain()
        v.tensor_mul(mstat[:, 32:48], mstat[:, 48:64], mstat[:, 16:32])
        v.drain()
        v.tensor_scalar(out=mstat[:, 48:64], in0=mstat[:, 32:48],
                        scalar1=c_scales[:, 1:2], scalar2=KAL,
                        op0=OP.mult, op1=OP.mult)
        v.drain()
        v_done.inc(v.nop())

    for oc in range(4):
        for rt in range(32):
            loadq(wdnb[:, rt * 512:(rt + 1) * 512],
                  pack_g(rt // 4, OFF_TD, (rt % 4) * 128, 128, oc * 512,
                         512, 2048),
                  wait=([(t_done, t_done.n)] if rt == 0 else ()))
        for tl in range(16):
            loadq(xq2t[:], ap2(d_xq2, tl * 128,
                               [[2048, 128], [128 * 2048, 32], [1, 128]]),
                  wait=[(t_done, t_done.n)])
            qv = q_dma.n
            P = PA if tl % 2 == 0 else PB
            @block.tensor
            def _(t, P=P, qv=qv):
                q_dma.wait_at(t, qv)
                P.acq(t)
                for rt in range(32):
                    ins = t.matmul(P.buf[:, 0:512],
                                   xq2t[:, rt * 128:(rt + 1) * 128],
                                   wdnb[:, rt * 512:(rt + 1) * 512],
                                   start=(rt == 0), stop=(rt == 31))
                t_done.inc(ins)
            tv = t_done.n
            loadq(ev1[:, 0:512],
                  ap2(d_x1, tl * 128 * 2048 + oc * 512,
                      [[2048, 128], [1, 512]]),
                  wait=[(v_done, v_done.n)])
            qv2 = q_dma.n
            @block.vector
            def _(v, P=P, tl=tl, tv=tv, qv2=qv2, sw=st_dma.n):
                t_done.wait_at(v, tv)
                q_dma.wait_at(v, qv2)
                st_dma.wait_at(v, sw)
                v.scalar_tensor_tensor(out=ev2[:, 0:512],
                                       in0=P.buf[:, 0:512],
                                       scalar=mstat[:, 48 + tl:49 + tl],
                                       in1=ev1[:, 0:512],
                                       op0=OP.mult, op1=OP.add)
                v.drain()
                v_done.inc(v.nop())
            P.rel(v_done)
            vv = v_done.n
            storeg(out.ap()[tl * 128:(tl + 1) * 128,
                            oc * 512:(oc + 1) * 512],
                   ev2[:, 0:512], wait=[(v_done, vv)])

    barrier()
    return nc, io, out


def _host_consts():
    eye = np.eye(128, dtype=np.float32)
    had = np.empty((128, 128), np.float32)
    for i in range(128):
        for j in range(128):
            had[i, j] = 1.0 if bin(i & j).count("1") % 2 == 0 else -1.0
    inv = (1.0 / (10000.0 ** (np.arange(0, 64, 2, dtype=np.float32)
                              / np.float32(64.0)))).astype(np.float32)
    pos = np.arange(4096, dtype=np.float32)
    ang = pos[:, None] * inv[None, :]
    cosf = np.cos(ang).astype(np.float32)
    sinf = np.sin(ang).astype(np.float32)
    cs1k = np.concatenate([cosf, sinf], axis=1)
    return eye, had, cosf, sinf, cs1k


def _core_tables(h, cosf, sinf):
    NEGM = np.float32(-1.0e30)
    p = np.arange(128)
    cs_own = np.zeros((128, 1024), np.float32)
    cs_prv = np.zeros((128, 1024), np.float32)
    for t in range(16):
        g = 2 * t + h
        posn = g * 128 + p
        cs_own[:, t * 64:t * 64 + 32] = cosf[posn]
        cs_own[:, t * 64 + 32:t * 64 + 64] = sinf[posn]
        if g - 1 >= 0:
            posp = (g - 1) * 128 + p
            cs_prv[:, t * 64:t * 64 + 32] = cosf[posp]
            cs_prv[:, t * 64 + 32:t * 64 + 64] = sinf[posp]
    spmask = np.zeros((128, 4096), np.float32)
    spflag = np.zeros((128, 16), np.float32)
    blk_end = np.arange(256) * 16 + 15
    for t in range(16):
        posn = (2 * t + h) * 128 + p
        valid = blk_end[None, :] < posn[:, None]
        spmask[:, t * 256:(t + 1) * 256] = np.where(valid, 0.0, NEGM)
        spflag[:, t] = (posn >= 16).astype(np.float32)
    j = np.arange(384)
    base = (p[:, None] + 1 <= j[None, :]) & (j[None, :] <= p[:, None] + 256)
    wmN = np.where(base, 0.0, NEGM).astype(np.float32)
    wm0 = np.where(base & (j[None, :] >= 256 - 128 * h), 0.0,
                   NEGM).astype(np.float32)
    def l1mask(d0):
        m = np.zeros((128, 512), np.float32)
        for st in range(4):
            if st < d0:
                continue
            elif st == d0:
                cc = np.arange(128)
                m[:, st * 128:(st + 1) * 128] = np.where(
                    cc[None, :] <= p[:, None], 0.0, NEGM)
            else:
                m[:, st * 128:(st + 1) * 128] = NEGM
        return m
    return (cs_own, cs_prv, spmask, spflag, wm0, wmN,
            l1mask(h), l1mask(h + 2))


def kernel(**inputs):
    fi = {k: np.asarray(v) for k, v in inputs.items()}
    eye, had, cosf, sinf, cs1k = _host_consts()
    f32 = np.float32
    x = fi["L0_attn_h"].astype(f32, copy=False)
    lut = fi["L0_ffn_lut"].astype(f32, copy=False)
    lutcols = np.broadcast_to(lut[None, :], (128, 16)).copy()
    qnw = np.broadcast_to(np.tile(fi["L0_attn_q_norm_w"].astype(f32), 16)
                          [None, :], (128, 2048)).copy()
    knw_rep = np.broadcast_to(fi["L0_attn_k_norm_w"].astype(f32)[None, :],
                              (128, 128)).copy()
    knw_col = fi["L0_attn_k_norm_w"].astype(f32).reshape(128, 1).copy()
    qalnw = np.broadcast_to(fi["L1_attn_q_a_ln_w"].astype(f32)[None, :],
                            (128, 1536)).copy()
    kvalnw = np.broadcast_to(fi["L1_attn_kv_a_ln_w"].astype(f32)[None, :],
                             (128, 512)).copy()
    kvcb = np.tile(fi["L0_attn_kvc_bias"].astype(f32).T, (1, 8)).copy()
    idxb = np.tile(fi["L0_attn_idx_c_bias"].astype(f32).T, (1, 8)).copy()
    gidxT = np.ascontiguousarray(fi["L0_ffn_gate_idx"].astype(np.int32).T)
    uidxT = np.ascontiguousarray(fi["L0_ffn_up_idx"].astype(np.int32).T)
    didxT_full = np.ascontiguousarray(fi["L0_ffn_down_idx"]
                                      .astype(np.int32).T)
    wupT_full = np.ascontiguousarray(fi["L1_ffn_w_up"].astype(f32).T)
    wdnT_full = np.ascontiguousarray(fi["L1_ffn_w_down"].astype(f32).T)
    Wkvb = fi["L1_attn_Wkvb"].astype(f32).reshape(512, 16, 256)
    Wkvb_n = np.ascontiguousarray(Wkvb[:, :, :128].reshape(512, 2048))
    Wkvb_v = np.ascontiguousarray(Wkvb[:, :, 128:].reshape(512, 2048))
    group_w = np.ascontiguousarray(
        fi["L0_attn_group_w"].astype(f32).reshape(2048, 512))

    common = {
        "w_qc": fi["L0_attn_w_qc"].astype(f32, copy=False),
        "w_qup": fi["L0_attn_w_qup"].astype(f32, copy=False),
        "kvc_w": fi["L0_attn_kvc_w"].astype(f32, copy=False),
        "kvc_wz": fi["L0_attn_kvc_wz"].astype(f32, copy=False),
        "k_proj_w": fi["L0_attn_k_proj_w"].astype(f32, copy=False),
        "v_proj_w": fi["L0_attn_v_proj_w"].astype(f32, copy=False),
        "idx_c_w": fi["L0_attn_idx_c_w"].astype(f32, copy=False),
        "idx_c_wz": fi["L0_attn_idx_c_wz"].astype(f32, copy=False),
        "w_dqT": np.ascontiguousarray(fi["L0_attn_w_dq"].astype(f32).T),
        "w_iuq": fi["L0_attn_w_iuq"].astype(f32, copy=False),
        "w_w": fi["L0_attn_w_w"].astype(f32, copy=False),
        "group_w": group_w,
        "final_w": fi["L0_attn_final_w"].astype(f32, copy=False),
        "Wqa": fi["L1_attn_Wqa"].astype(f32, copy=False),
        "Wqb": fi["L1_attn_Wqb"].astype(f32, copy=False),
        "Wkva": fi["L1_attn_Wkva"].astype(f32, copy=False),
        "Wkvb_n": Wkvb_n,
        "Wkvb_v": Wkvb_v,
        "Wo": fi["L1_attn_Wo"].astype(f32, copy=False),
        "w_up": fi["L1_ffn_w_up"].astype(f32, copy=False),
        "w_down": fi["L1_ffn_w_down"].astype(f32, copy=False),
        "lutcols": lutcols,
        "eye_f": eye,
        "had_f": had,
        "qnw_rep": qnw,
        "knw_rep": knw_rep,
        "knw_col": knw_col,
        "qalnw_rep": qalnw,
        "kvalnw_rep": kvalnw,
        "kvcb_rep": kvcb,
        "idxb_rep": idxb,
        "cs1k": cs1k,
    }

    percore_h = {}
    for h in range(2):
        (cs_own, cs_prv, spmask, spflag, wm0, wmN, l1mA,
         l1mB) = _core_tables(h, cosf, sinf)
        percore_h[h] = {
            "cs0own": cs_own, "cs0prv": cs_prv, "cs1q": cs_own,
            "spmask": spmask, "spflag": spflag, "wm0": wm0, "wmN": wmN,
            "l1mA": l1mA, "l1mB": l1mB,
        }

    maps = []
    for c in range(8):
        s, h = c // 2, c % 2
        xs = x[s]
        own = np.concatenate(
            [xs[(2 * t + h) * 128:(2 * t + h + 1) * 128] for t in range(16)])
        prv_tiles = []
        for t in range(16):
            g = 2 * t + h - 1
            if g < 0:
                prv_tiles.append(np.zeros((128, 2048), f32))
            else:
                prv_tiles.append(xs[g * 128:(g + 1) * 128])
        prv = np.concatenate(prv_tiles)
        m = dict(common)
        m.update(percore_h[h])
        m.update({
            "x_own": np.ascontiguousarray(own),
            "x_prev": np.ascontiguousarray(prv),
            "x_full": np.ascontiguousarray(xs),
            "gidxT": np.ascontiguousarray(gidxT[c * 256:(c + 1) * 256]),
            "uidxT": np.ascontiguousarray(uidxT[c * 256:(c + 1) * 256]),
            "didxT": np.ascontiguousarray(
                didxT_full[:, c * 256:(c + 1) * 256]).reshape(688, 2048),
            "wupT": np.ascontiguousarray(wupT_full[c * 256:(c + 1) * 256]),
            "wdnT": np.ascontiguousarray(wdnT_full[c * 512:(c + 1) * 512]),
        })
        maps.append(m)

    nc, io, out = build_program(stop=_STOP)
    res = run_bass_kernel_spmd(nc, maps, core_ids=list(range(8)))

    y = np.zeros((4, 4096, 2048), np.float32)
    for c in range(8):
        s, h = c // 2, c % 2
        oc = res.results[c]["out"]
        for t in range(16):
            y[s, (2 * t + h) * 128:(2 * t + h + 1) * 128] = \
                oc[t * 128:(t + 1) * 128]
    if _STOP is not None:
        return [res.results[c]["out"] for c in range(8)]
    return y


_STOP = None


if __name__ == "__main__":
    nc, io, out = build_program()
    print("BUILD OK")


# version 10
# speedup vs baseline: 1.0126x; 1.0126x over previous
import math
import numpy as np
import ml_dtypes

import concourse.bass as bass
import concourse.mybir as mybir
from concourse.bass_utils import run_bass_kernel_spmd

F32 = mybir.dt.float32
BF16 = mybir.dt.bfloat16
I32 = mybir.dt.int32
AF = mybir.ActivationFunctionType
OP = mybir.AluOpType
AX = mybir.AxisListType
BF = ml_dtypes.bfloat16

NEG = -1.0e30
SC0 = float(1.0 / np.float32(np.sqrt(128.0)))
SC1 = float(np.float64(192.0) ** -0.5)
INV127 = float(1.0 / 127.0)

G_ROWS, G_COLS = 256, 5504
D_ROWS, D_COLS = 688, 2048
TU_ROWS, TU_COLS = 256, 4096
TD_ROWS, TD_COLS = 512, 2048
OFF_G = 0
OFF_U = OFF_G + G_ROWS * G_COLS
OFF_D = OFF_U + G_ROWS * G_COLS
OFF_TU = OFF_D + D_ROWS * D_COLS
OFF_TD = OFF_TU + TU_ROWS * TU_COLS
PACK = OFF_TD + TD_ROWS * TD_COLS


class SemC:
    def __init__(self, nc, name):
        self.h = nc.semaphore(name).__enter__()
        self.n = 0

    def inc(self, instr, k=1):
        instr.then_inc(self.h, k)
        self.n += k
        return instr

    def wait(self, eng):
        if self.n:
            eng.wait_ge(self.h, self.n)

    def wait_at(self, eng, val):
        if val:
            eng.wait_ge(self.h, val)


class Psum:

    def __init__(self, buf):
        self.buf = buf
        self.dep = None

    def acq(self, eng):
        if self.dep is not None:
            self.dep[0].wait_at(eng, self.dep[1])
            self.dep = None

    def rel(self, sem):
        self.dep = (sem, sem.n)


def build_program(stop=None):
    nc = bass.Bass("TRN2", target_bir_lowering=False, debug=False)

    def din(name, shape, dt=F32):
        return nc.dram_tensor(name, shape, dt, kind="ExternalInput")

    io = {}
    io["w_qc"] = din("w_qc", [2048, 512])
    io["w_qup"] = din("w_qup", [512, 2048])
    io["kvc_w"] = din("kvc_w", [2048, 128])
    io["kvc_wz"] = din("kvc_wz", [2048, 128])
    io["k_proj_w"] = din("k_proj_w", [2048, 128])
    io["v_proj_w"] = din("v_proj_w", [2048, 128])
    io["idx_c_w"] = din("idx_c_w", [2048, 64])
    io["idx_c_wz"] = din("idx_c_wz", [2048, 64])
    io["w_dqT"] = din("w_dqT", [64, 2048])
    io["w_iuq"] = din("w_iuq", [64, 256])
    io["w_w"] = din("w_w", [2048, 4])
    io["group_w"] = din("group_w", [2048, 512])
    io["final_w"] = din("final_w", [2048, 2048])
    io["Wqa"] = din("Wqa", [2048, 1536])
    io["Wqb"] = din("Wqb", [1536, 3072])
    io["Wkva"] = din("Wkva", [2048, 576])
    io["Wkvb_n"] = din("Wkvb_n", [512, 2048])
    io["Wkvb_v"] = din("Wkvb_v", [512, 2048])
    io["Wo"] = din("Wo", [2048, 2048])
    io["w_up"] = din("w_up", [4096, 2048])
    io["w_down"] = din("w_down", [2048, 4096])
    io["gidxT"] = din("gidxT", [256, 5504], I32)
    io["uidxT"] = din("uidxT", [256, 5504], I32)
    io["didxT"] = din("didxT", [688, 2048], I32)
    io["wupT"] = din("wupT", [256, 4096])
    io["wdnT"] = din("wdnT", [512, 2048])
    io["lutcols"] = din("lutcols", [128, 16])
    io["eye_f"] = din("eye_f", [128, 128])
    io["had_f"] = din("had_f", [128, 128])
    io["qnw_rep"] = din("qnw_rep", [128, 2048])
    io["knw_rep"] = din("knw_rep", [128, 128])
    io["knw_col"] = din("knw_col", [128, 1])
    io["qalnw_rep"] = din("qalnw_rep", [128, 1536])
    io["kvalnw_rep"] = din("kvalnw_rep", [128, 512])
    io["kvcb_rep"] = din("kvcb_rep", [128, 128])
    io["idxb_rep"] = din("idxb_rep", [64, 128])
    io["cs0own"] = din("cs0own", [128, 1024])
    io["cs0prv"] = din("cs0prv", [128, 1024])
    io["cs1q"] = din("cs1q", [128, 1024])
    io["cs1k"] = din("cs1k", [4096, 64])
    io["spmask"] = din("spmask", [128, 4096])
    io["spflag"] = din("spflag", [128, 16])
    io["wm0"] = din("wm0", [128, 384])
    io["wmN"] = din("wmN", [128, 384])
    io["l1mA"] = din("l1mA", [128, 512])
    io["l1mB"] = din("l1mB", [128, 512])
    io["x_own"] = din("x_own", [2048, 2048])
    io["x_prev"] = din("x_prev", [2048, 2048])
    io["x_full"] = din("x_full", [4096, 2048])

    out = nc.dram_tensor("out", [2048, 2048], F32, kind="ExternalOutput")

    d_pack_src = nc.dram_tensor("d_pack_src", [1, PACK], BF16)
    d_pack = nc.dram_tensor("d_pack", [8, PACK], BF16)
    d_ckv_src = nc.dram_tensor("d_ckv_src", [2048, 576], F32)
    d_ckv = nc.dram_tensor("d_ckv", [2, 2048, 576], F32)
    d_x1 = nc.dram_tensor("d_x1", [2048, 2048], F32)
    d_oT = nc.dram_tensor("d_oT", [16, 128, 2048], BF16)
    d_qaT = nc.dram_tensor("d_qaT", [12, 128, 2048], BF16)
    d_kT = nc.dram_tensor("d_kT", [16, 128, 4096], BF16)
    d_V = nc.dram_tensor("d_V", [16, 4096, 128], BF16)
    d_qT = nc.dram_tensor("d_qT", [16, 192, 2048], BF16)
    d_xn0T = nc.dram_tensor("d_xn0T", [16, 128, 2048], BF16)
    d_xq1 = nc.dram_tensor("d_xq1", [16, 128, 2048], BF16)
    d_h2T = nc.dram_tensor("d_h2T", [32, 128, 2048], BF16)
    d_xq2 = nc.dram_tensor("d_xq2", [32, 128, 2048], BF16)

    t_done = SemC(nc, "t_done")
    v_done = SemC(nc, "v_done")
    s_done = SemC(nc, "s_done")
    g_done = SemC(nc, "g_done")
    q_dma = SemC(nc, "q_dma")
    g_dma = SemC(nc, "g_dma")
    st_dma = SemC(nc, "st_dma")
    cc_sem = SemC(nc, "cc_sem")
    cc_sem2 = SemC(nc, "cc_sem2")

    ALL = {}

    def sb(name, shape, dt):
        h = nc.sbuf_tensor(name, shape, dt).__enter__()
        ALL[name] = h
        return h

    c_eye_f = sb("c_eye_f", [128, 128], F32)
    c_eye_b = sb("c_eye_b", [128, 128], BF16)
    c_had_b = sb("c_had_b", [128, 128], BF16)
    c_lut = sb("c_lut", [128, 16], F32)
    c_scales = sb("c_scales", [128, 2], F32)
    c_knwc = sb("c_knwc", [128, 1], F32)
    c_ones = sb("c_ones", [128, 512], F32)

    PA = Psum(nc.psum_tensor("PA", [128, 512], F32).__enter__())
    PB = Psum(nc.psum_tensor("PB", [128, 512], F32).__enter__())
    PC = Psum(nc.psum_tensor("PC", [128, 512], F32).__enter__())
    PD = Psum(nc.psum_tensor("PD", [128, 512], F32).__enter__())
    PE2 = Psum(nc.psum_tensor("PE2", [128, 512], F32).__enter__())
    PF = Psum(nc.psum_tensor("PF", [128, 512], F32).__enter__())
    PTa = Psum(nc.psum_tensor("PTa", [128, 1024], BF16).__enter__())
    PTb = Psum(nc.psum_tensor("PTb", [128, 1024], BF16).__enter__())

    block = nc.Block().__enter__()

    def barrier():
        @block.vector
        def _(v):
            v.drain()
            v_done.inc(v.nop())
        @block.scalar
        def _(s):
            s.drain()
            s_done.inc(s.nop())
        @block.gpsimd
        def _(g):
            g_done.inc(g.nop())
        tv, sv, gv, ttv = v_done.n, s_done.n, g_done.n, t_done.n
        qv, gdv, stv = q_dma.n, g_dma.n, st_dma.n

        def wait_all(eng, skip):
            for sem, val in ((t_done, ttv), (v_done, tv), (s_done, sv),
                             (g_done, gv), (q_dma, qv), (g_dma, gdv),
                             (st_dma, stv)):
                if sem is not skip:
                    sem.wait_at(eng, val)
        @block.tensor
        def _(t):
            wait_all(t, t_done)
        @block.vector
        def _(v):
            wait_all(v, v_done)
        @block.scalar
        def _(s):
            wait_all(s, s_done)
        @block.gpsimd
        def _(g):
            wait_all(g, g_done)
        @block.sync
        def _(q):
            wait_all(q, None)

    def loadq(dst_ap, src_ap, wait=()):
        @block.sync
        def _(q):
            for sem, val in wait:
                sem.wait_at(q, val)
            q_dma.inc(q.dma_start(out=dst_ap, in_=src_ap), 16)

    def loadg(dst_ap, src_ap, wait=()):
        @block.gpsimd
        def _(g):
            for sem, val in wait:
                sem.wait_at(g, val)
            g_dma.inc(g.dma_start(out=dst_ap, in_=src_ap), 16)

    def storeg(dst_ap, src_ap, wait=()):
        @block.gpsimd
        def _(g):
            for sem, val in wait:
                sem.wait_at(g, val)
            st_dma.inc(g.dma_start(out=dst_ap, in_=src_ap), 16)

    def ap2(buf, off, dims):
        return bass.AP(buf[:].tensor, off, dims)

    def dump_and_stop(src_dram, nrows, ncols):
        @block.gpsimd
        def _(g):
            st_dma.inc(g.dma_start(out=out.ap()[0:nrows, 0:ncols],
                                   in_=src_dram.ap()), 16)
        barrier()

    def dump_and_stop2(src_dram, dims):
        n = 1
        for _, c in dims:
            n = max(n, c)
        @block.gpsimd
        def _(g):
            st_dma.inc(g.dma_start(out=ap2(out, 0, [[1, n]]),
                                   in_=ap2(src_dram, 0, dims)), 16)
        barrier()

    def layernorm(xbuf, D, outb, tmp, small, eps=1e-5, wait_dma=None):
        @block.vector
        def _(v):
            if wait_dma is not None:
                wait_dma[0].wait_at(v, wait_dma[1])
            v.reduce_sum(small[:, 0:1], xbuf[:, 0:D], axis=AX.X)
            v.tensor_mul(tmp[:, 0:D], xbuf[:, 0:D], xbuf[:, 0:D])
            v.drain()
            v.reduce_sum(small[:, 1:2], tmp[:, 0:D], axis=AX.X)
            v.tensor_scalar(out=small[:, 0:1], in0=small[:, 0:1],
                            scalar1=1.0 / D, scalar2=None, op0=OP.mult)
            v.drain()
            v.tensor_scalar(out=small[:, 2:3], in0=small[:, 1:2],
                            scalar1=1.0 / D, scalar2=None, op0=OP.mult)
            v.drain()
            v.tensor_mul(small[:, 3:4], small[:, 0:1], small[:, 0:1])
            v.drain()
            v.tensor_sub(small[:, 1:2], small[:, 2:3], small[:, 3:4])
            v.drain()
            v.tensor_scalar(out=small[:, 1:2], in0=small[:, 1:2],
                            scalar1=0.0, scalar2=eps, op0=OP.max, op1=OP.add)
            v.drain()
            v_done.inc(v.nop())
        vv = v_done.n
        @block.scalar
        def _(s):
            v_done.wait_at(s, vv)
            s.activation(out=small[:, 2:3], in_=small[:, 1:2], func=AF.Ln)
            s.drain()
            s.activation(out=small[:, 1:2], in_=small[:, 2:3], func=AF.Exp,
                         scale=-0.5)
            s.drain()
            s_done.inc(s.nop())
        svv = s_done.n
        @block.vector
        def _(v):
            s_done.wait_at(v, svv)
            v.tensor_scalar(out=outb[:, 0:D], in0=xbuf[:, 0:D],
                            scalar1=small[:, 0:1], scalar2=small[:, 1:2],
                            op0=OP.subtract, op1=OP.mult)
            v.drain()
            v_done.inc(v.nop())
        return v_done.n

    def rmsnorm_rows(xbuf, D, outb, tmp, small, wrep=None, eps=1e-6,
                     out_slice=None, x_slice=None):
        xa = xbuf if x_slice is None else x_slice
        oa = outb if out_slice is None else out_slice
        @block.vector
        def _(v):
            v.tensor_mul(tmp[:, 0:D], xa, xa)
            v.drain()
            v.reduce_sum(small[:, 0:1], tmp[:, 0:D], axis=AX.X)
            v.drain()
            v.tensor_scalar(out=small[:, 0:1], in0=small[:, 0:1],
                            scalar1=1.0 / D, scalar2=eps, op0=OP.mult,
                            op1=OP.add)
            v.drain()
            v_done.inc(v.nop())
        vv = v_done.n
        @block.scalar
        def _(s):
            v_done.wait_at(s, vv)
            s.activation(out=small[:, 1:2], in_=small[:, 0:1], func=AF.Ln)
            s.drain()
            s.activation(out=small[:, 0:1], in_=small[:, 1:2], func=AF.Exp,
                         scale=-0.5)
            s.drain()
            s_done.inc(s.nop())
        svv = s_done.n
        @block.vector
        def _(v):
            s_done.wait_at(v, svv)
            if wrep is None:
                v.tensor_scalar(out=oa, in0=xa, scalar1=small[:, 0:1],
                                scalar2=None, op0=OP.mult)
            else:
                v.scalar_tensor_tensor(out=oa, in0=xa, scalar=small[:, 0:1],
                                       in1=wrep, op0=OP.mult, op1=OP.mult)
            v.drain()
            v_done.inc(v.nop())
        return v_done.n

    def transp(items, PT):
        assert len(items) <= 8
        vw = v_done.n
        @block.tensor
        def _(t):
            v_done.wait_at(t, vw)
            PT.acq(t)
            for i, (src, _) in enumerate(items):
                ins = t.transpose(PT.buf[:, i * 128:(i + 1) * 128], src,
                                  c_eye_b[:])
            t_done.inc(ins)
        tv = t_done.n
        sw = st_dma.n
        @block.vector
        def _(v):
            t_done.wait_at(v, tv)
            st_dma.wait_at(v, sw)
            for i, (_, dst) in enumerate(items):
                v.tensor_copy(out=dst, in_=PT.buf[:, i * 128:(i + 1) * 128])
            v.drain()
            v_done.inc(v.nop())
        PT.rel(v_done)
        return v_done.n

    loadq(c_eye_f[:], io["eye_f"].ap())
    loadg(c_eye_b[:], io["eye_f"].ap())
    loadg(c_had_b[:], io["had_f"].ap())
    loadq(c_lut[:], io["lutcols"].ap())
    loadq(c_knwc[:], io["knw_col"].ap())

    @block.vector
    def _(v):
        v.memset(c_ones[:], 1.0)
        v.drain()
        v_done.inc(v.nop())

    with (
        nc.sbuf_tensor("p0_idx", [128, 5504], BF16) as p0_idx,
        nc.sbuf_tensor("p0_acc", [128, 5504], BF16) as p0_acc,
        nc.sbuf_tensor("p0_tmp", [128, 5504], BF16) as p0_tmp,
        nc.sbuf_tensor("p0_w", [128, 4096], F32) as p0_w,
        nc.sbuf_tensor("p0_wt", [128, 4096], BF16) as p0_wt,
        nc.sbuf_tensor("p0_t1", [128, 4096], F32) as p0_t1,
        nc.sbuf_tensor("p0_abs", [128, 48], F32) as p0_abs,
        nc.sbuf_tensor("p0_sml", [128, 8], F32) as p0_sml,
    ):
        regions = [
            (io["gidxT"], 256, 5504, OFF_G),
            (io["uidxT"], 256, 5504, OFF_U),
            (io["didxT"], 688, 2048, OFF_D),
        ]
        for (src, nr, ncol, poff) in regions:
            rt = 0
            while rt * 128 < nr:
                rows = min(128, nr - rt * 128)
                loadg(p0_idx[0:rows, 0:ncol],
                      src.ap()[rt * 128:rt * 128 + rows, :])
                gw = g_dma.n
                @block.vector
                def _(v, rows=rows, ncol=ncol, gw=gw, qw=q_dma.n,
                      sw=st_dma.n):
                    g_dma.wait_at(v, gw)
                    q_dma.wait_at(v, qw)
                    st_dma.wait_at(v, sw)
                    v.tensor_scalar(out=p0_acc[0:rows, 0:ncol],
                                    in0=p0_idx[0:rows, 0:ncol],
                                    scalar1=0.0, scalar2=c_lut[0:rows, 0:1],
                                    op0=OP.is_equal, op1=OP.mult)
                    for vv in range(1, 16):
                        v.tensor_scalar(out=p0_tmp[0:rows, 0:ncol],
                                        in0=p0_idx[0:rows, 0:ncol],
                                        scalar1=float(vv),
                                        scalar2=c_lut[0:rows, vv:vv + 1],
                                        op0=OP.is_equal, op1=OP.mult)
                        v.drain()
                        v.tensor_add(p0_acc[0:rows, 0:ncol],
                                     p0_acc[0:rows, 0:ncol],
                                     p0_tmp[0:rows, 0:ncol])
                        v.drain()
                    v_done.inc(v.nop())
                vv_ = v_done.n
                storeg(
                    ap2(d_pack_src, poff + rt * 128 * ncol,
                        [[ncol, rows], [1, ncol]]),
                    p0_acc[0:rows, 0:ncol],
                    wait=[(v_done, vv_)])
                @block.gpsimd
                def _(g, vv_=vv_):
                    v_done.wait_at(g, vv_)
                rt += 1

        for t in range(32):
            loadq(p0_w[:, 0:2048], io["w_up"].ap()[t * 128:(t + 1) * 128, :],
                  wait=[(v_done, v_done.n)])
            qv = q_dma.n
            @block.vector
            def _(v, t=t, qv=qv):
                q_dma.wait_at(v, qv)
                v.reduce_sum(p0_abs[:, t:t + 1], p0_w[:, 0:2048], axis=AX.X,
                             apply_absolute_value=True)
                v.drain()
                v_done.inc(v.nop())
        for t in range(16):
            loadq(p0_w[:, 0:4096], io["w_down"].ap()[t * 128:(t + 1) * 128, :],
                  wait=[(v_done, v_done.n)])
            qv = q_dma.n
            @block.vector
            def _(v, t=t, qv=qv):
                q_dma.wait_at(v, qv)
                v.reduce_sum(p0_abs[:, 32 + t:33 + t], p0_w[:, 0:4096],
                             axis=AX.X, apply_absolute_value=True)
                v.drain()
                v_done.inc(v.nop())
        @block.vector
        def _(v):
            v.reduce_sum(p0_sml[:, 0:1], p0_abs[:, 0:32], axis=AX.X)
            v.reduce_sum(p0_sml[:, 1:2], p0_abs[:, 32:48], axis=AX.X)
            v.drain()
            v.tensor_copy(out=p0_w[:, 0:2], in_=p0_sml[:, 0:2])
            v.drain()
            v_done.inc(v.nop())
        vv_ = v_done.n
        @block.tensor
        def _(t):
            v_done.wait_at(t, vv_)
            PA.acq(t)
            ins = t.matmul(PA.buf[0:1, 0:2], c_ones[:, 0:1],
                           p0_w[:, 0:2], start=True, stop=True)
            t_done.inc(ins)
        tv = t_done.n
        @block.vector
        def _(v):
            t_done.wait_at(v, tv)
            v.tensor_scalar(out=p0_sml[0:1, 0:2], in0=PA.buf[0:1, 0:2],
                            scalar1=1.0 / (4096.0 * 2048.0), scalar2=1e-5,
                            op0=OP.mult, op1=OP.max)
            v.drain()
            v_done.inc(v.nop())
        PA.rel(v_done)
        vv_ = v_done.n
        @block.tensor
        def _(t):
            v_done.wait_at(t, vv_)
            PB.acq(t)
            ins = t.matmul(PB.buf[:, 0:2], c_ones[0:1, 0:128],
                           p0_sml[0:1, 0:2], start=True, stop=True)
            t_done.inc(ins)
        tv = t_done.n
        @block.vector
        def _(v):
            t_done.wait_at(v, tv)
            v.tensor_copy(out=c_scales[:, 0:2], in_=PB.buf[:, 0:2])
            v.drain()
            v.tensor_scalar(out=p0_sml[:, 4:5], in0=c_scales[:, 0:1],
                            scalar1=0.5, scalar2=None, op0=OP.mult)
            v.tensor_scalar(out=p0_sml[:, 5:6], in0=c_scales[:, 0:1],
                            scalar1=-0.5, scalar2=None, op0=OP.mult)
            v.tensor_scalar(out=p0_sml[:, 6:7], in0=c_scales[:, 1:2],
                            scalar1=0.5, scalar2=None, op0=OP.mult)
            v.tensor_scalar(out=p0_sml[:, 7:8], in0=c_scales[:, 1:2],
                            scalar1=-0.5, scalar2=None, op0=OP.mult)
            v.drain()
            v_done.inc(v.nop())
        PB.rel(v_done)

        tern_jobs = [
            (io["wupT"], 2, 4096, OFF_TU, 4, 5),
            (io["wdnT"], 4, 2048, OFF_TD, 6, 7),
        ]
        for (src, ntile, ncol, poff, hi, lo) in tern_jobs:
            for t in range(ntile):
                loadq(p0_w[:, 0:ncol], src.ap()[t * 128:(t + 1) * 128, :],
                      wait=[(v_done, v_done.n)])
                qv = q_dma.n
                @block.vector
                def _(v, ncol=ncol, hi=hi, lo=lo, qv=qv, sw=st_dma.n):
                    q_dma.wait_at(v, qv)
                    st_dma.wait_at(v, sw)
                    v.tensor_scalar(out=p0_t1[:, 0:ncol], in0=p0_w[:, 0:ncol],
                                    scalar1=p0_sml[:, hi:hi + 1], scalar2=None,
                                    op0=OP.is_gt)
                    v.drain()
                    v.tensor_scalar(out=p0_w[:, 0:ncol], in0=p0_w[:, 0:ncol],
                                    scalar1=p0_sml[:, lo:lo + 1], scalar2=None,
                                    op0=OP.is_lt)
                    v.drain()
                    v.tensor_sub(p0_wt[:, 0:ncol], p0_t1[:, 0:ncol],
                                 p0_w[:, 0:ncol])
                    v.drain()
                    v_done.inc(v.nop())
                vv_ = v_done.n
                storeg(
                    ap2(d_pack_src, poff + t * 128 * ncol,
                        [[ncol, 128], [1, ncol]]),
                    p0_wt[:, 0:ncol], wait=[(v_done, vv_)])
                @block.gpsimd
                def _(g, vv_=vv_):
                    v_done.wait_at(g, vv_)

        stv = st_dma.n
        @block.gpsimd
        def _(g):
            st_dma.wait_at(g, stv)
            ins = g.collective_compute(
                "AllGather", OP.bypass, replica_groups=[list(range(8))],
                ins=[d_pack_src.ap().opt()], outs=[d_pack.ap().opt()])
            cc_sem.inc(ins)

    barrier()

    if stop == "ptu":
        @block.gpsimd
        def _(g):
            st_dma.inc(g.dma_start(
                out=ap2(out, 0, [[1, 1048576]]),
                in_=ap2(d_pack_src, OFF_TU, [[1, 1048576]])), 16)
            st_dma.inc(g.dma_start(
                out=ap2(out, 1048576, [[1, 1048576]]),
                in_=ap2(d_pack_src, OFF_TD, [[1, 1048576]])), 16)
        barrier()
        return nc, io, out

    CMS = {}
    CMS_ORDER = []

    def sba(name, shape, dt):
        cm = nc.sbuf_tensor(name, shape, dt)
        h = cm.__enter__()
        CMS[name] = cm
        CMS_ORDER.append(name)
        return h

    def sbfree(*names):
        for n in sorted(names, key=lambda x: -CMS_ORDER.index(x)):
            assert CMS_ORDER[-1] == n, (n, CMS_ORDER[-5:])
            CMS_ORDER.pop()
            CMS.pop(n).__exit__(None, None, None)

    xt = sba("xt", [128, 2048], F32)
    xsq = sba("xsq", [128, 2048], F32)
    xnb = sba("xnb", [128, 2048], BF16)
    xnTtmp = sba("xnTtmp", [128, 16 * 128], BF16)
    sml = sba("sml", [128, 16], F32)
    ev1 = sba("ev1", [128, 512], F32)
    ev2 = sba("ev2", [128, 512], F32)
    evb = sba("evb", [128, 512], BF16)
    pac = sba("pac", [128, 8], F32)
    xnTd = sba("xnTd", [128, 2 * 2048], BF16)

    def pacer(buf_slice, p=128):
        vv_ = v_done.n
        @block.scalar
        def _(s, vv_=vv_):
            v_done.wait_at(s, vv_)
            s.activation(out=pac[0:p, 0:8], in_=buf_slice, func=AF.Exp,
                         scale=0.0)
            s.drain()
            s_done.inc(s.nop())
        return s_done.n

    swkT_o = sba("swkT_o", [128, 17 * 128], BF16)
    swv_o = sba("swv_o", [128, 17 * 128], BF16)
    swkT_p = sba("swkT_p", [128, 16 * 128], BF16)
    swv_p = sba("swv_p", [128, 16 * 128], BF16)
    kcompT = sba("kcompT", [128, 256], F32)
    icompT = sba("icompT", [64, 256], F32)
    icompT_b = sba("icompT_b", [64, 256], BF16)
    skvT_b = sba("skvT_b", [128, 256], BF16)
    skv_b = sba("skv_b", [128, 256], BF16)
    tau = sba("tau", [128, 16], F32)
    cs0own = sba("b_cs0own", [128, 1024], F32)
    spflag = sba("b_spflag", [128, 16], F32)
    spm = sba("spm", [128, 256], F32)
    selmask = sba("selmask", [128, 4096], F32)

    cs0prv = sba("b_cs0prv", [128, 1024], F32)
    knw_rep = sba("b_knw_rep", [128, 128], F32)
    kvcb = sba("kvcb", [128, 128], F32)
    idxb = sba("idxb", [64, 128], F32)
    w_kvc = sba("w_kvc", [128, 16 * 128], BF16)
    w_kvcz = sba("w_kvcz", [128, 16 * 128], BF16)
    w_kp = sba("w_kp", [128, 16 * 128], BF16)
    w_vp = sba("w_vp", [128, 16 * 128], BF16)
    w_idx = sba("w_idx", [128, 16 * 64], BF16)
    w_idxz = sba("w_idxz", [128, 16 * 64], BF16)

    def ld16(dst, src, ncols, cast=True, col0=0):
        for kt in range(16):
            (loadg if cast else loadq)(
                dst[:, kt * ncols:(kt + 1) * ncols],
                src.ap()[kt * 128:(kt + 1) * 128, col0:col0 + ncols])

    ld16(w_kvc, io["kvc_w"], 128)
    ld16(w_kvcz, io["kvc_wz"], 128)
    ld16(w_kp, io["k_proj_w"], 128)
    ld16(w_vp, io["v_proj_w"], 128)
    ld16(w_idx, io["idx_c_w"], 64)
    ld16(w_idxz, io["idx_c_wz"], 64)
    loadq(cs0own[:], io["cs0own"].ap())
    loadq(cs0prv[:], io["cs0prv"].ap())
    loadq(spflag[:], io["spflag"].ap())
    loadq(knw_rep[:], io["knw_rep"].ap())
    loadq(kvcb[:], io["kvcb_rep"].ap())
    loadq(idxb[:], io["idxb_rep"].ap())

    @block.vector
    def _(v):
        v.memset(swkT_o[:, 0:128], 0.0)
        v.memset(swv_o[:, 0:128], 0.0)
        v.drain()
        v_done.inc(v.nop())

    barrier()

    def rope0(qdst, cs, csoff, nh, stride, tmpf, base=0):
        @block.vector
        def _(v):
            for hh in range(nh):
                o = base + hh * stride
                x1 = qdst[:, o:o + 32]
                x2 = qdst[:, o + 32:o + 64]
                cos = cs[:, csoff:csoff + 32]
                sin = cs[:, csoff + 32:csoff + 64]
                v.tensor_mul(tmpf[:, 0:32], x1, cos)
                v.tensor_mul(tmpf[:, 32:64], x2, sin)
                v.tensor_mul(tmpf[:, 64:96], x1, sin)
                v.tensor_mul(tmpf[:, 96:128], x2, cos)
                v.drain()
                v.tensor_sub(x1, tmpf[:, 0:32], tmpf[:, 32:64])
                v.tensor_add(x2, tmpf[:, 64:96], tmpf[:, 96:128])
                v.drain()
            v_done.inc(v.nop())
        return v_done.n

    def kv_proj_tile(kdstT_slice, vdst_slice, cs, csoff):
        vw = v_done.n
        @block.tensor
        def _(t, vw=vw):
            v_done.wait_at(t, vw)
            PA.acq(t)
            for kt in range(16):
                ins = t.matmul(PA.buf[:, 0:128],
                               xnTtmp[:, kt * 128:(kt + 1) * 128],
                               w_kp[:, kt * 128:(kt + 1) * 128],
                               start=(kt == 0), stop=(kt == 15))
            t_done.inc(ins)
            PB.acq(t)
            for kt in range(16):
                ins = t.matmul(PB.buf[:, 0:128],
                               xnTtmp[:, kt * 128:(kt + 1) * 128],
                               w_vp[:, kt * 128:(kt + 1) * 128],
                               start=(kt == 0), stop=(kt == 15))
            t_done.inc(ins)
        tv = t_done.n
        @block.vector
        def _(v):
            t_done.wait_at(v, tv)
            v.tensor_copy(out=ev1[:, 0:128], in_=PA.buf[:, 0:128])
            v.tensor_copy(out=vdst_slice, in_=PB.buf[:, 0:128])
            v.drain()
            v_done.inc(v.nop())
        PA.rel(v_done)
        PB.rel(v_done)
        rmsnorm_rows(ev1, 128, ev1, ev2, sml, wrep=knw_rep[:],
                     out_slice=ev1[:, 0:128], x_slice=ev1[:, 0:128])
        rope0(ev1, cs, csoff, 1, 128, ev2)
        @block.vector
        def _(v):
            v.tensor_copy(out=evb[:, 0:128], in_=ev1[:, 0:128])
            v.drain()
            v_done.inc(v.nop())
        transp([(evb[:, 0:128], kdstT_slice)], PTa)

    def matmul_wait(t, xn_wait):
        pass

    def ln_transpose_tile(src_dram_ap, dst16=None, store_tile=None):
        loadq(xt[:], src_dram_ap, wait=[(v_done, v_done.n)])
        layernorm(xt, 2048, xnb, xsq, sml, wait_dma=(q_dma, q_dma.n))
        for half in range(2):
            items = [(xnb[:, (half * 8 + j) * 128:(half * 8 + j + 1) * 128],
                      xnTtmp[:, (half * 8 + j) * 128:(half * 8 + j + 1) * 128])
                     for j in range(8)]
            transp(items, PTa if half == 0 else PTb)
        if store_tile is not None:
            tl = store_tile
            vv = v_done.n
            storeg(ap2(d_xn0T, tl * 128,
                       [[2048, 128], [128 * 2048, 16], [1, 128]]),
                   ap2(xnTtmp, 0, [[2048, 128], [128, 16], [1, 128]]),
                   wait=[(v_done, vv)])

    for tl in range(16):
        ln_transpose_tile(io["x_own"].ap()[tl * 128:(tl + 1) * 128, :],
                          store_tile=tl)
        kv_proj_tile(swkT_o[:, (tl + 1) * 128:(tl + 2) * 128],
                     swv_o[:, (tl + 1) * 128:(tl + 2) * 128],
                     cs0own, tl * 64)

    for tl in range(16):
        ln_transpose_tile(io["x_prev"].ap()[tl * 128:(tl + 1) * 128, :])
        kv_proj_tile(swkT_p[:, tl * 128:(tl + 1) * 128],
                     swv_p[:, tl * 128:(tl + 1) * 128],
                     cs0prv, tl * 64)

    def compress_tile(gt):
        ln_transpose_tile(io["x_full"].ap()[gt * 128:(gt + 1) * 128, :])
        vwc = v_done.n
        @block.tensor
        def _(t, vwc=vwc):
            v_done.wait_at(t, vwc)
            for (P, wmat, dpr, ncw) in ((PA, w_kvc, 128, 128),
                                        (PB, w_kvcz, 128, 128),
                                        (PC, w_idx, 64, 64),
                                        (PD, w_idxz, 64, 64)):
                P.acq(t)
                for kt in range(16):
                    ins = t.matmul(P.buf[0:dpr, 0:128],
                                   wmat[:, kt * ncw:(kt + 1) * ncw],
                                   xnTtmp[:, kt * 128:(kt + 1) * 128],
                                   start=(kt == 0), stop=(kt == 15))
                t_done.inc(ins)
        tv = t_done.n
        for (Pc, Pz, dpr, bias, dstT) in (
            (PA, PB, 128, kvcb, kcompT),
            (PC, PD, 64, idxb, icompT),
        ):
            @block.vector
            def _(v, Pz=Pz, dpr=dpr, bias=bias, tv=tv):
                t_done.wait_at(v, tv)
                v.tensor_add(ev1[0:dpr, 0:128], Pz.buf[0:dpr, 0:128],
                             bias[0:dpr, 0:128])
                v.drain()
                v_done.inc(v.nop())
            vv = v_done.n
            @block.scalar
            def _(s, dpr=dpr, vv=vv):
                v_done.wait_at(s, vv)
                s.activation(out=ev1[0:dpr, 0:128], in_=ev1[0:dpr, 0:128],
                             func=AF.Exp)
                s.drain()
                s_done.inc(s.nop())
            sv = s_done.n
            @block.vector
            def _(v, Pc=Pc, dpr=dpr, sv=sv, dstT=dstT, gt=gt):
                s_done.wait_at(v, sv)
                v.tensor_mul(ev2[0:dpr, 0:128], ev1[0:dpr, 0:128],
                             Pc.buf[0:dpr, 0:128])
                v.drain()
                v.reduce_sum(ap2(xsq, 0, [[2048, dpr], [1, 8]]),
                             ap2(ev2, 0, [[512, dpr], [16, 8], [1, 16]]),
                             axis=AX.X)
                v.reduce_sum(ap2(xsq, 8, [[2048, dpr], [1, 8]]),
                             ap2(ev1, 0, [[512, dpr], [16, 8], [1, 16]]),
                             axis=AX.X)
                v.drain()
                v.reciprocal(ap2(xsq, 16, [[2048, dpr], [1, 8]]),
                             ap2(xsq, 8, [[2048, dpr], [1, 8]]))
                v.drain()
                v.tensor_mul(dstT[0:dpr, gt * 8:(gt + 1) * 8],
                             ap2(xsq, 0, [[2048, dpr], [1, 8]]),
                             ap2(xsq, 16, [[2048, dpr], [1, 8]]))
                v.drain()
                v_done.inc(v.nop())
            Pc.rel(v_done)
            Pz.rel(v_done)

    for gt in range(32):
        compress_tile(gt)

    @block.vector
    def _(v):
        v.tensor_mul(ev1[:, 0:256], kcompT[:], kcompT[:])
        v.drain()
        v_done.inc(v.nop())
    vv = v_done.n
    @block.tensor
    def _(t):
        v_done.wait_at(t, vv)
        PA.acq(t)
        ins = t.matmul(PA.buf[0:1, 0:256], c_ones[:, 0:1], ev1[:, 0:256],
                       start=True, stop=True)
        t_done.inc(ins)
    tv = t_done.n
    @block.vector
    def _(v):
        t_done.wait_at(v, tv)
        v.tensor_scalar(out=ev2[0:1, 0:256], in0=PA.buf[0:1, 0:256],
                        scalar1=1.0 / 128.0, scalar2=1e-6, op0=OP.mult,
                        op1=OP.add)
        v.drain()
        v_done.inc(v.nop())
    PA.rel(v_done)
    vv = v_done.n
    @block.scalar
    def _(s):
        v_done.wait_at(s, vv)
        s.activation(out=ev2[0:1, 256:512], in_=ev2[0:1, 0:256], func=AF.Ln)
        s.drain()
        s.activation(out=ev2[0:1, 0:256], in_=ev2[0:1, 256:512], func=AF.Exp,
                     scale=-0.5)
        s.drain()
        s_done.inc(s.nop())
    sv = s_done.n
    @block.tensor
    def _(t):
        s_done.wait_at(t, sv)
        PB.acq(t)
        ins = t.matmul(PB.buf[:, 0:256], c_ones[0:1, 0:128],
                       ev2[0:1, 0:256], start=True, stop=True)
        t_done.inc(ins)
    tv = t_done.n
    @block.vector
    def _(v):
        t_done.wait_at(v, tv)
        v.scalar_tensor_tensor(out=ev1[:, 0:256], in0=kcompT[:],
                               scalar=c_knwc[:, 0:1], in1=PB.buf[:, 0:256],
                               op0=OP.mult, op1=OP.mult)
        v.drain()
        v.tensor_copy(out=skvT_b[:], in_=ev1[:, 0:256])
        v.tensor_copy(out=icompT_b[0:64, :], in_=icompT[0:64, :])
        v.drain()
        v_done.inc(v.nop())
    PB.rel(v_done)
    transp([(skvT_b[:, 0:128], skv_b[:, 0:128]),
            (skvT_b[:, 128:256], skv_b[:, 128:256])], PTa)

    barrier()
    sbfree("b_cs0prv", "b_knw_rep", "kvcb", "idxb", "w_kvc", "w_kvcz",
           "w_kp", "w_vp", "w_idx", "w_idxz")

    if stop == "xn0":
        dump_and_stop2(d_xn0T, [[1, 2048 * 2048]])
        return nc, io, out

    iscall = sba("iscall", [128, 4096], F32)
    xn_dma = SemC(nc, "xn_dma")
    w_dqTb = sba("w_dqTb", [64, 2048], BF16)
    w_iuqb = sba("w_iuqb", [64, 256], BF16)
    w_wb = sba("w_wb", [128, 16 * 4], BF16)
    weff = sba("weff", [128, 16 * 256], BF16)
    qiT = sba("qiT", [64, 512], BF16)
    wihb = sba("wihb", [128, 4], F32)
    dbg = sba("dbg", [128, 16 * 516 + 4096], F32) if stop == "p2dbg" else None

    loadg(w_dqTb[:], io["w_dqT"].ap())
    ld16(w_wb, io["w_w"], 4)
    loadg(w_iuqb[:], io["w_iuq"].ap())
    barrier()

    for kt in range(16):
        @block.tensor
        def _(t, kt=kt):
            PA.acq(t)
            t.matmul(PA.buf[0:128, 0:256],
                     w_dqTb[0:64, kt * 128:(kt + 1) * 128],
                     w_iuqb[0:64, 0:256], start=True, stop=True)
            ins2 = t.matmul(PA.buf[0:1, 256:257], c_ones[0:1, 0:1],
                            c_ones[0:1, 0:1], start=True, stop=True)
            t_done.inc(ins2)
        tv = t_done.n
        @block.vector
        def _(v, kt=kt, tv=tv):
            t_done.wait_at(v, tv)
            v.tensor_copy(out=ev1[:, 481:482], in_=PA.buf[0:128, 0:1])
            v.drain()
            v.tensor_copy(out=weff[:, kt * 256:(kt + 1) * 256],
                          in_=PA.buf[0:128, 0:256])
            v.drain()
            v_done.inc(v.nop())
        PA.rel(v_done)
    barrier()

    def xn_load(tl, bi):
        tw_ = t_done.n
        @block.sync
        def _(q, tl=tl, bi=bi, tw_=tw_):
            t_done.wait_at(q, tw_)
            xn_dma.inc(q.dma_start(
                out=xnTd[:, bi * 2048:(bi + 1) * 2048],
                in_=ap2(d_xn0T, tl * 128,
                        [[2048, 128], [128 * 2048, 16], [1, 128]])), 16)
        return xn_dma.n

    xn_rdy = [xn_load(0, 0)]
    for tl in range(16):
        bo = (tl % 2) * 2048
        if tl + 1 < 16:
            xn_rdy.append(xn_load(tl + 1, (tl + 1) % 2))
        qv = xn_rdy[tl]
        @block.tensor
        def _(t, bo=bo, qv=qv):
            xn_dma.wait_at(t, qv)
            PA.acq(t)
            for ih in range(4):
                for kt in range(16):
                    t.matmul(PA.buf[0:64, ih * 128:(ih + 1) * 128],
                             weff[:, kt * 256 + ih * 64:
                                  kt * 256 + (ih + 1) * 64],
                             xnTd[:, bo + kt * 128:bo + (kt + 1) * 128],
                             start=(kt == 0), stop=(kt == 15))
            PB.acq(t)
            for kt in range(16):
                t.matmul(PB.buf[0:128, 0:4],
                         xnTd[:, bo + kt * 128:bo + (kt + 1) * 128],
                         w_wb[:, kt * 4:(kt + 1) * 4],
                         start=(kt == 0), stop=(kt == 15))
            ins2 = t.matmul(PB.buf[0:1, 8:9], c_ones[0:1, 0:1],
                            c_ones[0:1, 0:1], start=True, stop=True)
            t_done.inc(ins2)
        tv = t_done.n
        @block.vector
        def _(v, tv=tv):
            t_done.wait_at(v, tv)
            v.tensor_copy(out=ev1[:, 481:482], in_=PB.buf[0:128, 0:1])
            v.drain()
            v.tensor_copy(out=qiT[0:64, :], in_=PA.buf[0:64, 0:512])
            v.tensor_copy(out=wihb[:], in_=PB.buf[0:128, 0:4])
            v.drain()
            v_done.inc(v.nop())
        PA.rel(v_done)
        PB.rel(v_done)
        if stop == "p2dbg":
            @block.vector
            def _(v, tl=tl, bo=bo):
                v.tensor_copy(out=dbg[:, tl * 516:tl * 516 + 4], in_=wihb[:])
                v.tensor_copy(out=dbg[0:64, tl * 516 + 4:tl * 516 + 516],
                              in_=qiT[0:64, :])
                if tl in (0, 8):
                    off = 16 * 516 + (tl // 8) * 2048
                    v.tensor_copy(out=dbg[:, off:off + 2048],
                                  in_=xnTd[:, bo:bo + 2048])
                v.drain()
                v_done.inc(v.nop())
        loadq(spm[:], io["spmask"].ap()[:, tl * 256:(tl + 1) * 256],
              wait=[(v_done, v_done.n)])
        qv = q_dma.n
        svq = pacer(qiT[0:64, 0:8], p=64)
        for ih in range(4):
            @block.tensor
            def _(t, ih=ih, svq=svq):
                s_done.wait_at(t, svq)
                PD.acq(t)
                ins = t.matmul(PD.buf[0:128, 0:256],
                               qiT[0:64, ih * 128:(ih + 1) * 128],
                               icompT_b[0:64, 0:256], start=True, stop=True)
                t_done.inc(ins)
            tv = t_done.n
            @block.vector
            def _(v, ih=ih, tv=tv, qv=qv, tl=tl):
                t_done.wait_at(v, tv)
                v.tensor_copy(out=ev1[:, 481:482], in_=PD.buf[0:128, 0:1])
                v.drain()
                v.tensor_relu(ev1[:, 0:256], PD.buf[0:128, 0:256])
                v.drain()
                if ih == 0:
                    v.tensor_scalar(out=ev2[:, 0:256], in0=ev1[:, 0:256],
                                    scalar1=wihb[:, 0:1], scalar2=None,
                                    op0=OP.mult)
                else:
                    v.scalar_tensor_tensor(out=ev2[:, 0:256],
                                           in0=ev1[:, 0:256],
                                           scalar=wihb[:, ih:ih + 1],
                                           in1=ev2[:, 0:256],
                                           op0=OP.mult, op1=OP.add)
                v.drain()
                if ih == 3:
                    q_dma.wait_at(v, qv)
                    v.tensor_add(iscall[:, tl * 256:(tl + 1) * 256],
                                 ev2[:, 0:256], spm[:])
                    v.drain()
                v_done.inc(v.nop())
            PD.rel(v_done)

    if stop == "p2dbg":
        barrier()
        NDBG = 16 * 516 + 4096
        storeg(ap2(d_x1, 0, [[NDBG, 128], [1, NDBG]]), dbg[:])
        dump_and_stop2(d_x1, [[1, 128 * NDBG]])
        return nc, io, out

    barrier()
    sbfree("w_dqTb", "w_iuqb", "w_wb", "weff", "qiT", "wihb")

    srtA = sba("srtA", [128, 2048], F32)
    srtB = sba("srtB", [128, 2048], F32)

    def sort_view(buf, k, j, bitk, bitj, nt):
        hi_cnt = max(256 // (2 * k), 1)
        mid_cnt = max(k // (2 * j), 1)
        off = bitk * k + bitj * j
        dims = [buf[:].ap[0], [256, nt]]
        if hi_cnt > 1:
            dims.append([2 * k, hi_cnt])
        if mid_cnt > 1:
            dims.append([2 * j, mid_cnt])
        dims.append([1, j])
        return bass.AP(buf[:].tensor, off, dims)

    for half in range(2):
        @block.vector
        def _(v, half=half):
            v.tensor_copy(out=srtA[:], in_=iscall[:, half * 2048:(half + 1) * 2048])
            v.drain()
            cur, oth = srtA, srtB
            k = 2
            while k <= 256:
                j = k // 2
                while j >= 1:
                    for bitk in (0, 1):
                        if k == 256 and bitk == 1:
                            continue
                        lo0 = sort_view(cur, k, j, bitk, 0, 8)
                        hi0 = sort_view(cur, k, j, bitk, 1, 8)
                        olo = sort_view(oth, k, j, bitk, 0, 8)
                        ohi = sort_view(oth, k, j, bitk, 1, 8)
                        if bitk == 0:
                            v.tensor_tensor(out=olo, in0=lo0, in1=hi0, op=OP.max)
                            v.tensor_tensor(out=ohi, in0=lo0, in1=hi0, op=OP.min)
                        else:
                            v.tensor_tensor(out=olo, in0=lo0, in1=hi0, op=OP.min)
                            v.tensor_tensor(out=ohi, in0=lo0, in1=hi0, op=OP.max)
                    v.drain()
                    cur, oth = oth, cur
                    j //= 2
                k *= 2
            assert cur is srtA
            v.tensor_copy(out=tau[:, half * 8:(half + 1) * 8],
                          in_=ap2(srtA, 63, [[2048, 128], [256, 8]]))
            v.drain()
            v_done.inc(v.nop())

    for tl in range(16):
        loadq(spm[:], io["spmask"].ap()[:, tl * 256:(tl + 1) * 256],
              wait=[(v_done, v_done.n)])
        qv = q_dma.n
        @block.vector
        def _(v, tl=tl, qv=qv):
            q_dma.wait_at(v, qv)
            isl = iscall[:, tl * 256:(tl + 1) * 256]
            gt_ = srtA[:, 0:256]
            eq_ = srtA[:, 256:512]
            incl = srtA[:, 512:768]
            ind = srtA[:, 768:1024]
            v.tensor_scalar(out=gt_, in0=isl, scalar1=tau[:, tl:tl + 1],
                            scalar2=None, op0=OP.is_gt)
            v.tensor_scalar(out=eq_, in0=isl, scalar1=tau[:, tl:tl + 1],
                            scalar2=None, op0=OP.is_equal)
            v.drain()
            v.reduce_sum(sml[:, 0:1], gt_, axis=AX.X)
            v.tensor_tensor_scan(out=incl, data0=eq_, data1=c_ones[:, 0:256],
                                 initial=0.0, op0=OP.add, op1=OP.mult)
            v.drain()
            v.tensor_scalar(out=sml[:, 1:2], in0=sml[:, 0:1], scalar1=-1.0,
                            scalar2=64.0, op0=OP.mult, op1=OP.add)
            v.tensor_sub(incl, incl, eq_)
            v.drain()
            v.tensor_scalar(out=ind, in0=incl, scalar1=sml[:, 1:2],
                            scalar2=None, op0=OP.is_lt)
            v.drain()
            v.tensor_mul(eq_, eq_, ind)
            v.drain()
            v.tensor_add(gt_, gt_, eq_)
            v.drain()
            v.tensor_scalar(out=gt_, in0=gt_, scalar1=1.0, scalar2=1.0e30,
                            op0=OP.subtract, op1=OP.mult)
            v.drain()
            v.tensor_add(selmask[:, tl * 256:(tl + 1) * 256], gt_, spm[:])
            v.drain()
            v_done.inc(v.nop())

    if stop in ("isc", "sel"):
        buf = iscall if stop == "isc" else selmask
        vvx = v_done.n
        storeg(ap2(d_x1, 0, [[4096, 128], [1, 4096]]), buf[:],
               wait=[(v_done, vvx)])
        dump_and_stop2(d_x1, [[1, 2048 * 2048]])
        return nc, io, out

    barrier()
    sbfree("srtA", "srtB", "iscall")

    w_qcb = sba("w_qcb", [128, 16 * 512], BF16)
    w_qupb = sba("w_qupb", [128, 4 * 2048], BF16)
    w_gwb = sba("w_gwb", [128, 16 * 512], BF16)
    b_qnw = sba("b_qnw", [128, 2048], F32)
    b_wm0 = sba("b_wm0", [128, 384], F32)
    b_wmN = sba("b_wmN", [128, 384], F32)
    qT = sba("qT", [128, 16 * 128], BF16)
    qcT = sba("qcT", [128, 4 * 128], BF16)
    ptb = sba("ptb", [128, 384], BF16)
    attnac = sba("attnac", [128, 2048], F32)

    ld16(w_qcb, io["w_qc"], 512)
    for j in range(4):
        loadg(w_qupb[:, j * 2048:(j + 1) * 2048],
              io["w_qup"].ap()[j * 128:(j + 1) * 128, :])
    ld16(w_gwb, io["group_w"], 512)
    loadq(b_qnw[:], io["qnw_rep"].ap())
    loadq(b_wm0[:], io["wm0"].ap())
    loadq(b_wmN[:], io["wmN"].ap())
    barrier()

    xn_rdy = [xn_load(0, 0)]
    for tl in range(16):
        bo = (tl % 2) * 2048
        if tl + 1 < 16:
            xn_rdy.append(xn_load(tl + 1, (tl + 1) % 2))
        qv = xn_rdy[tl]
        @block.tensor
        def _(t, bo=bo, qv=qv):
            xn_dma.wait_at(t, qv)
            PA.acq(t)
            for kt in range(16):
                ins = t.matmul(PA.buf[:, 0:512],
                               xnTd[:, bo + kt * 128:bo + (kt + 1) * 128],
                               w_qcb[:, kt * 512:(kt + 1) * 512],
                               start=(kt == 0), stop=(kt == 15))
            t_done.inc(ins)
        tv = t_done.n
        @block.vector
        def _(v, tv=tv):
            t_done.wait_at(v, tv)
            v.tensor_copy(out=evb[:, 0:512], in_=PA.buf[:, 0:512])
            v.drain()
            v_done.inc(v.nop())
        PA.rel(v_done)
        transp([(evb[:, j * 128:(j + 1) * 128], qcT[:, j * 128:(j + 1) * 128])
                for j in range(4)], PTa)
        vwq = v_done.n
        for ch in range(4):
            @block.tensor
            def _(t, ch=ch, vwq=vwq):
                v_done.wait_at(t, vwq)
                PB.acq(t)
                for j in range(4):
                    ins = t.matmul(PB.buf[:, 0:512],
                                   qcT[:, j * 128:(j + 1) * 128],
                                   w_qupb[:, j * 2048 + ch * 512:
                                          j * 2048 + (ch + 1) * 512],
                                   start=(j == 0), stop=(j == 3))
                t_done.inc(ins)
            tv = t_done.n
            @block.vector
            def _(v, ch=ch, tv=tv):
                t_done.wait_at(v, tv)
                v.tensor_copy(out=xt[:, ch * 512:(ch + 1) * 512],
                              in_=PB.buf[:, 0:512])
                v.drain()
                v_done.inc(v.nop())
            PB.rel(v_done)
        rope0(xt, cs0own, tl * 64, 16, 128, ev1)
        @block.vector
        def _(v):
            v.tensor_mul(xsq[:], xt[:], xt[:])
            v.drain()
            v.reduce_sum(ap2(sml, 0, [[16, 128], [1, 16]]),
                         ap2(xsq, 0, [[2048, 128], [128, 16], [1, 128]]),
                         axis=AX.X)
            v.drain()
            v.tensor_scalar(out=sml[:, 0:16], in0=sml[:, 0:16],
                            scalar1=1.0 / 128.0, scalar2=1e-6,
                            op0=OP.mult, op1=OP.add)
            v.drain()
            v_done.inc(v.nop())
        vv = v_done.n
        @block.scalar
        def _(s, vv=vv):
            v_done.wait_at(s, vv)
            s.activation(out=sml[:, 0:16], in_=sml[:, 0:16], func=AF.Ln)
            s.drain()
            s.activation(out=sml[:, 0:16], in_=sml[:, 0:16], func=AF.Exp,
                         scale=-0.5)
            s.drain()
            s_done.inc(s.nop())
        sv = s_done.n
        @block.vector
        def _(v, sv=sv):
            s_done.wait_at(v, sv)
            for hh in range(16):
                v.tensor_scalar(out=xsq[:, hh * 128:(hh + 1) * 128],
                                in0=xt[:, hh * 128:(hh + 1) * 128],
                                scalar1=sml[:, hh:hh + 1], scalar2=None,
                                op0=OP.mult)
            v.drain()
            v.tensor_mul(xnb[:], xsq[:], b_qnw[:])
            v.drain()
            v_done.inc(v.nop())
        for half in range(2):
            transp([(xnb[:, (half * 8 + j) * 128:(half * 8 + j + 1) * 128],
                     qT[:, (half * 8 + j) * 128:(half * 8 + j + 1) * 128])
                    for j in range(8)], PTa if half == 0 else PTb)
        vwh = v_done.n
        for hh in range(16):
            qTh = qT[:, hh * 128:(hh + 1) * 128]
            @block.tensor
            def _(t, qTh=qTh, vwh=vwh):
                v_done.wait_at(t, vwh)
                PC.acq(t)
                ins = t.matmul(PC.buf[:, 0:256], qTh, skvT_b[:, 0:256],
                               start=True, stop=True)
                t_done.inc(ins)
            tv = t_done.n
            @block.vector
            def _(v, tv=tv, tl=tl):
                t_done.wait_at(v, tv)
                v.scalar_tensor_tensor(out=ev1[:, 0:256],
                                       in0=PC.buf[:, 0:256], scalar=SC0,
                                       in1=selmask[:, tl * 256:(tl + 1) * 256],
                                       op0=OP.mult, op1=OP.add)
                v.drain()
                v.reduce_max(sml[:, 0:1], ev1[:, 0:256], axis=AX.X,
                             negate=True)
                v.drain()
                v_done.inc(v.nop())
            PC.rel(v_done)
            vv = v_done.n
            @block.scalar
            def _(s, vv=vv):
                v_done.wait_at(s, vv)
                s.activation(out=evb[:, 0:256], in_=ev1[:, 0:256], func=AF.Exp,
                             bias=sml[:, 0:1], accum_out=sml[:, 1:2])
                s.drain()
                s_done.inc(s.nop())
            sv = s_done.n
            @block.tensor
            def _(t, sv=sv):
                s_done.wait_at(t, sv)
                PTa.acq(t)
                ins = t.transpose(PTa.buf[:, 0:128], evb[:, 0:128], c_eye_b[:])
                ins = t.transpose(PTa.buf[:, 128:256], evb[:, 128:256],
                                  c_eye_b[:])
                t_done.inc(ins)
            tv = t_done.n
            @block.vector
            def _(v, tv=tv, sv=sv):
                t_done.wait_at(v, tv)
                s_done.wait_at(v, sv)
                v.tensor_copy(out=ptb[:, 0:256], in_=PTa.buf[:, 0:256])
                v.drain()
                v.reciprocal(sml[:, 2:3], sml[:, 1:2])
                v.drain()
                v.tensor_mul(sml[:, 3:4], sml[:, 2:3], spflag[:, tl:tl + 1])
                v.drain()
                v_done.inc(v.nop())
            PTa.rel(v_done)
            vv = v_done.n
            @block.tensor
            def _(t, vv=vv):
                v_done.wait_at(t, vv)
                PD.acq(t)
                ins = t.matmul(PD.buf[:, 0:128], ptb[:, 0:128],
                               skv_b[:, 0:128], start=True, stop=False)
                ins = t.matmul(PD.buf[:, 0:128], ptb[:, 128:256],
                               skv_b[:, 128:256], start=False, stop=True)
                t_done.inc(ins)
            tv = t_done.n
            @block.vector
            def _(v, tv=tv, hh=hh):
                t_done.wait_at(v, tv)
                v.tensor_scalar(out=attnac[:, hh * 128:(hh + 1) * 128],
                                in0=PD.buf[:, 0:128], scalar1=sml[:, 3:4],
                                scalar2=None, op0=OP.mult)
                v.drain()
                v_done.inc(v.nop())
            PD.rel(v_done)
            @block.tensor
            def _(t, qTh=qTh, tl=tl):
                PE2.acq(t)
                ins = t.matmul(PE2.buf[:, 0:128], qTh,
                               swkT_o[:, tl * 128:(tl + 1) * 128],
                               start=True, stop=True)
                ins = t.matmul(PE2.buf[:, 128:256], qTh,
                               swkT_p[:, tl * 128:(tl + 1) * 128],
                               start=True, stop=True)
                ins = t.matmul(PE2.buf[:, 256:384], qTh,
                               swkT_o[:, (tl + 1) * 128:(tl + 2) * 128],
                               start=True, stop=True)
                t_done.inc(ins)
            tv = t_done.n
            wmask = b_wm0 if tl == 0 else b_wmN
            @block.vector
            def _(v, tv=tv, wmask=wmask):
                t_done.wait_at(v, tv)
                v.scalar_tensor_tensor(out=ev1[:, 0:384],
                                       in0=PE2.buf[:, 0:384], scalar=SC0,
                                       in1=wmask[:, 0:384],
                                       op0=OP.mult, op1=OP.add)
                v.drain()
                v.reduce_max(sml[:, 4:5], ev1[:, 0:384], axis=AX.X,
                             negate=True)
                v.drain()
                v_done.inc(v.nop())
            PE2.rel(v_done)
            vv = v_done.n
            @block.scalar
            def _(s, vv=vv):
                v_done.wait_at(s, vv)
                s.activation(out=evb[:, 0:384], in_=ev1[:, 0:384], func=AF.Exp,
                             bias=sml[:, 4:5], accum_out=sml[:, 5:6])
                s.drain()
                s_done.inc(s.nop())
            sv = s_done.n
            @block.tensor
            def _(t, sv=sv):
                s_done.wait_at(t, sv)
                PTb.acq(t)
                for j in range(3):
                    ins = t.transpose(PTb.buf[:, j * 128:(j + 1) * 128],
                                      evb[:, j * 128:(j + 1) * 128], c_eye_b[:])
                t_done.inc(ins)
            tv = t_done.n
            @block.vector
            def _(v, tv=tv):
                t_done.wait_at(v, tv)
                v.tensor_copy(out=ptb[:, 0:384], in_=PTb.buf[:, 0:384])
                v.drain()
                v.reciprocal(sml[:, 6:7], sml[:, 5:6])
                v.drain()
                v_done.inc(v.nop())
            PTb.rel(v_done)
            vv = v_done.n
            @block.tensor
            def _(t, vv=vv, tl=tl):
                v_done.wait_at(t, vv)
                PF.acq(t)
                ins = t.matmul(PF.buf[:, 0:128], ptb[:, 0:128],
                               swv_o[:, tl * 128:(tl + 1) * 128],
                               start=True, stop=False)
                ins = t.matmul(PF.buf[:, 0:128], ptb[:, 128:256],
                               swv_p[:, tl * 128:(tl + 1) * 128],
                               start=False, stop=False)
                ins = t.matmul(PF.buf[:, 0:128], ptb[:, 256:384],
                               swv_o[:, (tl + 1) * 128:(tl + 2) * 128],
                               start=False, stop=True)
                t_done.inc(ins)
            tv = t_done.n
            @block.vector
            def _(v, tv=tv, hh=hh):
                t_done.wait_at(v, tv)
                v.scalar_tensor_tensor(
                    out=attnac[:, hh * 128:(hh + 1) * 128],
                    in0=PF.buf[:, 0:128], scalar=sml[:, 6:7],
                    in1=attnac[:, hh * 128:(hh + 1) * 128],
                    op0=OP.mult, op1=OP.add)
                v.drain()
                v_done.inc(v.nop())
            PF.rel(v_done)
        for gg in range(4):
            @block.vector
            def _(v, gg=gg):
                v.tensor_copy(out=evb[:, 0:512],
                              in_=attnac[:, gg * 512:(gg + 1) * 512])
                v.drain()
                v_done.inc(v.nop())
            transp([(evb[:, j * 128:(j + 1) * 128],
                     qcT[:, j * 128:(j + 1) * 128]) for j in range(4)], PTa)
            vv = v_done.n
            @block.tensor
            def _(t, gg=gg, vv=vv):
                v_done.wait_at(t, vv)
                PA.acq(t)
                for j in range(4):
                    ins = t.matmul(PA.buf[:, 0:512],
                                   qcT[:, j * 128:(j + 1) * 128],
                                   w_gwb[:, (4 * gg + j) * 512:
                                          (4 * gg + j + 1) * 512],
                                   start=(j == 0), stop=(j == 3))
                t_done.inc(ins)
            tv = t_done.n
            @block.vector
            def _(v, tv=tv):
                t_done.wait_at(v, tv)
                v.tensor_copy(out=evb[:, 0:512], in_=PA.buf[:, 0:512])
                v.drain()
                v_done.inc(v.nop())
            PA.rel(v_done)
            vv = v_done.n
            @block.tensor
            def _(t, vv=vv):
                v_done.wait_at(t, vv)
                PTb.acq(t)
                for j in range(4):
                    ins = t.transpose(PTb.buf[:, j * 128:(j + 1) * 128],
                                      evb[:, j * 128:(j + 1) * 128],
                                      c_eye_b[:])
                t_done.inc(ins)
            tv = t_done.n
            @block.vector
            def _(v, tv=tv, sw=st_dma.n):
                t_done.wait_at(v, tv)
                st_dma.wait_at(v, sw)
                v.tensor_copy(out=xnb[:, 0:512], in_=PTb.buf[:, 0:512])
                v.drain()
                v_done.inc(v.nop())
            PTb.rel(v_done)
            vv = v_done.n
            for j in range(4):
                storeg(ap2(d_oT, (gg * 4 + j) * 128 * 2048 + tl * 128,
                           [[2048, 128], [1, 128]]),
                       xnb[:, j * 128:(j + 1) * 128], wait=[(v_done, vv)])

    barrier()
    sbfree("attnac", "ptb", "qcT", "qT", "b_wmN", "b_wm0", "b_qnw",
           "w_gwb", "w_qupb", "w_qcb")
    sbfree("spm", "selmask", "b_spflag", "b_cs0own", "tau", "skv_b",
           "skvT_b", "icompT_b", "icompT", "kcompT", "swv_p", "swkT_p",
           "swv_o", "swkT_o")

    oTsb = sba("oTsb", [128, 16 * 2048], BF16)
    fwb = sba("fwb", [128, 16 * 512], BF16)

    for ft in range(16):
        loadq(oTsb[:, ft * 2048:(ft + 1) * 2048],
              ap2(d_oT, ft * 128 * 2048, [[2048, 128], [1, 2048]]))
    barrier()

    for ch in range(4):
        for kt in range(16):
            loadg(fwb[:, kt * 512:(kt + 1) * 512],
                  io["final_w"].ap()[kt * 128:(kt + 1) * 128,
                                     ch * 512:(ch + 1) * 512],
                  wait=[(t_done, t_done.n)])
        gv = g_dma.n
        for tl in range(16):
            @block.tensor
            def _(t, tl=tl, gv=gv):
                g_dma.wait_at(t, gv)
                PA.acq(t)
                for kt in range(16):
                    ins = t.matmul(PA.buf[:, 0:512],
                                   oTsb[:, kt * 2048 + tl * 128:
                                        kt * 2048 + (tl + 1) * 128],
                                   fwb[:, kt * 512:(kt + 1) * 512],
                                   start=(kt == 0), stop=(kt == 15))
                t_done.inc(ins)
            tv = t_done.n
            loadq(ev1[:, 0:512],
                  io["x_own"].ap()[tl * 128:(tl + 1) * 128,
                                   ch * 512:(ch + 1) * 512],
              wait=[(v_done, v_done.n)])
            qv = q_dma.n
            @block.vector
            def _(v, tv=tv, qv=qv, sw=st_dma.n):
                t_done.wait_at(v, tv)
                q_dma.wait_at(v, qv)
                st_dma.wait_at(v, sw)
                v.tensor_add(ev2[:, 0:512], PA.buf[:, 0:512], ev1[:, 0:512])
                v.drain()
                v_done.inc(v.nop())
            PA.rel(v_done)
            vv = v_done.n
            storeg(ap2(d_x1, tl * 128 * 2048 + ch * 512,
                       [[2048, 128], [1, 512]]),
                   ev2[:, 0:512], wait=[(v_done, vv)])

    barrier()
    sbfree("fwb", "oTsb")

    if stop == "x0a":
        dump_and_stop(d_x1, 2048, 2048)
        return nc, io, out

    xn0Th = sba("xn0Th", [128, 16 * 1024], BF16)
    hT = sba("hT", [128, 43 * 1024], BF16)

    @block.gpsimd
    def _(g):
        cc_sem.wait_at(g, 1)
        g_done.inc(g.nop())
    gv0 = g_done.n
    @block.sync
    def _(q):
        g_done.wait_at(q, gv0)

    def pack_g(shard, off, row0, nrows, col0, ncols, rowstride):
        return ap2(d_pack, shard * PACK + off + row0 * rowstride + col0,
                   [[rowstride, nrows], [1, ncols]])

    for half in range(2):
        for kt in range(16):
            loadq(xn0Th[:, kt * 1024:(kt + 1) * 1024],
                  ap2(d_xn0T, kt * 128 * 2048 + half * 1024,
                      [[2048, 128], [1, 1024]]),
                  wait=[(t_done, t_done.n)])
        with (
            nc.sbuf_tensor(f"wg_c{half}", [128, 16 * 512], BF16) as wg_c,
            nc.sbuf_tensor(f"wu_c{half}", [128, 16 * 512], BF16) as wu_c,
        ):
            for oc in range(11):
                ocn = 512 if oc < 10 else 384
                for s8 in range(8):
                    for r2 in range(2):
                        kt = s8 * 2 + r2
                        loadq(wg_c[:, kt * 512:kt * 512 + ocn],
                              pack_g(s8, OFF_G, r2 * 128, 128, oc * 512,
                                     ocn, 5504),
                              wait=[(t_done, t_done.n)])
                        loadq(wu_c[:, kt * 512:kt * 512 + ocn],
                              pack_g(s8, OFF_U, r2 * 128, 128, oc * 512,
                                     ocn, 5504))
                qv = q_dma.n
                for tt in range(8):
                    @block.tensor
                    def _(t, tt=tt, ocn=ocn, qv=qv):
                        q_dma.wait_at(t, qv)
                        PA.acq(t)
                        for kt in range(16):
                            ins = t.matmul(
                                PA.buf[:, 0:ocn],
                                xn0Th[:, kt * 1024 + tt * 128:
                                      kt * 1024 + (tt + 1) * 128],
                                wg_c[:, kt * 512:kt * 512 + ocn],
                                start=(kt == 0), stop=(kt == 15))
                        t_done.inc(ins)
                        PB.acq(t)
                        for kt in range(16):
                            ins = t.matmul(
                                PB.buf[:, 0:ocn],
                                xn0Th[:, kt * 1024 + tt * 128:
                                      kt * 1024 + (tt + 1) * 128],
                                wu_c[:, kt * 512:kt * 512 + ocn],
                                start=(kt == 0), stop=(kt == 15))
                        t_done.inc(ins)
                    tv = t_done.n
                    @block.scalar
                    def _(s, tv=tv, ocn=ocn):
                        t_done.wait_at(s, tv)
                        s.activation(out=ev1[:, 0:ocn], in_=PA.buf[:, 0:ocn],
                                     func=AF.Silu)
                        s.drain()
                        s_done.inc(s.nop())
                    sv = s_done.n
                    @block.vector
                    def _(v, sv=sv, ocn=ocn):
                        s_done.wait_at(v, sv)
                        v.tensor_mul(evb[:, 0:ocn], ev1[:, 0:ocn],
                                     PB.buf[:, 0:ocn])
                        v.drain()
                        v_done.inc(v.nop())
                    PA.rel(v_done)
                    PB.rel(v_done)
                    nj = ocn // 128
                    transp([(evb[:, j * 128:(j + 1) * 128],
                             hT[:, (oc * 4 + j) * 1024 + tt * 128:
                                (oc * 4 + j) * 1024 + (tt + 1) * 128])
                            for j in range(nj)], PTa)
        with nc.sbuf_tensor(f"wd_c{half}", [128, 43 * 512], BF16) as wd_c:
            for oc2 in range(4):
                for rt in range(43):
                    for s2 in range(2):
                        s8 = oc2 * 2 + s2
                        loadq(wd_c[:, rt * 512 + s2 * 256:
                                   rt * 512 + (s2 + 1) * 256],
                              pack_g(s8, OFF_D, rt * 128, 128, 0, 256, 256),
                              wait=[(t_done, t_done.n)])
                qv = q_dma.n
                vwd = v_done.n
                for tt in range(8):
                    @block.tensor
                    def _(t, tt=tt, qv=qv, vwd=vwd):
                        q_dma.wait_at(t, qv)
                        v_done.wait_at(t, vwd)
                        PC.acq(t)
                        for rt in range(43):
                            ins = t.matmul(
                                PC.buf[:, 0:512],
                                hT[:, rt * 1024 + tt * 128:
                                   rt * 1024 + (tt + 1) * 128],
                                wd_c[:, rt * 512:(rt + 1) * 512],
                                start=(rt == 0), stop=(rt == 42))
                        t_done.inc(ins)
                    tv = t_done.n
                    tl = half * 8 + tt
                    loadq(ev1[:, 0:512],
                          ap2(d_x1, tl * 128 * 2048 + oc2 * 512,
                              [[2048, 128], [1, 512]]),
                          wait=[(v_done, v_done.n)])
                    qv2 = q_dma.n
                    @block.vector
                    def _(v, tv=tv, qv2=qv2, sw=st_dma.n):
                        t_done.wait_at(v, tv)
                        q_dma.wait_at(v, qv2)
                        st_dma.wait_at(v, sw)
                        v.tensor_add(ev2[:, 0:512], PC.buf[:, 0:512],
                                     ev1[:, 0:512])
                        v.drain()
                        v_done.inc(v.nop())
                    PC.rel(v_done)
                    vv = v_done.n
                    storeg(ap2(d_x1, tl * 128 * 2048 + oc2 * 512,
                               [[2048, 128], [1, 512]]),
                           ev2[:, 0:512], wait=[(v_done, vv)])

    barrier()
    sbfree("hT", "xn0Th")


    if stop == "x1":
        dump_and_stop(d_x1, 2048, 2048)
        return nc, io, out

    w_kva = sba("w_kva", [128, 16 * 576], BF16)
    ckvb = sba("ckvb", [128, 576], F32)
    for kt in range(16):
        loadg(w_kva[:, kt * 576:(kt + 1) * 576],
              io["Wkva"].ap()[kt * 128:(kt + 1) * 128, :])
    barrier()

    for tl in range(16):
        loadq(xt[:], ap2(d_x1, tl * 128 * 2048, [[2048, 128], [1, 2048]]),
              wait=[(v_done, v_done.n)])
        layernorm(xt, 2048, xnb, xsq, sml, wait_dma=(q_dma, q_dma.n))
        for half in range(2):
            items = [(xnb[:, (half * 8 + j) * 128:(half * 8 + j + 1) * 128],
                      xnTtmp[:, (half * 8 + j) * 128:(half * 8 + j + 1) * 128])
                     for j in range(8)]
            transp(items, PTa if half == 0 else PTb)
        vv = v_done.n
        storeg(ap2(d_xn0T, tl * 128,
                   [[2048, 128], [128 * 2048, 16], [1, 128]]),
               ap2(xnTtmp, 0, [[2048, 128], [128, 16], [1, 128]]),
               wait=[(v_done, vv)])
        @block.tensor
        def _(t, vv=vv):
            v_done.wait_at(t, vv)
            PA.acq(t)
            for kt in range(16):
                ins = t.matmul(PA.buf[:, 0:512],
                               xnTtmp[:, kt * 128:(kt + 1) * 128],
                               w_kva[:, kt * 576:kt * 576 + 512],
                               start=(kt == 0), stop=(kt == 15))
            t_done.inc(ins)
            PB.acq(t)
            for kt in range(16):
                ins = t.matmul(PB.buf[:, 0:64],
                               xnTtmp[:, kt * 128:(kt + 1) * 128],
                               w_kva[:, kt * 576 + 512:(kt + 1) * 576],
                               start=(kt == 0), stop=(kt == 15))
            t_done.inc(ins)
        tv = t_done.n
        @block.vector
        def _(v, tv=tv, sw=st_dma.n):
            t_done.wait_at(v, tv)
            st_dma.wait_at(v, sw)
            v.tensor_copy(out=ckvb[:, 0:512], in_=PA.buf[:, 0:512])
            v.tensor_copy(out=ckvb[:, 512:576], in_=PB.buf[:, 0:64])
            v.drain()
            v_done.inc(v.nop())
        PA.rel(v_done)
        PB.rel(v_done)
        vv = v_done.n
        storeg(ap2(d_ckv_src, tl * 128 * 576, [[576, 128], [1, 576]]),
               ckvb[:, 0:576], wait=[(v_done, vv)])

    stv = st_dma.n
    @block.gpsimd
    def _(g):
        st_dma.wait_at(g, stv)
        ins = g.collective_compute(
            "AllGather", OP.bypass,
            replica_groups=[[0, 1], [2, 3], [4, 5], [6, 7]],
            ins=[d_ckv_src.ap().opt()], outs=[d_ckv.ap().opt()])
        cc_sem2.inc(ins)

    barrier()
    sbfree("w_kva", "ckvb")

    if stop == "ckv":
        dump_and_stop(d_ckv_src, 2048, 576)
        return nc, io, out

    w_qa = sba("w_qa", [128, 16 * 1536], BF16)
    qaln = sba("qaln", [128, 1536], F32)
    for kt in range(16):
        loadg(w_qa[:, kt * 1536:(kt + 1) * 1536],
              io["Wqa"].ap()[kt * 128:(kt + 1) * 128, :])
    loadq(qaln[:], io["qalnw_rep"].ap())
    barrier()

    xn_rdy = [xn_load(0, 0)]
    for tl in range(16):
        bo = (tl % 2) * 2048
        if tl + 1 < 16:
            xn_rdy.append(xn_load(tl + 1, (tl + 1) % 2))
        qv = xn_rdy[tl]
        for ch in range(3):
            P = (PA, PB, PC)[ch]
            @block.tensor
            def _(t, ch=ch, P=P, bo=bo, qv=qv):
                xn_dma.wait_at(t, qv)
                P.acq(t)
                for kt in range(16):
                    ins = t.matmul(P.buf[:, 0:512],
                                   xnTd[:, bo + kt * 128:
                                        bo + (kt + 1) * 128],
                                   w_qa[:, kt * 1536 + ch * 512:
                                        kt * 1536 + (ch + 1) * 512],
                                   start=(kt == 0), stop=(kt == 15))
                t_done.inc(ins)
            tv = t_done.n
            @block.vector
            def _(v, ch=ch, P=P, tv=tv):
                t_done.wait_at(v, tv)
                v.tensor_copy(out=xt[:, ch * 512:(ch + 1) * 512],
                              in_=P.buf[:, 0:512])
                v.drain()
                v_done.inc(v.nop())
            P.rel(v_done)
        rmsnorm_rows(xt, 1536, xnb, xsq, sml, wrep=qaln[:],
                     out_slice=xnb[:, 0:1536], x_slice=xt[:, 0:1536])
        transp([(xnb[:, j * 128:(j + 1) * 128],
                 xnTtmp[:, j * 128:(j + 1) * 128]) for j in range(8)], PTa)
        transp([(xnb[:, (8 + j) * 128:(9 + j) * 128],
                 xnTtmp[:, (8 + j) * 128:(9 + j) * 128]) for j in range(4)],
               PTb)
        vv = v_done.n
        storeg(ap2(d_qaT, tl * 128,
                   [[2048, 128], [128 * 2048, 12], [1, 128]]),
               ap2(xnTtmp, 0, [[2048, 128], [128, 12], [1, 128]]),
               wait=[(v_done, vv)])

    barrier()
    sbfree("w_qa", "qaln")

    w_qb = sba("w_qb", [128, 12 * 3072], BF16)
    cs1qb = sba("cs1qb", [128, 1024], F32)
    q3 = sba("q3", [128, 3072], F32)
    qb3 = sba("qb3", [128, 3072], BF16)
    qTn = sba("qTn", [128, 2048], BF16)
    qTp = sba("qTp", [64, 2048], BF16)
    for kt in range(12):
        loadg(w_qb[:, kt * 3072:(kt + 1) * 3072],
              io["Wqb"].ap()[kt * 128:(kt + 1) * 128, :])
    loadq(cs1qb[:], io["cs1q"].ap())
    barrier()

    def qa_load(tl, bi):
        tw_ = t_done.n
        @block.sync
        def _(q, tl=tl, bi=bi, tw_=tw_):
            t_done.wait_at(q, tw_)
            xn_dma.inc(q.dma_start(
                out=xnTd[:, bi * 2048:bi * 2048 + 1536],
                in_=ap2(d_qaT, tl * 128,
                        [[2048, 128], [128 * 2048, 12], [1, 128]])), 16)
        return xn_dma.n

    xn_rdy = [qa_load(0, 0)]
    for tl in range(16):
        bo = (tl % 2) * 2048
        if tl + 1 < 16:
            xn_rdy.append(qa_load(tl + 1, (tl + 1) % 2))
        qv = xn_rdy[tl]
        for ch in range(6):
            P = PA if ch % 2 == 0 else PB
            @block.tensor
            def _(t, ch=ch, P=P, bo=bo, qv=qv):
                xn_dma.wait_at(t, qv)
                P.acq(t)
                for j in range(12):
                    ins = t.matmul(P.buf[:, 0:512],
                                   xnTd[:, bo + j * 128:bo + (j + 1) * 128],
                                   w_qb[:, j * 3072 + ch * 512:
                                        j * 3072 + (ch + 1) * 512],
                                   start=(j == 0), stop=(j == 11))
                t_done.inc(ins)
            tv = t_done.n
            @block.vector
            def _(v, ch=ch, P=P, tv=tv):
                t_done.wait_at(v, tv)
                v.tensor_copy(out=q3[:, ch * 512:(ch + 1) * 512],
                              in_=P.buf[:, 0:512])
                v.drain()
                v_done.inc(v.nop())
            P.rel(v_done)
        rope0(q3, cs1qb, tl * 64, 16, 192, ev1, base=128)
        @block.vector
        def _(v):
            v.tensor_copy(out=qb3[:], in_=q3[:])
            v.drain()
            v_done.inc(v.nop())
        for half in range(2):
            transp([(qb3[:, (half * 8 + j) * 192:(half * 8 + j) * 192 + 128],
                     qTn[:, (half * 8 + j) * 128:(half * 8 + j + 1) * 128])
                    for j in range(8)], PTa)
            vvh = v_done.n
            @block.tensor
            def _(t, half=half, vvh=vvh):
                v_done.wait_at(t, vvh)
                PTb.acq(t)
                for j in range(8):
                    hh = half * 8 + j
                    ins = t.transpose(PTb.buf[0:64, j * 128:(j + 1) * 128],
                                      qb3[:, hh * 192 + 128:hh * 192 + 192],
                                      c_eye_b[:])
                t_done.inc(ins)
            tv = t_done.n
            @block.vector
            def _(v, half=half, tv=tv, sw=st_dma.n):
                t_done.wait_at(v, tv)
                st_dma.wait_at(v, sw)
                v.tensor_copy(out=qTp[:, half * 1024:(half + 1) * 1024],
                              in_=PTb.buf[0:64, 0:1024])
                v.drain()
                v_done.inc(v.nop())
            PTb.rel(v_done)
        vv = v_done.n
        for hh in range(16):
            storeg(ap2(d_qT, hh * 192 * 2048 + tl * 128,
                       [[2048, 128], [1, 128]]),
                   qTn[:, hh * 128:(hh + 1) * 128], wait=[(v_done, vv)])
            storeg(ap2(d_qT, hh * 192 * 2048 + 128 * 2048 + tl * 128,
                       [[2048, 64], [1, 128]]),
                   qTp[:, hh * 128:(hh + 1) * 128], wait=[(v_done, vv)])

    barrier()
    sbfree("w_qb", "cs1qb", "q3", "qb3", "qTn", "qTp")

    kpeT = sba("kpeT", [64, 4096], BF16)
    w_kvbn = sba("w_kvbn", [128, 4 * 2048], BF16)
    w_kvbv = sba("w_kvbv", [128, 4 * 2048], BF16)
    kvaln = sba("kvaln", [128, 512], F32)
    ckvnT = sba("ckvnT", [128, 4 * 512], BF16)
    csb = sba("csb", [128, 64], F32)
    for cc in range(4):
        loadg(w_kvbn[:, cc * 2048:(cc + 1) * 2048],
              io["Wkvb_n"].ap()[cc * 128:(cc + 1) * 128, :])
        loadg(w_kvbv[:, cc * 2048:(cc + 1) * 2048],
              io["Wkvb_v"].ap()[cc * 128:(cc + 1) * 128, :])
    loadq(kvaln[:], io["kvalnw_rep"].ap())
    @block.gpsimd
    def _(g):
        g.wait_ge(cc_sem2.h, 1)
        g_done.inc(g.nop())
    gv_ckv = g_done.n
    barrier()

    for g_grp in range(8):
        for gi in range(4):
            gg = g_grp * 4 + gi
            par, t_loc = gg % 2, gg // 2
            loadq(xt[:, 0:576],
                  ap2(d_ckv, par * 2048 * 576 + t_loc * 128 * 576,
                      [[576, 128], [1, 576]]),
                  wait=[(g_done, gv_ckv), (v_done, v_done.n)])
            loadq(csb[:], io["cs1k"].ap()[gg * 128:(gg + 1) * 128, :],
                  wait=[(v_done, v_done.n)])
            qv = q_dma.n
            @block.vector
            def _(v, qv=qv):
                q_dma.wait_at(v, qv)
            rope0(xt, csb, 0, 1, 0, ev1, base=512)
            @block.vector
            def _(v):
                v.tensor_copy(out=evb[:, 0:64], in_=xt[:, 512:576])
                v.drain()
                v_done.inc(v.nop())
            vv = v_done.n
            @block.tensor
            def _(t, vv=vv):
                v_done.wait_at(t, vv)
                PTa.acq(t)
                ins = t.transpose(PTa.buf[0:64, 0:128], evb[:, 0:64],
                                  c_eye_b[:])
                t_done.inc(ins)
            tv = t_done.n
            @block.vector
            def _(v, tv=tv, gg=gg):
                t_done.wait_at(v, tv)
                v.tensor_copy(out=kpeT[:, gg * 128:(gg + 1) * 128],
                              in_=PTa.buf[0:64, 0:128])
                v.drain()
                v_done.inc(v.nop())
            PTa.rel(v_done)
            rmsnorm_rows(xt, 512, xnb, xsq, sml, wrep=kvaln[:],
                         out_slice=xnb[:, 0:512], x_slice=xt[:, 0:512])
            transp([(xnb[:, c * 128:(c + 1) * 128],
                     ckvnT[:, c * 512 + gi * 128:c * 512 + gi * 128 + 128])
                    for c in range(4)], PTb)
        vwk = v_done.n
        for hh in range(16):
            @block.tensor
            def _(t, hh=hh, vwk=vwk):
                v_done.wait_at(t, vwk)
                PA.acq(t)
                for cc in range(4):
                    ins = t.matmul(PA.buf[:, 0:512],
                                   w_kvbn[:, cc * 2048 + hh * 128:
                                          cc * 2048 + (hh + 1) * 128],
                                   ckvnT[:, cc * 512:(cc + 1) * 512],
                                   start=(cc == 0), stop=(cc == 3))
                t_done.inc(ins)
            tv = t_done.n
            slot = hh % 2
            @block.scalar
            def _(s, tv=tv, slot=slot, sw=st_dma.n):
                t_done.wait_at(s, tv)
                st_dma.wait_at(s, sw)
                s.activation(out=xnb[:, slot * 512:(slot + 1) * 512],
                             in_=PA.buf[:, 0:512], func=AF.Copy)
                s.drain()
                s_done.inc(s.nop())
            PA.rel(s_done)
            sv = s_done.n
            storeg(ap2(d_kT, hh * 128 * 4096 + g_grp * 512,
                       [[4096, 128], [1, 512]]),
                   xnb[:, slot * 512:(slot + 1) * 512],
                   wait=[(s_done, sv)])
        for gi in range(4):
            for vg in range(4):
                @block.tensor
                def _(t, gi=gi, vg=vg, vwk=vwk):
                    v_done.wait_at(t, vwk)
                    PB.acq(t)
                    for cc in range(4):
                        ins = t.matmul(
                            PB.buf[:, 0:512],
                            ckvnT[:, cc * 512 + gi * 128:
                                  cc * 512 + gi * 128 + 128],
                            w_kvbv[:, cc * 2048 + vg * 512:
                                   cc * 2048 + (vg + 1) * 512],
                            start=(cc == 0), stop=(cc == 3))
                    t_done.inc(ins)
                tv = t_done.n
                slot = vg % 2
                @block.scalar
                def _(s, tv=tv, slot=slot, sw=st_dma.n):
                    t_done.wait_at(s, tv)
                    st_dma.wait_at(s, sw)
                    s.activation(out=xnb[:, 1024 + slot * 512:
                                         1024 + (slot + 1) * 512],
                                 in_=PB.buf[:, 0:512], func=AF.Copy)
                    s.drain()
                    s_done.inc(s.nop())
                PB.rel(s_done)
                sv = s_done.n
                gg = g_grp * 4 + gi
                for j in range(4):
                    hh = vg * 4 + j
                    storeg(ap2(d_V, hh * 4096 * 128 + gg * 128 * 128,
                               [[128, 128], [1, 128]]),
                           xnb[:, 1024 + slot * 512 + j * 128:
                               1024 + slot * 512 + (j + 1) * 128],
                           wait=[(s_done, sv)])

    barrier()
    sbfree("w_kvbn", "w_kvbv", "kvaln", "ckvnT", "csb")

    if stop == "kv":
        dump_and_stop(d_kT, 2048, 2048)
        return nc, io, out

    l1ma = sba("l1ma", [128, 512], F32)
    l1mb = sba("l1mb", [128, 512], F32)
    KTg = sba("KTg", [128, 4 * 4096], BF16)
    Vg = sba("Vg", [128, 4 * 4096], BF16)
    QTn10 = sba("QTn10", [128, 4 * 2048], BF16)
    QTp10 = sba("QTp10", [64, 4 * 2048], BF16)
    scrs = sba("scrs", [128, 4096], F32)
    pexp = sba("pexp", [128, 4096], BF16)
    pts = sba("pts", [128, 256], BF16)
    loadq(l1ma[:], io["l1mA"].ap())
    loadq(l1mb[:], io["l1mB"].ap())
    barrier()

    for hg in range(4):
        for hl in range(4):
            hh = hg * 4 + hl
            w8 = [(t_done, t_done.n)]
            loadq(KTg[:, hl * 4096:(hl + 1) * 4096],
                  ap2(d_kT, hh * 128 * 4096, [[4096, 128], [1, 4096]]),
                  wait=w8)
            loadq(Vg[:, hl * 4096:(hl + 1) * 4096],
                  ap2(d_V, hh * 4096 * 128,
                      [[128, 128], [16384, 32], [1, 128]]))
            loadq(QTn10[:, hl * 2048:(hl + 1) * 2048],
                  ap2(d_qT, hh * 192 * 2048, [[2048, 128], [1, 2048]]))
            loadq(QTp10[:, hl * 2048:(hl + 1) * 2048],
                  ap2(d_qT, hh * 192 * 2048 + 128 * 2048,
                      [[2048, 64], [1, 2048]]))
        qv_grp = q_dma.n
        for tl in range(16):
            nch = (2 * tl + 1) // 4 + 1
            NK = nch * 512
            for hl in range(4):
                hh = hg * 4 + hl
                for ch in range(nch):
                    P = PA if ch % 2 == 0 else PB
                    @block.tensor
                    def _(t, P=P, ch=ch, hl=hl, tl=tl, qv_grp=qv_grp):
                        q_dma.wait_at(t, qv_grp)
                        P.acq(t)
                        ins = t.matmul(P.buf[:, 0:512],
                                       QTn10[:, hl * 2048 + tl * 128:
                                             hl * 2048 + (tl + 1) * 128],
                                       KTg[:, hl * 4096 + ch * 512:
                                           hl * 4096 + (ch + 1) * 512],
                                       start=True, stop=False)
                        ins = t.matmul(P.buf[:, 0:512],
                                       QTp10[:, hl * 2048 + tl * 128:
                                             hl * 2048 + (tl + 1) * 128],
                                       kpeT[:, ch * 512:(ch + 1) * 512],
                                       start=False, stop=True)
                        t_done.inc(ins)
                    tv = t_done.n
                    if ch < nch - 1:
                        @block.scalar
                        def _(s, P=P, ch=ch, tv=tv):
                            t_done.wait_at(s, tv)
                            s.activation(out=scrs[:, ch * 512:(ch + 1) * 512],
                                         in_=P.buf[:, 0:512], func=AF.Copy,
                                         scale=SC1)
                            s.drain()
                            s_done.inc(s.nop())
                        P.rel(s_done)
                    else:
                        msk = l1ma if tl % 2 == 0 else l1mb
                        @block.vector
                        def _(v, P=P, ch=ch, tv=tv, msk=msk, sv=s_done.n):
                            t_done.wait_at(v, tv)
                            s_done.wait_at(v, sv)
                            v.scalar_tensor_tensor(
                                out=scrs[:, ch * 512:(ch + 1) * 512],
                                in0=P.buf[:, 0:512], scalar=SC1, in1=msk[:],
                                op0=OP.mult, op1=OP.add)
                            v.drain()
                            v_done.inc(v.nop())
                        P.rel(v_done)
                @block.vector
                def _(v, NK=NK, sv=s_done.n):
                    s_done.wait_at(v, sv)
                    v.reduce_max(sml[:, 0:1], scrs[:, 0:NK], axis=AX.X,
                                 negate=True)
                    v.drain()
                    v_done.inc(v.nop())
                vv = v_done.n
                @block.scalar
                def _(s, NK=NK, vv=vv, tw=t_done.n):
                    v_done.wait_at(s, vv)
                    t_done.wait_at(s, tw)
                    s.activation(out=pexp[:, 0:NK], in_=scrs[:, 0:NK],
                                 func=AF.Exp, bias=sml[:, 0:1],
                                 accum_out=sml[:, 1:2])
                    s.drain()
                    s_done.inc(s.nop())
                sv_exp = s_done.n
                @block.vector
                def _(v, sv_exp=sv_exp):
                    s_done.wait_at(v, sv_exp)
                    v.reciprocal(sml[:, 2:3], sml[:, 1:2])
                    v.drain()
                    v_done.inc(v.nop())
                vv_rcp = v_done.n
                nkt = 4 * nch
                pv_t = []
                for kt in range(nkt):
                    PT = PTa if kt % 2 == 0 else PTb
                    @block.tensor
                    def _(t, PT=PT, kt=kt, sv_exp=sv_exp):
                        if kt == 0:
                            s_done.wait_at(t, sv_exp)
                        PT.acq(t)
                        ins = t.transpose(PT.buf[:, 0:128],
                                          pexp[:, kt * 128:(kt + 1) * 128],
                                          c_eye_b[:])
                        t_done.inc(ins)
                    tv_tr = t_done.n
                    @block.scalar
                    def _(s, kt=kt, PT=PT, tv_tr=tv_tr,
                          pvw=(pv_t[kt - 2] if kt >= 2 else 0)):
                        t_done.wait_at(s, tv_tr)
                        if pvw:
                            t_done.wait_at(s, pvw)
                        s.activation(out=pts[:, (kt % 2) * 128:
                                             (kt % 2 + 1) * 128],
                                     in_=PT.buf[:, 0:128], func=AF.Copy)
                        s.drain()
                        s_done.inc(s.nop())
                    PT.rel(s_done)
                    sv_c = s_done.n
                    @block.tensor
                    def _(t, kt=kt, hl=hl, sv_c=sv_c, nkt=nkt):
                        s_done.wait_at(t, sv_c)
                        if kt == 0:
                            PC.acq(t)
                        ins = t.matmul(PC.buf[0:128, 0:128],
                                       pts[:, (kt % 2) * 128:
                                           (kt % 2 + 1) * 128],
                                       Vg[:, hl * 4096 + kt * 128:
                                          hl * 4096 + (kt + 1) * 128],
                                       start=(kt == 0), stop=(kt == nkt - 1))
                        t_done.inc(ins)
                    pv_t.append(t_done.n)
                @block.vector
                def _(v, tv=pv_t[-1], vv_rcp=vv_rcp):
                    t_done.wait_at(v, tv)
                    v.tensor_scalar(out=evb[:, 0:128],
                                    in0=PC.buf[0:128, 0:128],
                                    scalar1=sml[:, 2:3], scalar2=None,
                                    op0=OP.mult)
                    v.drain()
                    v_done.inc(v.nop())
                PC.rel(v_done)
                vv = v_done.n
                @block.tensor
                def _(t, vv=vv):
                    v_done.wait_at(t, vv)
                    PTa.acq(t)
                    ins = t.transpose(PTa.buf[:, 128:256], evb[:, 0:128],
                                      c_eye_b[:])
                    t_done.inc(ins)
                tv = t_done.n
                @block.vector
                def _(v, tv=tv, sw=st_dma.n):
                    t_done.wait_at(v, tv)
                    st_dma.wait_at(v, sw)
                    v.tensor_copy(out=evb[:, 128:256],
                                  in_=PTa.buf[:, 128:256])
                    v.drain()
                    v_done.inc(v.nop())
                PTa.rel(v_done)
                vv = v_done.n
                storeg(ap2(d_oT, hh * 128 * 2048 + tl * 128,
                           [[2048, 128], [1, 128]]),
                       evb[:, 128:256], wait=[(v_done, vv)])

    barrier()
    sbfree("l1ma", "l1mb", "KTg", "Vg", "QTn10", "QTp10", "scrs", "pexp",
           "pts")
    sbfree("kpeT")

    w_o = sba("w_o", [128, 16 * 2048], BF16)
    oTt = sba("oTt", [128, 2048], BF16)
    for kt in range(16):
        loadg(w_o[:, kt * 2048:(kt + 1) * 2048],
              io["Wo"].ap()[kt * 128:(kt + 1) * 128, :])
    barrier()

    for tl in range(16):
        for hh in range(16):
            loadq(oTt[:, hh * 128:(hh + 1) * 128],
                  ap2(d_oT, hh * 128 * 2048 + tl * 128,
                      [[2048, 128], [1, 128]]),
                  wait=([(t_done, t_done.n)] if hh == 0 else ()))
        qv = q_dma.n
        for oc in range(4):
            P = PA if oc % 2 == 0 else PB
            @block.tensor
            def _(t, P=P, oc=oc, qv=qv):
                q_dma.wait_at(t, qv)
                P.acq(t)
                for hh in range(16):
                    ins = t.matmul(P.buf[:, 0:512],
                                   oTt[:, hh * 128:(hh + 1) * 128],
                                   w_o[:, hh * 2048 + oc * 512:
                                       hh * 2048 + (oc + 1) * 512],
                                   start=(hh == 0), stop=(hh == 15))
                t_done.inc(ins)
            tv = t_done.n
            loadq(ev1[:, 0:512],
                  ap2(d_x1, tl * 128 * 2048 + oc * 512,
                      [[2048, 128], [1, 512]]),
                  wait=[(v_done, v_done.n)])
            qv2 = q_dma.n
            @block.vector
            def _(v, P=P, tv=tv, qv2=qv2, sw=st_dma.n):
                t_done.wait_at(v, tv)
                q_dma.wait_at(v, qv2)
                st_dma.wait_at(v, sw)
                v.tensor_add(ev2[:, 0:512], P.buf[:, 0:512], ev1[:, 0:512])
                v.drain()
                v_done.inc(v.nop())
            P.rel(v_done)
            vv = v_done.n
            storeg(ap2(d_x1, tl * 128 * 2048 + oc * 512,
                       [[2048, 128], [1, 512]]),
                   ev2[:, 0:512], wait=[(v_done, vv)])

    barrier()
    sbfree("w_o", "oTt")
    sbfree("xnTd")

    if stop == "xattn":
        dump_and_stop(d_x1, 2048, 2048)
        return nc, io, out

    SQ2048 = float(np.sqrt(2048.0))
    FLR2 = float(1e-5 * np.sqrt(4096.0) * 127.0 * 127.0)
    KAL = float(1.0 / (64.0 * 127.0 ** 3))
    mstat = sba("mstat", [128, 64], F32)
    y1 = sba("y1", [128, 8192], F32)
    y2 = sba("y2", [128, 8192], F32)
    qi32 = sba("qi32", [128, 8192], I32)
    sbc = sba("sbc", [128, 256], F32)
    xng = sba("xng", [128, 4096], BF16)
    xqg = sba("xqg", [128, 8192], BF16)

    def butterflies(nc_blocks, stages, width):
        @block.vector
        def _(v, sv=s_done.n):
            s_done.wait_at(v, sv)
            cur, oth = y1, y2
            for hstep in stages:
                for b0 in range(0, nc_blocks, 2 * hstep):
                    for k in range(hstep):
                        cA, cB = b0 + k, b0 + k + hstep
                        v.tensor_add(oth[:, cA * width:(cA + 1) * width],
                                     cur[:, cA * width:(cA + 1) * width],
                                     cur[:, cB * width:(cB + 1) * width])
                        v.tensor_sub(oth[:, cB * width:(cB + 1) * width],
                                     cur[:, cA * width:(cA + 1) * width],
                                     cur[:, cB * width:(cB + 1) * width])
                v.drain()
                cur, oth = oth, cur
            v_done.inc(v.nop())
        return y1 if len(stages) % 2 == 0 else y2

    def quant_group(g, nblk, ybuf, scr, flr_mode, width=256):
        @block.vector
        def _(v, g=g, nblk=nblk):
            half = nblk // 2

            def tree(op):
                for k in range(half):
                    v.tensor_tensor(out=scr[:, k * width:(k + 1) * width],
                                    in0=ybuf[:, k * width:(k + 1) * width],
                                    in1=ybuf[:, (k + half) * width:
                                             (k + half + 1) * width],
                                    op=op)
                v.drain()
                base, n = 0, half
                while n > 1:
                    for k in range(n // 2):
                        v.tensor_tensor(
                            out=scr[:, (base + n + k) * width:
                                    (base + n + k + 1) * width],
                            in0=scr[:, (base + k) * width:
                                    (base + k + 1) * width],
                            in1=scr[:, (base + k + n // 2) * width:
                                    (base + k + n // 2 + 1) * width],
                            op=op)
                    v.drain()
                    base += n
                    n //= 2
                return base
            v.drain()
            maxb = tree(OP.max)
            v.tensor_copy(out=ev2[:, 0:width],
                          in_=scr[:, maxb * width:(maxb + 1) * width])
            v.drain()
            minb = tree(OP.min)
            v.tensor_scalar(out=ev1[:, 0:width],
                            in0=scr[:, minb * width:(minb + 1) * width],
                            scalar1=-1.0, scalar2=None, op0=OP.mult)
            v.drain()
            v.tensor_tensor(out=ev2[:, 0:width], in0=ev1[:, 0:width],
                            in1=ev2[:, 0:width], op=OP.max)
            v.drain()
            v_done.inc(v.nop())
        vv = v_done.n
        @block.tensor
        def _(t, vv=vv):
            v_done.wait_at(t, vv)
            PC.acq(t)
            ins = t.transpose(PC.buf[:, 0:128], ev2[:, 0:128], c_eye_f[:])
            ins = t.transpose(PC.buf[:, 128:256], ev2[:, 128:256],
                              c_eye_f[:])
            t_done.inc(ins)
        tv = t_done.n
        @block.vector
        def _(v, g=g, tv=tv, flr_mode=flr_mode):
            t_done.wait_at(v, tv)
            for b in range(2):
                v.reduce_max(sml[:, 8 + b:9 + b],
                             PC.buf[:, b * 128:(b + 1) * 128], axis=AX.X)
            v.drain()
            for b in range(2):
                tile = 2 * g + b
                if flr_mode == 1:
                    v.tensor_scalar(out=mstat[:, tile:tile + 1],
                                    in0=sml[:, 8 + b:9 + b],
                                    scalar1=1.0 / SQ2048, scalar2=1e-5,
                                    op0=OP.mult, op1=OP.max)
                    v.drain()
                    v.reciprocal(sml[:, 10 + b:11 + b],
                                 mstat[:, tile:tile + 1])
                    v.drain()
                    v.tensor_scalar(out=sml[:, 12 + b:13 + b],
                                    in0=sml[:, 10 + b:11 + b],
                                    scalar1=127.0 / SQ2048, scalar2=None,
                                    op0=OP.mult)
                else:
                    v.tensor_scalar(out=sml[:, 10 + b:11 + b],
                                    in0=mstat[:, tile:tile + 1],
                                    scalar1=c_scales[:, 0:1], scalar2=None,
                                    op0=OP.mult)
                    v.drain()
                    v.tensor_mul(sml[:, 12 + b:13 + b],
                                 sml[:, 10 + b:11 + b],
                                 sml[:, 10 + b:11 + b])
                    v.drain()
                    v.reciprocal(sml[:, 10 + b:11 + b],
                                 sml[:, 12 + b:13 + b])
                    v.drain()
                    v.tensor_scalar(out=sml[:, 12 + b:13 + b],
                                    in0=sml[:, 10 + b:11 + b],
                                    scalar1=FLR2, scalar2=None, op0=OP.mult)
                    v.drain()
                    v.tensor_tensor(out=mstat[:, 16 + tile:17 + tile],
                                    in0=sml[:, 8 + b:9 + b],
                                    in1=sml[:, 12 + b:13 + b], op=OP.max)
                    v.drain()
                    v.reciprocal(sml[:, 10 + b:11 + b],
                                 mstat[:, 16 + tile:17 + tile])
                    v.drain()
                    v.tensor_scalar(out=sml[:, 12 + b:13 + b],
                                    in0=sml[:, 10 + b:11 + b],
                                    scalar1=127.0, scalar2=None, op0=OP.mult)
            v.drain()
            v_done.inc(v.nop())
        PC.rel(v_done)
        vv = v_done.n
        @block.tensor
        def _(t, vv=vv):
            v_done.wait_at(t, vv)
            PD.acq(t)
            ins = t.transpose(PD.buf[0:1, 0:128], sml[:, 12:13], c_eye_f[:])
            ins = t.transpose(PD.buf[0:1, 128:256], sml[:, 13:14],
                              c_eye_f[:])
            t_done.inc(ins)
        tv = t_done.n
        @block.vector
        def _(v, tv=tv):
            t_done.wait_at(v, tv)
            v.tensor_copy(out=ev1[0:1, 0:256], in_=PD.buf[0:1, 0:256])
            v.drain()
            v_done.inc(v.nop())
        PD.rel(v_done)
        vv = v_done.n
        @block.tensor
        def _(t, vv=vv):
            v_done.wait_at(t, vv)
            PE2.acq(t)
            ins = t.matmul(PE2.buf[:, 0:256], c_ones[0:1, 0:128],
                           ev1[0:1, 0:256], start=True, stop=True)
            t_done.inc(ins)
        tv = t_done.n
        @block.vector
        def _(v, tv=tv):
            t_done.wait_at(v, tv)
            v.tensor_copy(out=sbc[:], in_=PE2.buf[:, 0:256])
            v.drain()
            v_done.inc(v.nop())
        PE2.rel(v_done)

    def rne_to_bf16(nblk, ybuf, width=256):
        @block.vector
        def _(v, nblk=nblk, sw=st_dma.n):
            st_dma.wait_at(v, sw)
            for c in range(nblk):
                v.tensor_tensor(out=qi32[:, c * width:(c + 1) * width],
                                in0=ybuf[:, c * width:(c + 1) * width],
                                in1=sbc[:], op=OP.mult)
            v.drain()
            v.tensor_copy(out=y2[:, 0:nblk * width],
                          in_=qi32[:, 0:nblk * width])
            v.drain()
            v.tensor_copy(out=xqg[:, 0:nblk * width],
                          in_=y2[:, 0:nblk * width])
            v.drain()
            v_done.inc(v.nop())
        return v_done.n

    for g in range(8):
        loadq(xng[:], ap2(d_xn0T, g * 256,
                          [[2048, 128], [128 * 2048, 16], [1, 256]]),
              wait=[(t_done, t_done.n)])
        qv = q_dma.n
        for c in range(16):
            P = PA if c % 2 == 0 else PB
            @block.tensor
            def _(t, c=c, P=P, qv=qv):
                if c == 0:
                    q_dma.wait_at(t, qv)
                P.acq(t)
                ins = t.matmul(P.buf[:, 0:256], c_had_b[:],
                               xng[:, c * 256:(c + 1) * 256],
                               start=True, stop=True)
                t_done.inc(ins)
            tv = t_done.n
            @block.scalar
            def _(s, c=c, P=P, tv=tv, vw=v_done.n):
                t_done.wait_at(s, tv)
                if c == 0:
                    v_done.wait_at(s, vw)
                s.activation(out=y1[:, c * 256:(c + 1) * 256],
                             in_=P.buf[:, 0:256], func=AF.Copy)
                s.drain()
                s_done.inc(s.nop())
            P.rel(s_done)
        fin = butterflies(16, [1, 2, 4, 8], 256)
        assert fin is y1
        quant_group(g, 16, y1, y2, flr_mode=1)
        vv = rne_to_bf16(16, y1)
        storeg(ap2(d_xq1, g * 256,
                   [[2048, 128], [128 * 2048, 16], [1, 256]]),
               ap2(xqg, 0, [[8192, 128], [256, 16], [1, 256]]),
               wait=[(v_done, vv)])

    barrier()

    if stop == "xq1":
        dump_and_stop2(d_xq1, [[1, 2048 * 2048]])
        return nc, io, out

    wupb = sba("wupb", [128, 2 * 8192], BF16)
    xqs = sba("xqs", [128, 16 * 512], BF16)
    h2b = sba("h2b", [128, 1024], BF16)
    tdh = []
    for tg in range(4):
        loadq(xqs[:], ap2(d_xq1, tg * 512,
                          [[2048, 128], [128 * 2048, 16], [1, 512]]),
              wait=[(t_done, t_done.n)])
        qv_x = q_dma.n
        for iblk in range(8):
            dslot = iblk % 2
            wv = ([(t_done, tdh[-2])] if len(tdh) >= 2
                  else [(t_done, t_done.n)])
            for kt in range(16):
                s8, r2 = kt // 2, kt % 2
                loadq(wupb[:, dslot * 8192 + kt * 512:
                           dslot * 8192 + (kt + 1) * 512],
                      pack_g(s8, OFF_TU, r2 * 128, 128, iblk * 512, 512,
                             4096),
                      wait=(wv if kt == 0 else ()))
            qv = q_dma.n
            for isub in range(4):
                P = PA if isub % 2 == 0 else PB
                @block.tensor
                def _(t, P=P, isub=isub, dslot=dslot, qv=qv, qv_x=qv_x):
                    q_dma.wait_at(t, qv)
                    q_dma.wait_at(t, qv_x)
                    P.acq(t)
                    for kt in range(16):
                        ins = t.matmul(
                            P.buf[:, 0:512],
                            wupb[:, dslot * 8192 + kt * 512 + isub * 128:
                                 dslot * 8192 + kt * 512 + (isub + 1) * 128],
                            xqs[:, kt * 512:(kt + 1) * 512],
                            start=(kt == 0), stop=(kt == 15))
                    t_done.inc(ins)
                tv = t_done.n
                @block.scalar
                def _(s, P=P, tv=tv, vw=v_done.n):
                    t_done.wait_at(s, tv)
                    v_done.wait_at(s, vw)
                    s.activation(out=ev1[:, 0:512], in_=P.buf[:, 0:512],
                                 func=AF.Relu)
                    s.drain()
                    s_done.inc(s.nop())
                P.rel(s_done)
                sv = s_done.n
                slot = isub % 2
                @block.vector
                def _(v, sv=sv, slot=slot, sw=st_dma.n):
                    s_done.wait_at(v, sv)
                    st_dma.wait_at(v, sw)
                    v.tensor_mul(h2b[:, slot * 512:(slot + 1) * 512],
                                 ev1[:, 0:512], ev1[:, 0:512])
                    v.drain()
                    v_done.inc(v.nop())
                vv = v_done.n
                ic = iblk * 4 + isub
                storeg(ap2(d_h2T, ic * 128 * 2048 + tg * 512,
                           [[2048, 128], [1, 512]]),
                       h2b[:, slot * 512:(slot + 1) * 512],
                       wait=[(v_done, vv)])
            tdh.append(t_done.n)

    barrier()
    sbfree("wupb", "xqs", "h2b")

    if stop in ("h2a", "h2b"):
        off = 0 if stop == "h2a" else 2048 * 2048
        @block.gpsimd
        def _(g, off=off):
            st_dma.inc(g.dma_start(out=ap2(out, 0, [[1, 2048 * 2048]]),
                                   in_=ap2(d_h2T, off, [[1, 2048 * 2048]])),
                       16)
        barrier()
        return nc, io, out

    h2g = sba("h2g", [128, 8192], BF16)
    for g in range(8):
        loadq(h2g[:], ap2(d_h2T, g * 256,
                          [[2048, 128], [128 * 2048, 32], [1, 256]]),
              wait=[(t_done, t_done.n)])
        qv = q_dma.n
        for c in range(32):
            P = PA if c % 2 == 0 else PB
            @block.tensor
            def _(t, c=c, P=P, qv=qv):
                if c == 0:
                    q_dma.wait_at(t, qv)
                P.acq(t)
                ins = t.matmul(P.buf[:, 0:256], c_had_b[:],
                               h2g[:, c * 256:(c + 1) * 256],
                               start=True, stop=True)
                t_done.inc(ins)
            tv = t_done.n
            @block.scalar
            def _(s, c=c, P=P, tv=tv, vw=v_done.n):
                t_done.wait_at(s, tv)
                if c == 0:
                    v_done.wait_at(s, vw)
                s.activation(out=y1[:, c * 256:(c + 1) * 256],
                             in_=P.buf[:, 0:256], func=AF.Copy)
                s.drain()
                s_done.inc(s.nop())
            P.rel(s_done)
        fin = butterflies(32, [1, 2, 4, 8, 16], 256)
        assert fin is y2
        quant_group(g, 32, y2, y1, flr_mode=2)
        vv = rne_to_bf16(32, y2)
        storeg(ap2(d_xq2, g * 256,
                   [[2048, 128], [128 * 2048, 32], [1, 256]]),
               ap2(xqg, 0, [[8192, 128], [256, 32], [1, 256]]),
               wait=[(v_done, vv)])

    barrier()
    sbfree("y1", "y2", "qi32", "sbc", "xng", "xqg", "h2g")

    if stop in ("xq2a", "xq2b"):
        off = 0 if stop == "xq2a" else 2048 * 2048
        @block.gpsimd
        def _(g, off=off):
            st_dma.inc(g.dma_start(out=ap2(out, 0, [[1, 2048 * 2048]]),
                                   in_=ap2(d_xq2, off, [[1, 2048 * 2048]])),
                       16)
        barrier()
        return nc, io, out

    wdnb = sba("wdnb", [128, 32 * 512], BF16)
    xq2t = sba("xq2t", [128, 32 * 128], BF16)
    @block.vector
    def _(v):
        v.tensor_scalar(out=mstat[:, 32:48], in0=mstat[:, 0:16],
                        scalar1=c_scales[:, 0:1], scalar2=None, op0=OP.mult)
        v.drain()
        v.tensor_mul(mstat[:, 48:64], mstat[:, 32:48], mstat[:, 32:48])
        v.drain()
        v.tensor_mul(mstat[:, 32:48], mstat[:, 48:64], mstat[:, 16:32])
        v.drain()
        v.tensor_scalar(out=mstat[:, 48:64], in0=mstat[:, 32:48],
                        scalar1=c_scales[:, 1:2], scalar2=KAL,
                        op0=OP.mult, op1=OP.mult)
        v.drain()
        v_done.inc(v.nop())

    for oc in range(4):
        for rt in range(32):
            loadq(wdnb[:, rt * 512:(rt + 1) * 512],
                  pack_g(rt // 4, OFF_TD, (rt % 4) * 128, 128, oc * 512,
                         512, 2048),
                  wait=([(t_done, t_done.n)] if rt == 0 else ()))
        for tl in range(16):
            loadq(xq2t[:], ap2(d_xq2, tl * 128,
                               [[2048, 128], [128 * 2048, 32], [1, 128]]),
                  wait=[(t_done, t_done.n)])
            qv = q_dma.n
            P = PA if tl % 2 == 0 else PB
            @block.tensor
            def _(t, P=P, qv=qv):
                q_dma.wait_at(t, qv)
                P.acq(t)
                for rt in range(32):
                    ins = t.matmul(P.buf[:, 0:512],
                                   xq2t[:, rt * 128:(rt + 1) * 128],
                                   wdnb[:, rt * 512:(rt + 1) * 512],
                                   start=(rt == 0), stop=(rt == 31))
                t_done.inc(ins)
            tv = t_done.n
            loadq(ev1[:, 0:512],
                  ap2(d_x1, tl * 128 * 2048 + oc * 512,
                      [[2048, 128], [1, 512]]),
                  wait=[(v_done, v_done.n)])
            qv2 = q_dma.n
            @block.vector
            def _(v, P=P, tl=tl, tv=tv, qv2=qv2, sw=st_dma.n):
                t_done.wait_at(v, tv)
                q_dma.wait_at(v, qv2)
                st_dma.wait_at(v, sw)
                v.scalar_tensor_tensor(out=ev2[:, 0:512],
                                       in0=P.buf[:, 0:512],
                                       scalar=mstat[:, 48 + tl:49 + tl],
                                       in1=ev1[:, 0:512],
                                       op0=OP.mult, op1=OP.add)
                v.drain()
                v_done.inc(v.nop())
            P.rel(v_done)
            vv = v_done.n
            storeg(out.ap()[tl * 128:(tl + 1) * 128,
                            oc * 512:(oc + 1) * 512],
                   ev2[:, 0:512], wait=[(v_done, vv)])

    barrier()
    return nc, io, out


def _host_consts():
    eye = np.eye(128, dtype=np.float32)
    had = np.empty((128, 128), np.float32)
    for i in range(128):
        for j in range(128):
            had[i, j] = 1.0 if bin(i & j).count("1") % 2 == 0 else -1.0
    inv = (1.0 / (10000.0 ** (np.arange(0, 64, 2, dtype=np.float32)
                              / np.float32(64.0)))).astype(np.float32)
    pos = np.arange(4096, dtype=np.float32)
    ang = pos[:, None] * inv[None, :]
    cosf = np.cos(ang).astype(np.float32)
    sinf = np.sin(ang).astype(np.float32)
    cs1k = np.concatenate([cosf, sinf], axis=1)
    return eye, had, cosf, sinf, cs1k


def _core_tables(h, cosf, sinf):
    NEGM = np.float32(-1.0e30)
    p = np.arange(128)
    cs_own = np.zeros((128, 1024), np.float32)
    cs_prv = np.zeros((128, 1024), np.float32)
    for t in range(16):
        g = 2 * t + h
        posn = g * 128 + p
        cs_own[:, t * 64:t * 64 + 32] = cosf[posn]
        cs_own[:, t * 64 + 32:t * 64 + 64] = sinf[posn]
        if g - 1 >= 0:
            posp = (g - 1) * 128 + p
            cs_prv[:, t * 64:t * 64 + 32] = cosf[posp]
            cs_prv[:, t * 64 + 32:t * 64 + 64] = sinf[posp]
    spmask = np.zeros((128, 4096), np.float32)
    spflag = np.zeros((128, 16), np.float32)
    blk_end = np.arange(256) * 16 + 15
    for t in range(16):
        posn = (2 * t + h) * 128 + p
        valid = blk_end[None, :] < posn[:, None]
        spmask[:, t * 256:(t + 1) * 256] = np.where(valid, 0.0, NEGM)
        spflag[:, t] = (posn >= 16).astype(np.float32)
    j = np.arange(384)
    base = (p[:, None] + 1 <= j[None, :]) & (j[None, :] <= p[:, None] + 256)
    wmN = np.where(base, 0.0, NEGM).astype(np.float32)
    wm0 = np.where(base & (j[None, :] >= 256 - 128 * h), 0.0,
                   NEGM).astype(np.float32)
    def l1mask(d0):
        m = np.zeros((128, 512), np.float32)
        for st in range(4):
            if st < d0:
                continue
            elif st == d0:
                cc = np.arange(128)
                m[:, st * 128:(st + 1) * 128] = np.where(
                    cc[None, :] <= p[:, None], 0.0, NEGM)
            else:
                m[:, st * 128:(st + 1) * 128] = NEGM
        return m
    return (cs_own, cs_prv, spmask, spflag, wm0, wmN,
            l1mask(h), l1mask(h + 2))


def kernel(**inputs):
    fi = {k: np.asarray(v) for k, v in inputs.items()}
    eye, had, cosf, sinf, cs1k = _host_consts()
    f32 = np.float32
    x = fi["L0_attn_h"].astype(f32, copy=False)
    lut = fi["L0_ffn_lut"].astype(f32, copy=False)
    lutcols = np.broadcast_to(lut[None, :], (128, 16)).copy()
    qnw = np.broadcast_to(np.tile(fi["L0_attn_q_norm_w"].astype(f32), 16)
                          [None, :], (128, 2048)).copy()
    knw_rep = np.broadcast_to(fi["L0_attn_k_norm_w"].astype(f32)[None, :],
                              (128, 128)).copy()
    knw_col = fi["L0_attn_k_norm_w"].astype(f32).reshape(128, 1).copy()
    qalnw = np.broadcast_to(fi["L1_attn_q_a_ln_w"].astype(f32)[None, :],
                            (128, 1536)).copy()
    kvalnw = np.broadcast_to(fi["L1_attn_kv_a_ln_w"].astype(f32)[None, :],
                             (128, 512)).copy()
    kvcb = np.tile(fi["L0_attn_kvc_bias"].astype(f32).T, (1, 8)).copy()
    idxb = np.tile(fi["L0_attn_idx_c_bias"].astype(f32).T, (1, 8)).copy()
    gidxT = np.ascontiguousarray(fi["L0_ffn_gate_idx"].astype(np.int32).T)
    uidxT = np.ascontiguousarray(fi["L0_ffn_up_idx"].astype(np.int32).T)
    didxT_full = np.ascontiguousarray(fi["L0_ffn_down_idx"]
                                      .astype(np.int32).T)
    wupT_full = np.ascontiguousarray(fi["L1_ffn_w_up"].astype(f32).T)
    wdnT_full = np.ascontiguousarray(fi["L1_ffn_w_down"].astype(f32).T)
    Wkvb = fi["L1_attn_Wkvb"].astype(f32).reshape(512, 16, 256)
    Wkvb_n = np.ascontiguousarray(Wkvb[:, :, :128].reshape(512, 2048))
    Wkvb_v = np.ascontiguousarray(Wkvb[:, :, 128:].reshape(512, 2048))
    group_w = np.ascontiguousarray(
        fi["L0_attn_group_w"].astype(f32).reshape(2048, 512))

    common = {
        "w_qc": fi["L0_attn_w_qc"].astype(f32, copy=False),
        "w_qup": fi["L0_attn_w_qup"].astype(f32, copy=False),
        "kvc_w": fi["L0_attn_kvc_w"].astype(f32, copy=False),
        "kvc_wz": fi["L0_attn_kvc_wz"].astype(f32, copy=False),
        "k_proj_w": fi["L0_attn_k_proj_w"].astype(f32, copy=False),
        "v_proj_w": fi["L0_attn_v_proj_w"].astype(f32, copy=False),
        "idx_c_w": fi["L0_attn_idx_c_w"].astype(f32, copy=False),
        "idx_c_wz": fi["L0_attn_idx_c_wz"].astype(f32, copy=False),
        "w_dqT": np.ascontiguousarray(fi["L0_attn_w_dq"].astype(f32).T),
        "w_iuq": fi["L0_attn_w_iuq"].astype(f32, copy=False),
        "w_w": fi["L0_attn_w_w"].astype(f32, copy=False),
        "group_w": group_w,
        "final_w": fi["L0_attn_final_w"].astype(f32, copy=False),
        "Wqa": fi["L1_attn_Wqa"].astype(f32, copy=False),
        "Wqb": fi["L1_attn_Wqb"].astype(f32, copy=False),
        "Wkva": fi["L1_attn_Wkva"].astype(f32, copy=False),
        "Wkvb_n": Wkvb_n,
        "Wkvb_v": Wkvb_v,
        "Wo": fi["L1_attn_Wo"].astype(f32, copy=False),
        "w_up": fi["L1_ffn_w_up"].astype(f32, copy=False),
        "w_down": fi["L1_ffn_w_down"].astype(f32, copy=False),
        "lutcols": lutcols,
        "eye_f": eye,
        "had_f": had,
        "qnw_rep": qnw,
        "knw_rep": knw_rep,
        "knw_col": knw_col,
        "qalnw_rep": qalnw,
        "kvalnw_rep": kvalnw,
        "kvcb_rep": kvcb,
        "idxb_rep": idxb,
        "cs1k": cs1k,
    }

    percore_h = {}
    for h in range(2):
        (cs_own, cs_prv, spmask, spflag, wm0, wmN, l1mA,
         l1mB) = _core_tables(h, cosf, sinf)
        percore_h[h] = {
            "cs0own": cs_own, "cs0prv": cs_prv, "cs1q": cs_own,
            "spmask": spmask, "spflag": spflag, "wm0": wm0, "wmN": wmN,
            "l1mA": l1mA, "l1mB": l1mB,
        }

    maps = []
    for c in range(8):
        s, h = c // 2, c % 2
        xs = x[s]
        own = np.concatenate(
            [xs[(2 * t + h) * 128:(2 * t + h + 1) * 128] for t in range(16)])
        prv_tiles = []
        for t in range(16):
            g = 2 * t + h - 1
            if g < 0:
                prv_tiles.append(np.zeros((128, 2048), f32))
            else:
                prv_tiles.append(xs[g * 128:(g + 1) * 128])
        prv = np.concatenate(prv_tiles)
        m = dict(common)
        m.update(percore_h[h])
        m.update({
            "x_own": np.ascontiguousarray(own),
            "x_prev": np.ascontiguousarray(prv),
            "x_full": np.ascontiguousarray(xs),
            "gidxT": np.ascontiguousarray(gidxT[c * 256:(c + 1) * 256]),
            "uidxT": np.ascontiguousarray(uidxT[c * 256:(c + 1) * 256]),
            "didxT": np.ascontiguousarray(
                didxT_full[:, c * 256:(c + 1) * 256]).reshape(688, 2048),
            "wupT": np.ascontiguousarray(wupT_full[c * 256:(c + 1) * 256]),
            "wdnT": np.ascontiguousarray(wdnT_full[c * 512:(c + 1) * 512]),
        })
        maps.append(m)

    nc, io, out = build_program(stop=_STOP)
    res = run_bass_kernel_spmd(nc, maps, core_ids=list(range(8)))

    y = np.zeros((4, 4096, 2048), np.float32)
    for c in range(8):
        s, h = c // 2, c % 2
        oc = res.results[c]["out"]
        for t in range(16):
            y[s, (2 * t + h) * 128:(2 * t + h + 1) * 128] = \
                oc[t * 128:(t + 1) * 128]
    if _STOP is not None:
        return [res.results[c]["out"] for c in range(8)]
    return y


_STOP = None


if __name__ == "__main__":
    nc, io, out = build_program()
    print("BUILD OK")


# version 11 (final)
# speedup vs baseline: 1.0259x; 1.0131x over previous
import math
import numpy as np
import ml_dtypes

import concourse.bass as bass
import concourse.mybir as mybir
from concourse.bass_utils import run_bass_kernel_spmd

F32 = mybir.dt.float32
BF16 = mybir.dt.bfloat16
I32 = mybir.dt.int32
AF = mybir.ActivationFunctionType
OP = mybir.AluOpType
AX = mybir.AxisListType
BF = ml_dtypes.bfloat16

NEG = -1.0e30
SC0 = float(1.0 / np.float32(np.sqrt(128.0)))
SC1 = float(np.float64(192.0) ** -0.5)
INV127 = float(1.0 / 127.0)

G_ROWS, G_COLS = 256, 5504
D_ROWS, D_COLS = 688, 2048
TU_ROWS, TU_COLS = 256, 4096
TD_ROWS, TD_COLS = 512, 2048
OFF_G = 0
OFF_U = OFF_G + G_ROWS * G_COLS
OFF_D = OFF_U + G_ROWS * G_COLS
OFF_TU = OFF_D + D_ROWS * D_COLS
OFF_TD = OFF_TU + TU_ROWS * TU_COLS
PACK = OFF_TD + TD_ROWS * TD_COLS


class SemC:
    def __init__(self, nc, name):
        self.h = nc.semaphore(name).__enter__()
        self.n = 0

    def inc(self, instr, k=1):
        instr.then_inc(self.h, k)
        self.n += k
        return instr

    def wait(self, eng):
        if self.n:
            eng.wait_ge(self.h, self.n)

    def wait_at(self, eng, val):
        if val:
            eng.wait_ge(self.h, val)


class Psum:

    def __init__(self, buf):
        self.buf = buf
        self.dep = None

    def acq(self, eng):
        if self.dep is not None:
            self.dep[0].wait_at(eng, self.dep[1])
            self.dep = None

    def rel(self, sem):
        self.dep = (sem, sem.n)


def build_program(stop=None):
    nc = bass.Bass("TRN2", target_bir_lowering=False, debug=False)

    def din(name, shape, dt=F32):
        return nc.dram_tensor(name, shape, dt, kind="ExternalInput")

    io = {}
    io["w_qc"] = din("w_qc", [2048, 512])
    io["w_qup"] = din("w_qup", [512, 2048])
    io["kvc_w"] = din("kvc_w", [2048, 128])
    io["kvc_wz"] = din("kvc_wz", [2048, 128])
    io["k_proj_w"] = din("k_proj_w", [2048, 128])
    io["v_proj_w"] = din("v_proj_w", [2048, 128])
    io["idx_c_w"] = din("idx_c_w", [2048, 64])
    io["idx_c_wz"] = din("idx_c_wz", [2048, 64])
    io["w_dqT"] = din("w_dqT", [64, 2048])
    io["w_iuq"] = din("w_iuq", [64, 256])
    io["w_w"] = din("w_w", [2048, 4])
    io["group_w"] = din("group_w", [2048, 512])
    io["final_w"] = din("final_w", [2048, 2048])
    io["Wqa"] = din("Wqa", [2048, 1536])
    io["Wqb"] = din("Wqb", [1536, 3072])
    io["Wkva"] = din("Wkva", [2048, 576])
    io["Wkvb_n"] = din("Wkvb_n", [512, 2048])
    io["Wkvb_v"] = din("Wkvb_v", [512, 2048])
    io["Wo"] = din("Wo", [2048, 2048])
    io["w_up"] = din("w_up", [4096, 2048])
    io["w_down"] = din("w_down", [2048, 4096])
    io["gidxT"] = din("gidxT", [256, 5504], I32)
    io["uidxT"] = din("uidxT", [256, 5504], I32)
    io["didxT"] = din("didxT", [688, 2048], I32)
    io["wupT"] = din("wupT", [256, 4096])
    io["wdnT"] = din("wdnT", [512, 2048])
    io["lutcols"] = din("lutcols", [128, 16])
    io["eye_f"] = din("eye_f", [128, 128])
    io["had_f"] = din("had_f", [128, 128])
    io["qnw_rep"] = din("qnw_rep", [128, 2048])
    io["knw_rep"] = din("knw_rep", [128, 128])
    io["knw_col"] = din("knw_col", [128, 1])
    io["qalnw_rep"] = din("qalnw_rep", [128, 1536])
    io["kvalnw_rep"] = din("kvalnw_rep", [128, 512])
    io["kvcb_rep"] = din("kvcb_rep", [128, 128])
    io["idxb_rep"] = din("idxb_rep", [64, 128])
    io["cs0own"] = din("cs0own", [128, 1024])
    io["cs0prv"] = din("cs0prv", [128, 1024])
    io["cs1q"] = din("cs1q", [128, 1024])
    io["cs1k"] = din("cs1k", [4096, 64])
    io["spmask"] = din("spmask", [128, 4096])
    io["spflag"] = din("spflag", [128, 16])
    io["wm0"] = din("wm0", [128, 384])
    io["wmN"] = din("wmN", [128, 384])
    io["l1mA"] = din("l1mA", [128, 512])
    io["l1mB"] = din("l1mB", [128, 512])
    io["x_own"] = din("x_own", [2048, 2048])
    io["x_prev"] = din("x_prev", [2048, 2048])
    io["x_full"] = din("x_full", [4096, 2048])

    out = nc.dram_tensor("out", [2048, 2048], F32, kind="ExternalOutput")

    d_pack_src = nc.dram_tensor("d_pack_src", [1, PACK], BF16)
    d_pack = nc.dram_tensor("d_pack", [8, PACK], BF16)
    d_ckv_src = nc.dram_tensor("d_ckv_src", [2048, 576], F32)
    d_ckv = nc.dram_tensor("d_ckv", [2, 2048, 576], F32)
    d_x1 = nc.dram_tensor("d_x1", [2048, 2048], F32)
    d_oT = nc.dram_tensor("d_oT", [16, 128, 2048], BF16)
    d_qaT = nc.dram_tensor("d_qaT", [12, 128, 2048], BF16)
    d_kT = nc.dram_tensor("d_kT", [16, 128, 4096], BF16)
    d_V = nc.dram_tensor("d_V", [16, 4096, 128], BF16)
    d_qT = nc.dram_tensor("d_qT", [16, 192, 2048], BF16)
    d_xn0T = nc.dram_tensor("d_xn0T", [16, 128, 2048], BF16)
    d_xq1 = nc.dram_tensor("d_xq1", [16, 128, 2048], BF16)
    d_h2T = nc.dram_tensor("d_h2T", [32, 128, 2048], BF16)
    d_xq2 = nc.dram_tensor("d_xq2", [32, 128, 2048], BF16)

    t_done = SemC(nc, "t_done")
    v_done = SemC(nc, "v_done")
    s_done = SemC(nc, "s_done")
    g_done = SemC(nc, "g_done")
    q_dma = SemC(nc, "q_dma")
    g_dma = SemC(nc, "g_dma")
    st_dma = SemC(nc, "st_dma")
    cc_sem = SemC(nc, "cc_sem")
    cc_sem2 = SemC(nc, "cc_sem2")

    ALL = {}

    def sb(name, shape, dt):
        h = nc.sbuf_tensor(name, shape, dt).__enter__()
        ALL[name] = h
        return h

    c_eye_f = sb("c_eye_f", [128, 128], F32)
    c_eye_b = sb("c_eye_b", [128, 128], BF16)
    c_had_b = sb("c_had_b", [128, 128], BF16)
    c_lut = sb("c_lut", [128, 16], F32)
    c_scales = sb("c_scales", [128, 2], F32)
    c_knwc = sb("c_knwc", [128, 1], F32)
    c_ones = sb("c_ones", [128, 512], F32)

    PA = Psum(nc.psum_tensor("PA", [128, 512], F32).__enter__())
    PB = Psum(nc.psum_tensor("PB", [128, 512], F32).__enter__())
    PC = Psum(nc.psum_tensor("PC", [128, 512], F32).__enter__())
    PD = Psum(nc.psum_tensor("PD", [128, 512], F32).__enter__())
    PE2 = Psum(nc.psum_tensor("PE2", [128, 512], F32).__enter__())
    PF = Psum(nc.psum_tensor("PF", [128, 512], F32).__enter__())
    PTa = Psum(nc.psum_tensor("PTa", [128, 1024], BF16).__enter__())
    PTb = Psum(nc.psum_tensor("PTb", [128, 1024], BF16).__enter__())

    block = nc.Block().__enter__()

    def barrier():
        @block.vector
        def _(v):
            v.drain()
            v_done.inc(v.nop())
        @block.scalar
        def _(s):
            s.drain()
            s_done.inc(s.nop())
        @block.gpsimd
        def _(g):
            g_done.inc(g.nop())
        tv, sv, gv, ttv = v_done.n, s_done.n, g_done.n, t_done.n
        qv, gdv, stv = q_dma.n, g_dma.n, st_dma.n

        def wait_all(eng, skip):
            for sem, val in ((t_done, ttv), (v_done, tv), (s_done, sv),
                             (g_done, gv), (q_dma, qv), (g_dma, gdv),
                             (st_dma, stv)):
                if sem is not skip:
                    sem.wait_at(eng, val)
        @block.tensor
        def _(t):
            wait_all(t, t_done)
        @block.vector
        def _(v):
            wait_all(v, v_done)
        @block.scalar
        def _(s):
            wait_all(s, s_done)
        @block.gpsimd
        def _(g):
            wait_all(g, g_done)
        @block.sync
        def _(q):
            wait_all(q, None)

    def loadq(dst_ap, src_ap, wait=()):
        @block.sync
        def _(q):
            for sem, val in wait:
                sem.wait_at(q, val)
            q_dma.inc(q.dma_start(out=dst_ap, in_=src_ap), 16)

    def loadg(dst_ap, src_ap, wait=()):
        @block.gpsimd
        def _(g):
            for sem, val in wait:
                sem.wait_at(g, val)
            g_dma.inc(g.dma_start(out=dst_ap, in_=src_ap), 16)

    def storeg(dst_ap, src_ap, wait=()):
        @block.gpsimd
        def _(g):
            for sem, val in wait:
                sem.wait_at(g, val)
            st_dma.inc(g.dma_start(out=dst_ap, in_=src_ap), 16)

    def ap2(buf, off, dims):
        return bass.AP(buf[:].tensor, off, dims)

    def dump_and_stop(src_dram, nrows, ncols):
        @block.gpsimd
        def _(g):
            st_dma.inc(g.dma_start(out=out.ap()[0:nrows, 0:ncols],
                                   in_=src_dram.ap()), 16)
        barrier()

    def dump_and_stop2(src_dram, dims):
        n = 1
        for _, c in dims:
            n = max(n, c)
        @block.gpsimd
        def _(g):
            st_dma.inc(g.dma_start(out=ap2(out, 0, [[1, n]]),
                                   in_=ap2(src_dram, 0, dims)), 16)
        barrier()

    def layernorm(xbuf, D, outb, tmp, small, eps=1e-5, wait_dma=None):
        @block.vector
        def _(v):
            if wait_dma is not None:
                wait_dma[0].wait_at(v, wait_dma[1])
            v.reduce_sum(small[:, 0:1], xbuf[:, 0:D], axis=AX.X)
            v.tensor_mul(tmp[:, 0:D], xbuf[:, 0:D], xbuf[:, 0:D])
            v.drain()
            v.reduce_sum(small[:, 1:2], tmp[:, 0:D], axis=AX.X)
            v.tensor_scalar(out=small[:, 0:1], in0=small[:, 0:1],
                            scalar1=1.0 / D, scalar2=None, op0=OP.mult)
            v.drain()
            v.tensor_scalar(out=small[:, 2:3], in0=small[:, 1:2],
                            scalar1=1.0 / D, scalar2=None, op0=OP.mult)
            v.drain()
            v.tensor_mul(small[:, 3:4], small[:, 0:1], small[:, 0:1])
            v.drain()
            v.tensor_sub(small[:, 1:2], small[:, 2:3], small[:, 3:4])
            v.drain()
            v.tensor_scalar(out=small[:, 1:2], in0=small[:, 1:2],
                            scalar1=0.0, scalar2=eps, op0=OP.max, op1=OP.add)
            v.drain()
            v_done.inc(v.nop())
        vv = v_done.n
        @block.scalar
        def _(s):
            v_done.wait_at(s, vv)
            s.activation(out=small[:, 2:3], in_=small[:, 1:2], func=AF.Ln)
            s.drain()
            s.activation(out=small[:, 1:2], in_=small[:, 2:3], func=AF.Exp,
                         scale=-0.5)
            s.drain()
            s_done.inc(s.nop())
        svv = s_done.n
        @block.vector
        def _(v):
            s_done.wait_at(v, svv)
            v.tensor_scalar(out=outb[:, 0:D], in0=xbuf[:, 0:D],
                            scalar1=small[:, 0:1], scalar2=small[:, 1:2],
                            op0=OP.subtract, op1=OP.mult)
            v.drain()
            v_done.inc(v.nop())
        return v_done.n

    def rmsnorm_rows(xbuf, D, outb, tmp, small, wrep=None, eps=1e-6,
                     out_slice=None, x_slice=None):
        xa = xbuf if x_slice is None else x_slice
        oa = outb if out_slice is None else out_slice
        @block.vector
        def _(v):
            v.tensor_mul(tmp[:, 0:D], xa, xa)
            v.drain()
            v.reduce_sum(small[:, 0:1], tmp[:, 0:D], axis=AX.X)
            v.drain()
            v.tensor_scalar(out=small[:, 0:1], in0=small[:, 0:1],
                            scalar1=1.0 / D, scalar2=eps, op0=OP.mult,
                            op1=OP.add)
            v.drain()
            v_done.inc(v.nop())
        vv = v_done.n
        @block.scalar
        def _(s):
            v_done.wait_at(s, vv)
            s.activation(out=small[:, 1:2], in_=small[:, 0:1], func=AF.Ln)
            s.drain()
            s.activation(out=small[:, 0:1], in_=small[:, 1:2], func=AF.Exp,
                         scale=-0.5)
            s.drain()
            s_done.inc(s.nop())
        svv = s_done.n
        @block.vector
        def _(v):
            s_done.wait_at(v, svv)
            if wrep is None:
                v.tensor_scalar(out=oa, in0=xa, scalar1=small[:, 0:1],
                                scalar2=None, op0=OP.mult)
            else:
                v.scalar_tensor_tensor(out=oa, in0=xa, scalar=small[:, 0:1],
                                       in1=wrep, op0=OP.mult, op1=OP.mult)
            v.drain()
            v_done.inc(v.nop())
        return v_done.n

    def transp(items, PT):
        assert len(items) <= 8
        vw = v_done.n
        @block.tensor
        def _(t):
            v_done.wait_at(t, vw)
            PT.acq(t)
            for i, (src, _) in enumerate(items):
                ins = t.transpose(PT.buf[:, i * 128:(i + 1) * 128], src,
                                  c_eye_b[:])
            t_done.inc(ins)
        tv = t_done.n
        sw = st_dma.n
        @block.vector
        def _(v):
            t_done.wait_at(v, tv)
            st_dma.wait_at(v, sw)
            for i, (_, dst) in enumerate(items):
                v.tensor_copy(out=dst, in_=PT.buf[:, i * 128:(i + 1) * 128])
            v.drain()
            v_done.inc(v.nop())
        PT.rel(v_done)
        return v_done.n

    loadq(c_eye_f[:], io["eye_f"].ap())
    loadg(c_eye_b[:], io["eye_f"].ap())
    loadg(c_had_b[:], io["had_f"].ap())
    loadq(c_lut[:], io["lutcols"].ap())
    loadq(c_knwc[:], io["knw_col"].ap())

    @block.vector
    def _(v):
        v.memset(c_ones[:], 1.0)
        v.drain()
        v_done.inc(v.nop())

    with (
        nc.sbuf_tensor("p0_idx", [128, 5504], BF16) as p0_idx,
        nc.sbuf_tensor("p0_acc", [128, 5504], BF16) as p0_acc,
        nc.sbuf_tensor("p0_tmp", [128, 5504], BF16) as p0_tmp,
        nc.sbuf_tensor("p0_w", [128, 4096], F32) as p0_w,
        nc.sbuf_tensor("p0_wt", [128, 4096], BF16) as p0_wt,
        nc.sbuf_tensor("p0_t1", [128, 4096], F32) as p0_t1,
        nc.sbuf_tensor("p0_abs", [128, 48], F32) as p0_abs,
        nc.sbuf_tensor("p0_sml", [128, 8], F32) as p0_sml,
    ):
        regions = [
            (io["gidxT"], 256, 5504, OFF_G),
            (io["uidxT"], 256, 5504, OFF_U),
            (io["didxT"], 688, 2048, OFF_D),
        ]
        for (src, nr, ncol, poff) in regions:
            rt = 0
            while rt * 128 < nr:
                rows = min(128, nr - rt * 128)
                loadg(p0_idx[0:rows, 0:ncol],
                      src.ap()[rt * 128:rt * 128 + rows, :])
                gw = g_dma.n
                @block.vector
                def _(v, rows=rows, ncol=ncol, gw=gw, qw=q_dma.n,
                      sw=st_dma.n):
                    g_dma.wait_at(v, gw)
                    q_dma.wait_at(v, qw)
                    st_dma.wait_at(v, sw)
                    v.tensor_scalar(out=p0_acc[0:rows, 0:ncol],
                                    in0=p0_idx[0:rows, 0:ncol],
                                    scalar1=0.0, scalar2=c_lut[0:rows, 0:1],
                                    op0=OP.is_equal, op1=OP.mult)
                    for vv in range(1, 16):
                        v.tensor_scalar(out=p0_tmp[0:rows, 0:ncol],
                                        in0=p0_idx[0:rows, 0:ncol],
                                        scalar1=float(vv),
                                        scalar2=c_lut[0:rows, vv:vv + 1],
                                        op0=OP.is_equal, op1=OP.mult)
                        v.drain()
                        v.tensor_add(p0_acc[0:rows, 0:ncol],
                                     p0_acc[0:rows, 0:ncol],
                                     p0_tmp[0:rows, 0:ncol])
                        v.drain()
                    v_done.inc(v.nop())
                vv_ = v_done.n
                storeg(
                    ap2(d_pack_src, poff + rt * 128 * ncol,
                        [[ncol, rows], [1, ncol]]),
                    p0_acc[0:rows, 0:ncol],
                    wait=[(v_done, vv_)])
                @block.gpsimd
                def _(g, vv_=vv_):
                    v_done.wait_at(g, vv_)
                rt += 1

        for t in range(32):
            loadq(p0_w[:, 0:2048], io["w_up"].ap()[t * 128:(t + 1) * 128, :],
                  wait=[(v_done, v_done.n)])
            qv = q_dma.n
            @block.vector
            def _(v, t=t, qv=qv):
                q_dma.wait_at(v, qv)
                v.reduce_sum(p0_abs[:, t:t + 1], p0_w[:, 0:2048], axis=AX.X,
                             apply_absolute_value=True)
                v.drain()
                v_done.inc(v.nop())
        for t in range(16):
            loadq(p0_w[:, 0:4096], io["w_down"].ap()[t * 128:(t + 1) * 128, :],
                  wait=[(v_done, v_done.n)])
            qv = q_dma.n
            @block.vector
            def _(v, t=t, qv=qv):
                q_dma.wait_at(v, qv)
                v.reduce_sum(p0_abs[:, 32 + t:33 + t], p0_w[:, 0:4096],
                             axis=AX.X, apply_absolute_value=True)
                v.drain()
                v_done.inc(v.nop())
        @block.vector
        def _(v):
            v.reduce_sum(p0_sml[:, 0:1], p0_abs[:, 0:32], axis=AX.X)
            v.reduce_sum(p0_sml[:, 1:2], p0_abs[:, 32:48], axis=AX.X)
            v.drain()
            v.tensor_copy(out=p0_w[:, 0:2], in_=p0_sml[:, 0:2])
            v.drain()
            v_done.inc(v.nop())
        vv_ = v_done.n
        @block.tensor
        def _(t):
            v_done.wait_at(t, vv_)
            PA.acq(t)
            ins = t.matmul(PA.buf[0:1, 0:2], c_ones[:, 0:1],
                           p0_w[:, 0:2], start=True, stop=True)
            t_done.inc(ins)
        tv = t_done.n
        @block.vector
        def _(v):
            t_done.wait_at(v, tv)
            v.tensor_scalar(out=p0_sml[0:1, 0:2], in0=PA.buf[0:1, 0:2],
                            scalar1=1.0 / (4096.0 * 2048.0), scalar2=1e-5,
                            op0=OP.mult, op1=OP.max)
            v.drain()
            v_done.inc(v.nop())
        PA.rel(v_done)
        vv_ = v_done.n
        @block.tensor
        def _(t):
            v_done.wait_at(t, vv_)
            PB.acq(t)
            ins = t.matmul(PB.buf[:, 0:2], c_ones[0:1, 0:128],
                           p0_sml[0:1, 0:2], start=True, stop=True)
            t_done.inc(ins)
        tv = t_done.n
        @block.vector
        def _(v):
            t_done.wait_at(v, tv)
            v.tensor_copy(out=c_scales[:, 0:2], in_=PB.buf[:, 0:2])
            v.drain()
            v.tensor_scalar(out=p0_sml[:, 4:5], in0=c_scales[:, 0:1],
                            scalar1=0.5, scalar2=None, op0=OP.mult)
            v.tensor_scalar(out=p0_sml[:, 5:6], in0=c_scales[:, 0:1],
                            scalar1=-0.5, scalar2=None, op0=OP.mult)
            v.tensor_scalar(out=p0_sml[:, 6:7], in0=c_scales[:, 1:2],
                            scalar1=0.5, scalar2=None, op0=OP.mult)
            v.tensor_scalar(out=p0_sml[:, 7:8], in0=c_scales[:, 1:2],
                            scalar1=-0.5, scalar2=None, op0=OP.mult)
            v.drain()
            v_done.inc(v.nop())
        PB.rel(v_done)

        tern_jobs = [
            (io["wupT"], 2, 4096, OFF_TU, 4, 5),
            (io["wdnT"], 4, 2048, OFF_TD, 6, 7),
        ]
        for (src, ntile, ncol, poff, hi, lo) in tern_jobs:
            for t in range(ntile):
                loadq(p0_w[:, 0:ncol], src.ap()[t * 128:(t + 1) * 128, :],
                      wait=[(v_done, v_done.n)])
                qv = q_dma.n
                @block.vector
                def _(v, ncol=ncol, hi=hi, lo=lo, qv=qv, sw=st_dma.n):
                    q_dma.wait_at(v, qv)
                    st_dma.wait_at(v, sw)
                    v.tensor_scalar(out=p0_t1[:, 0:ncol], in0=p0_w[:, 0:ncol],
                                    scalar1=p0_sml[:, hi:hi + 1], scalar2=None,
                                    op0=OP.is_gt)
                    v.drain()
                    v.tensor_scalar(out=p0_w[:, 0:ncol], in0=p0_w[:, 0:ncol],
                                    scalar1=p0_sml[:, lo:lo + 1], scalar2=None,
                                    op0=OP.is_lt)
                    v.drain()
                    v.tensor_sub(p0_wt[:, 0:ncol], p0_t1[:, 0:ncol],
                                 p0_w[:, 0:ncol])
                    v.drain()
                    v_done.inc(v.nop())
                vv_ = v_done.n
                storeg(
                    ap2(d_pack_src, poff + t * 128 * ncol,
                        [[ncol, 128], [1, ncol]]),
                    p0_wt[:, 0:ncol], wait=[(v_done, vv_)])
                @block.gpsimd
                def _(g, vv_=vv_):
                    v_done.wait_at(g, vv_)

        stv = st_dma.n
        @block.gpsimd
        def _(g):
            st_dma.wait_at(g, stv)
            ins = g.collective_compute(
                "AllGather", OP.bypass, replica_groups=[list(range(8))],
                ins=[d_pack_src.ap().opt()], outs=[d_pack.ap().opt()])
            cc_sem.inc(ins)

    barrier()

    if stop == "ptu":
        @block.gpsimd
        def _(g):
            st_dma.inc(g.dma_start(
                out=ap2(out, 0, [[1, 1048576]]),
                in_=ap2(d_pack_src, OFF_TU, [[1, 1048576]])), 16)
            st_dma.inc(g.dma_start(
                out=ap2(out, 1048576, [[1, 1048576]]),
                in_=ap2(d_pack_src, OFF_TD, [[1, 1048576]])), 16)
        barrier()
        return nc, io, out

    CMS = {}
    CMS_ORDER = []

    def sba(name, shape, dt):
        cm = nc.sbuf_tensor(name, shape, dt)
        h = cm.__enter__()
        CMS[name] = cm
        CMS_ORDER.append(name)
        return h

    def sbfree(*names):
        for n in sorted(names, key=lambda x: -CMS_ORDER.index(x)):
            assert CMS_ORDER[-1] == n, (n, CMS_ORDER[-5:])
            CMS_ORDER.pop()
            CMS.pop(n).__exit__(None, None, None)

    xt = sba("xt", [128, 2048], F32)
    xsq = sba("xsq", [128, 2048], F32)
    xnb = sba("xnb", [128, 2048], BF16)
    xnTtmp = sba("xnTtmp", [128, 16 * 128], BF16)
    sml = sba("sml", [128, 16], F32)
    ev1 = sba("ev1", [128, 512], F32)
    ev2 = sba("ev2", [128, 512], F32)
    evb = sba("evb", [128, 512], BF16)
    pac = sba("pac", [128, 8], F32)
    xnTd = sba("xnTd", [128, 2 * 2048], BF16)

    def pacer(buf_slice, p=128):
        vv_ = v_done.n
        @block.scalar
        def _(s, vv_=vv_):
            v_done.wait_at(s, vv_)
            s.activation(out=pac[0:p, 0:8], in_=buf_slice, func=AF.Exp,
                         scale=0.0)
            s.drain()
            s_done.inc(s.nop())
        return s_done.n

    swkT_o = sba("swkT_o", [128, 17 * 128], BF16)
    swv_o = sba("swv_o", [128, 17 * 128], BF16)
    swkT_p = sba("swkT_p", [128, 16 * 128], BF16)
    swv_p = sba("swv_p", [128, 16 * 128], BF16)
    kcompT = sba("kcompT", [128, 256], F32)
    icompT = sba("icompT", [64, 256], F32)
    icompT_b = sba("icompT_b", [64, 256], BF16)
    skvT_b = sba("skvT_b", [128, 256], BF16)
    skv_b = sba("skv_b", [128, 256], BF16)
    tau = sba("tau", [128, 16], F32)
    cs0own = sba("b_cs0own", [128, 1024], F32)
    spflag = sba("b_spflag", [128, 16], F32)
    spm = sba("spm", [128, 256], F32)
    selmask = sba("selmask", [128, 4096], F32)

    cs0prv = sba("b_cs0prv", [128, 1024], F32)
    knw_rep = sba("b_knw_rep", [128, 128], F32)
    kvcb = sba("kvcb", [128, 128], F32)
    idxb = sba("idxb", [64, 128], F32)
    w_kvc = sba("w_kvc", [128, 16 * 128], BF16)
    w_kvcz = sba("w_kvcz", [128, 16 * 128], BF16)
    w_kp = sba("w_kp", [128, 16 * 128], BF16)
    w_vp = sba("w_vp", [128, 16 * 128], BF16)
    w_idx = sba("w_idx", [128, 16 * 64], BF16)
    w_idxz = sba("w_idxz", [128, 16 * 64], BF16)

    def ld16(dst, src, ncols, cast=True, col0=0):
        for kt in range(16):
            (loadg if cast else loadq)(
                dst[:, kt * ncols:(kt + 1) * ncols],
                src.ap()[kt * 128:(kt + 1) * 128, col0:col0 + ncols])

    ld16(w_kvc, io["kvc_w"], 128)
    ld16(w_kvcz, io["kvc_wz"], 128)
    ld16(w_kp, io["k_proj_w"], 128)
    ld16(w_vp, io["v_proj_w"], 128)
    ld16(w_idx, io["idx_c_w"], 64)
    ld16(w_idxz, io["idx_c_wz"], 64)
    loadq(cs0own[:], io["cs0own"].ap())
    loadq(cs0prv[:], io["cs0prv"].ap())
    loadq(spflag[:], io["spflag"].ap())
    loadq(knw_rep[:], io["knw_rep"].ap())
    loadq(kvcb[:], io["kvcb_rep"].ap())
    loadq(idxb[:], io["idxb_rep"].ap())

    @block.vector
    def _(v):
        v.memset(swkT_o[:, 0:128], 0.0)
        v.memset(swv_o[:, 0:128], 0.0)
        v.drain()
        v_done.inc(v.nop())

    barrier()

    def rope0(qdst, cs, csoff, nh, stride, tmpf, base=0):
        @block.vector
        def _(v):
            for hh in range(nh):
                o = base + hh * stride
                x1 = qdst[:, o:o + 32]
                x2 = qdst[:, o + 32:o + 64]
                cos = cs[:, csoff:csoff + 32]
                sin = cs[:, csoff + 32:csoff + 64]
                v.tensor_mul(tmpf[:, 0:32], x1, cos)
                v.tensor_mul(tmpf[:, 32:64], x2, sin)
                v.tensor_mul(tmpf[:, 64:96], x1, sin)
                v.tensor_mul(tmpf[:, 96:128], x2, cos)
                v.drain()
                v.tensor_sub(x1, tmpf[:, 0:32], tmpf[:, 32:64])
                v.tensor_add(x2, tmpf[:, 64:96], tmpf[:, 96:128])
                v.drain()
            v_done.inc(v.nop())
        return v_done.n

    def kv_proj_tile(kdstT_slice, vdst_slice, cs, csoff):
        vw = v_done.n
        @block.tensor
        def _(t, vw=vw):
            v_done.wait_at(t, vw)
            PA.acq(t)
            for kt in range(16):
                ins = t.matmul(PA.buf[:, 0:128],
                               xnTtmp[:, kt * 128:(kt + 1) * 128],
                               w_kp[:, kt * 128:(kt + 1) * 128],
                               start=(kt == 0), stop=(kt == 15))
            t_done.inc(ins)
            PB.acq(t)
            for kt in range(16):
                ins = t.matmul(PB.buf[:, 0:128],
                               xnTtmp[:, kt * 128:(kt + 1) * 128],
                               w_vp[:, kt * 128:(kt + 1) * 128],
                               start=(kt == 0), stop=(kt == 15))
            t_done.inc(ins)
        tv = t_done.n
        @block.vector
        def _(v):
            t_done.wait_at(v, tv)
            v.tensor_copy(out=ev1[:, 0:128], in_=PA.buf[:, 0:128])
            v.tensor_copy(out=vdst_slice, in_=PB.buf[:, 0:128])
            v.drain()
            v_done.inc(v.nop())
        PA.rel(v_done)
        PB.rel(v_done)
        rmsnorm_rows(ev1, 128, ev1, ev2, sml, wrep=knw_rep[:],
                     out_slice=ev1[:, 0:128], x_slice=ev1[:, 0:128])
        rope0(ev1, cs, csoff, 1, 128, ev2)
        @block.vector
        def _(v):
            v.tensor_copy(out=evb[:, 0:128], in_=ev1[:, 0:128])
            v.drain()
            v_done.inc(v.nop())
        transp([(evb[:, 0:128], kdstT_slice)], PTa)

    def matmul_wait(t, xn_wait):
        pass

    def ln_transpose_tile(src_dram_ap, dst16=None, store_tile=None):
        loadq(xt[:], src_dram_ap, wait=[(v_done, v_done.n)])
        layernorm(xt, 2048, xnb, xsq, sml, wait_dma=(q_dma, q_dma.n))
        for half in range(2):
            items = [(xnb[:, (half * 8 + j) * 128:(half * 8 + j + 1) * 128],
                      xnTtmp[:, (half * 8 + j) * 128:(half * 8 + j + 1) * 128])
                     for j in range(8)]
            transp(items, PTa if half == 0 else PTb)
        if store_tile is not None:
            tl = store_tile
            vv = v_done.n
            storeg(ap2(d_xn0T, tl * 128,
                       [[2048, 128], [128 * 2048, 16], [1, 128]]),
                   ap2(xnTtmp, 0, [[2048, 128], [128, 16], [1, 128]]),
                   wait=[(v_done, vv)])

    for tl in range(16):
        ln_transpose_tile(io["x_own"].ap()[tl * 128:(tl + 1) * 128, :],
                          store_tile=tl)
        kv_proj_tile(swkT_o[:, (tl + 1) * 128:(tl + 2) * 128],
                     swv_o[:, (tl + 1) * 128:(tl + 2) * 128],
                     cs0own, tl * 64)

    for tl in range(16):
        ln_transpose_tile(io["x_prev"].ap()[tl * 128:(tl + 1) * 128, :])
        kv_proj_tile(swkT_p[:, tl * 128:(tl + 1) * 128],
                     swv_p[:, tl * 128:(tl + 1) * 128],
                     cs0prv, tl * 64)

    def compress_tile(gt):
        ln_transpose_tile(io["x_full"].ap()[gt * 128:(gt + 1) * 128, :])
        vwc = v_done.n
        @block.tensor
        def _(t, vwc=vwc):
            v_done.wait_at(t, vwc)
            for (P, wmat, dpr, ncw) in ((PA, w_kvc, 128, 128),
                                        (PB, w_kvcz, 128, 128),
                                        (PC, w_idx, 64, 64),
                                        (PD, w_idxz, 64, 64)):
                P.acq(t)
                for kt in range(16):
                    ins = t.matmul(P.buf[0:dpr, 0:128],
                                   wmat[:, kt * ncw:(kt + 1) * ncw],
                                   xnTtmp[:, kt * 128:(kt + 1) * 128],
                                   start=(kt == 0), stop=(kt == 15))
                t_done.inc(ins)
        tv = t_done.n
        for (Pc, Pz, dpr, bias, dstT) in (
            (PA, PB, 128, kvcb, kcompT),
            (PC, PD, 64, idxb, icompT),
        ):
            @block.vector
            def _(v, Pz=Pz, dpr=dpr, bias=bias, tv=tv):
                t_done.wait_at(v, tv)
                v.tensor_add(ev1[0:dpr, 0:128], Pz.buf[0:dpr, 0:128],
                             bias[0:dpr, 0:128])
                v.drain()
                v_done.inc(v.nop())
            vv = v_done.n
            @block.scalar
            def _(s, dpr=dpr, vv=vv):
                v_done.wait_at(s, vv)
                s.activation(out=ev1[0:dpr, 0:128], in_=ev1[0:dpr, 0:128],
                             func=AF.Exp)
                s.drain()
                s_done.inc(s.nop())
            sv = s_done.n
            @block.vector
            def _(v, Pc=Pc, dpr=dpr, sv=sv, dstT=dstT, gt=gt):
                s_done.wait_at(v, sv)
                v.tensor_mul(ev2[0:dpr, 0:128], ev1[0:dpr, 0:128],
                             Pc.buf[0:dpr, 0:128])
                v.drain()
                v.reduce_sum(ap2(xsq, 0, [[2048, dpr], [1, 8]]),
                             ap2(ev2, 0, [[512, dpr], [16, 8], [1, 16]]),
                             axis=AX.X)
                v.reduce_sum(ap2(xsq, 8, [[2048, dpr], [1, 8]]),
                             ap2(ev1, 0, [[512, dpr], [16, 8], [1, 16]]),
                             axis=AX.X)
                v.drain()
                v.reciprocal(ap2(xsq, 16, [[2048, dpr], [1, 8]]),
                             ap2(xsq, 8, [[2048, dpr], [1, 8]]))
                v.drain()
                v.tensor_mul(dstT[0:dpr, gt * 8:(gt + 1) * 8],
                             ap2(xsq, 0, [[2048, dpr], [1, 8]]),
                             ap2(xsq, 16, [[2048, dpr], [1, 8]]))
                v.drain()
                v_done.inc(v.nop())
            Pc.rel(v_done)
            Pz.rel(v_done)

    for gt in range(32):
        compress_tile(gt)

    @block.vector
    def _(v):
        v.tensor_mul(ev1[:, 0:256], kcompT[:], kcompT[:])
        v.drain()
        v_done.inc(v.nop())
    vv = v_done.n
    @block.tensor
    def _(t):
        v_done.wait_at(t, vv)
        PA.acq(t)
        ins = t.matmul(PA.buf[0:1, 0:256], c_ones[:, 0:1], ev1[:, 0:256],
                       start=True, stop=True)
        t_done.inc(ins)
    tv = t_done.n
    @block.vector
    def _(v):
        t_done.wait_at(v, tv)
        v.tensor_scalar(out=ev2[0:1, 0:256], in0=PA.buf[0:1, 0:256],
                        scalar1=1.0 / 128.0, scalar2=1e-6, op0=OP.mult,
                        op1=OP.add)
        v.drain()
        v_done.inc(v.nop())
    PA.rel(v_done)
    vv = v_done.n
    @block.scalar
    def _(s):
        v_done.wait_at(s, vv)
        s.activation(out=ev2[0:1, 256:512], in_=ev2[0:1, 0:256], func=AF.Ln)
        s.drain()
        s.activation(out=ev2[0:1, 0:256], in_=ev2[0:1, 256:512], func=AF.Exp,
                     scale=-0.5)
        s.drain()
        s_done.inc(s.nop())
    sv = s_done.n
    @block.tensor
    def _(t):
        s_done.wait_at(t, sv)
        PB.acq(t)
        ins = t.matmul(PB.buf[:, 0:256], c_ones[0:1, 0:128],
                       ev2[0:1, 0:256], start=True, stop=True)
        t_done.inc(ins)
    tv = t_done.n
    @block.vector
    def _(v):
        t_done.wait_at(v, tv)
        v.scalar_tensor_tensor(out=ev1[:, 0:256], in0=kcompT[:],
                               scalar=c_knwc[:, 0:1], in1=PB.buf[:, 0:256],
                               op0=OP.mult, op1=OP.mult)
        v.drain()
        v.tensor_copy(out=skvT_b[:], in_=ev1[:, 0:256])
        v.tensor_copy(out=icompT_b[0:64, :], in_=icompT[0:64, :])
        v.drain()
        v_done.inc(v.nop())
    PB.rel(v_done)
    transp([(skvT_b[:, 0:128], skv_b[:, 0:128]),
            (skvT_b[:, 128:256], skv_b[:, 128:256])], PTa)

    barrier()
    sbfree("b_cs0prv", "b_knw_rep", "kvcb", "idxb", "w_kvc", "w_kvcz",
           "w_kp", "w_vp", "w_idx", "w_idxz")

    if stop == "xn0":
        dump_and_stop2(d_xn0T, [[1, 2048 * 2048]])
        return nc, io, out

    iscall = sba("iscall", [128, 4096], F32)
    xn_dma = SemC(nc, "xn_dma")
    w_dqTb = sba("w_dqTb", [64, 2048], BF16)
    w_iuqb = sba("w_iuqb", [64, 256], BF16)
    w_wb = sba("w_wb", [128, 16 * 4], BF16)
    weff = sba("weff", [128, 16 * 256], BF16)
    qiT = sba("qiT", [64, 512], BF16)
    wihb = sba("wihb", [128, 4], F32)
    dbg = sba("dbg", [128, 16 * 516 + 4096], F32) if stop == "p2dbg" else None

    loadg(w_dqTb[:], io["w_dqT"].ap())
    ld16(w_wb, io["w_w"], 4)
    loadg(w_iuqb[:], io["w_iuq"].ap())
    barrier()

    for kt in range(16):
        @block.tensor
        def _(t, kt=kt):
            PA.acq(t)
            t.matmul(PA.buf[0:128, 0:256],
                     w_dqTb[0:64, kt * 128:(kt + 1) * 128],
                     w_iuqb[0:64, 0:256], start=True, stop=True)
            ins2 = t.matmul(PA.buf[0:1, 256:257], c_ones[0:1, 0:1],
                            c_ones[0:1, 0:1], start=True, stop=True)
            t_done.inc(ins2)
        tv = t_done.n
        @block.vector
        def _(v, kt=kt, tv=tv):
            t_done.wait_at(v, tv)
            v.tensor_copy(out=ev1[:, 481:482], in_=PA.buf[0:128, 0:1])
            v.drain()
            v.tensor_copy(out=weff[:, kt * 256:(kt + 1) * 256],
                          in_=PA.buf[0:128, 0:256])
            v.drain()
            v_done.inc(v.nop())
        PA.rel(v_done)
    barrier()

    def xn_load(tl, bi):
        tw_ = t_done.n
        @block.sync
        def _(q, tl=tl, bi=bi, tw_=tw_):
            t_done.wait_at(q, tw_)
            xn_dma.inc(q.dma_start(
                out=xnTd[:, bi * 2048:(bi + 1) * 2048],
                in_=ap2(d_xn0T, tl * 128,
                        [[2048, 128], [128 * 2048, 16], [1, 128]])), 16)
        return xn_dma.n

    xn_rdy = [xn_load(0, 0)]
    for tl in range(16):
        bo = (tl % 2) * 2048
        if tl + 1 < 16:
            xn_rdy.append(xn_load(tl + 1, (tl + 1) % 2))
        qv = xn_rdy[tl]
        @block.tensor
        def _(t, bo=bo, qv=qv):
            xn_dma.wait_at(t, qv)
            PA.acq(t)
            for ih in range(4):
                for kt in range(16):
                    t.matmul(PA.buf[0:64, ih * 128:(ih + 1) * 128],
                             weff[:, kt * 256 + ih * 64:
                                  kt * 256 + (ih + 1) * 64],
                             xnTd[:, bo + kt * 128:bo + (kt + 1) * 128],
                             start=(kt == 0), stop=(kt == 15))
            PB.acq(t)
            for kt in range(16):
                t.matmul(PB.buf[0:128, 0:4],
                         xnTd[:, bo + kt * 128:bo + (kt + 1) * 128],
                         w_wb[:, kt * 4:(kt + 1) * 4],
                         start=(kt == 0), stop=(kt == 15))
            ins2 = t.matmul(PB.buf[0:1, 8:9], c_ones[0:1, 0:1],
                            c_ones[0:1, 0:1], start=True, stop=True)
            t_done.inc(ins2)
        tv = t_done.n
        @block.vector
        def _(v, tv=tv):
            t_done.wait_at(v, tv)
            v.tensor_copy(out=ev1[:, 481:482], in_=PB.buf[0:128, 0:1])
            v.drain()
            v.tensor_copy(out=qiT[0:64, :], in_=PA.buf[0:64, 0:512])
            v.tensor_copy(out=wihb[:], in_=PB.buf[0:128, 0:4])
            v.drain()
            v_done.inc(v.nop())
        PA.rel(v_done)
        PB.rel(v_done)
        if stop == "p2dbg":
            @block.vector
            def _(v, tl=tl, bo=bo):
                v.tensor_copy(out=dbg[:, tl * 516:tl * 516 + 4], in_=wihb[:])
                v.tensor_copy(out=dbg[0:64, tl * 516 + 4:tl * 516 + 516],
                              in_=qiT[0:64, :])
                if tl in (0, 8):
                    off = 16 * 516 + (tl // 8) * 2048
                    v.tensor_copy(out=dbg[:, off:off + 2048],
                                  in_=xnTd[:, bo:bo + 2048])
                v.drain()
                v_done.inc(v.nop())
        loadq(spm[:], io["spmask"].ap()[:, tl * 256:(tl + 1) * 256],
              wait=[(v_done, v_done.n)])
        qv = q_dma.n
        svq = pacer(qiT[0:64, 0:8], p=64)
        for ih in range(4):
            @block.tensor
            def _(t, ih=ih, svq=svq):
                s_done.wait_at(t, svq)
                PD.acq(t)
                ins = t.matmul(PD.buf[0:128, 0:256],
                               qiT[0:64, ih * 128:(ih + 1) * 128],
                               icompT_b[0:64, 0:256], start=True, stop=True)
                t_done.inc(ins)
            tv = t_done.n
            @block.vector
            def _(v, ih=ih, tv=tv, qv=qv, tl=tl):
                t_done.wait_at(v, tv)
                v.tensor_copy(out=ev1[:, 481:482], in_=PD.buf[0:128, 0:1])
                v.drain()
                v.tensor_relu(ev1[:, 0:256], PD.buf[0:128, 0:256])
                v.drain()
                if ih == 0:
                    v.tensor_scalar(out=ev2[:, 0:256], in0=ev1[:, 0:256],
                                    scalar1=wihb[:, 0:1], scalar2=None,
                                    op0=OP.mult)
                else:
                    v.scalar_tensor_tensor(out=ev2[:, 0:256],
                                           in0=ev1[:, 0:256],
                                           scalar=wihb[:, ih:ih + 1],
                                           in1=ev2[:, 0:256],
                                           op0=OP.mult, op1=OP.add)
                v.drain()
                if ih == 3:
                    q_dma.wait_at(v, qv)
                    v.tensor_add(iscall[:, tl * 256:(tl + 1) * 256],
                                 ev2[:, 0:256], spm[:])
                    v.drain()
                v_done.inc(v.nop())
            PD.rel(v_done)

    if stop == "p2dbg":
        barrier()
        NDBG = 16 * 516 + 4096
        storeg(ap2(d_x1, 0, [[NDBG, 128], [1, NDBG]]), dbg[:])
        dump_and_stop2(d_x1, [[1, 128 * NDBG]])
        return nc, io, out

    barrier()
    sbfree("w_dqTb", "w_iuqb", "w_wb", "weff", "qiT", "wihb")

    srtA = sba("srtA", [128, 2048], F32)
    srtB = sba("srtB", [128, 2048], F32)

    def sort_view(buf, k, j, bitk, bitj, nt):
        hi_cnt = max(256 // (2 * k), 1)
        mid_cnt = max(k // (2 * j), 1)
        off = bitk * k + bitj * j
        dims = [buf[:].ap[0], [256, nt]]
        if hi_cnt > 1:
            dims.append([2 * k, hi_cnt])
        if mid_cnt > 1:
            dims.append([2 * j, mid_cnt])
        dims.append([1, j])
        return bass.AP(buf[:].tensor, off, dims)

    for half in range(2):
        @block.vector
        def _(v, half=half):
            v.tensor_copy(out=srtA[:], in_=iscall[:, half * 2048:(half + 1) * 2048])
            v.drain()
            cur, oth = srtA, srtB
            k = 2
            while k <= 256:
                j = k // 2
                while j >= 1:
                    for bitk in (0, 1):
                        if k == 256 and bitk == 1:
                            continue
                        lo0 = sort_view(cur, k, j, bitk, 0, 8)
                        hi0 = sort_view(cur, k, j, bitk, 1, 8)
                        olo = sort_view(oth, k, j, bitk, 0, 8)
                        ohi = sort_view(oth, k, j, bitk, 1, 8)
                        if bitk == 0:
                            v.tensor_tensor(out=olo, in0=lo0, in1=hi0, op=OP.max)
                            v.tensor_tensor(out=ohi, in0=lo0, in1=hi0, op=OP.min)
                        else:
                            v.tensor_tensor(out=olo, in0=lo0, in1=hi0, op=OP.min)
                            v.tensor_tensor(out=ohi, in0=lo0, in1=hi0, op=OP.max)
                    v.drain()
                    cur, oth = oth, cur
                    j //= 2
                k *= 2
            assert cur is srtA
            v.tensor_copy(out=tau[:, half * 8:(half + 1) * 8],
                          in_=ap2(srtA, 63, [[2048, 128], [256, 8]]))
            v.drain()
            v_done.inc(v.nop())

    for tl in range(16):
        loadq(spm[:], io["spmask"].ap()[:, tl * 256:(tl + 1) * 256],
              wait=[(v_done, v_done.n)])
        qv = q_dma.n
        @block.vector
        def _(v, tl=tl, qv=qv):
            q_dma.wait_at(v, qv)
            isl = iscall[:, tl * 256:(tl + 1) * 256]
            gt_ = srtA[:, 0:256]
            eq_ = srtA[:, 256:512]
            incl = srtA[:, 512:768]
            ind = srtA[:, 768:1024]
            v.tensor_scalar(out=gt_, in0=isl, scalar1=tau[:, tl:tl + 1],
                            scalar2=None, op0=OP.is_gt)
            v.tensor_scalar(out=eq_, in0=isl, scalar1=tau[:, tl:tl + 1],
                            scalar2=None, op0=OP.is_equal)
            v.drain()
            v.reduce_sum(sml[:, 0:1], gt_, axis=AX.X)
            v.tensor_tensor_scan(out=incl, data0=eq_, data1=c_ones[:, 0:256],
                                 initial=0.0, op0=OP.add, op1=OP.mult)
            v.drain()
            v.tensor_scalar(out=sml[:, 1:2], in0=sml[:, 0:1], scalar1=-1.0,
                            scalar2=64.0, op0=OP.mult, op1=OP.add)
            v.tensor_sub(incl, incl, eq_)
            v.drain()
            v.tensor_scalar(out=ind, in0=incl, scalar1=sml[:, 1:2],
                            scalar2=None, op0=OP.is_lt)
            v.drain()
            v.tensor_mul(eq_, eq_, ind)
            v.drain()
            v.tensor_add(gt_, gt_, eq_)
            v.drain()
            v.tensor_scalar(out=gt_, in0=gt_, scalar1=1.0, scalar2=1.0e30,
                            op0=OP.subtract, op1=OP.mult)
            v.drain()
            v.tensor_add(selmask[:, tl * 256:(tl + 1) * 256], gt_, spm[:])
            v.drain()
            v_done.inc(v.nop())

    if stop in ("isc", "sel"):
        buf = iscall if stop == "isc" else selmask
        vvx = v_done.n
        storeg(ap2(d_x1, 0, [[4096, 128], [1, 4096]]), buf[:],
               wait=[(v_done, vvx)])
        dump_and_stop2(d_x1, [[1, 2048 * 2048]])
        return nc, io, out

    barrier()
    sbfree("srtA", "srtB", "iscall")

    w_qcb = sba("w_qcb", [128, 16 * 512], BF16)
    w_qupb = sba("w_qupb", [128, 4 * 2048], BF16)
    w_gwb = sba("w_gwb", [128, 16 * 512], BF16)
    b_qnw = sba("b_qnw", [128, 2048], F32)
    b_wm0 = sba("b_wm0", [128, 384], F32)
    b_wmN = sba("b_wmN", [128, 384], F32)
    qT = sba("qT", [128, 16 * 128], BF16)
    qcT = sba("qcT", [128, 4 * 128], BF16)
    ptb = sba("ptb", [128, 384], BF16)
    attnac = sba("attnac", [128, 2048], F32)

    ld16(w_qcb, io["w_qc"], 512)
    for j in range(4):
        loadg(w_qupb[:, j * 2048:(j + 1) * 2048],
              io["w_qup"].ap()[j * 128:(j + 1) * 128, :])
    ld16(w_gwb, io["group_w"], 512)
    loadq(b_qnw[:], io["qnw_rep"].ap())
    loadq(b_wm0[:], io["wm0"].ap())
    loadq(b_wmN[:], io["wmN"].ap())
    barrier()

    xn_rdy = [xn_load(0, 0)]
    for tl in range(16):
        bo = (tl % 2) * 2048
        if tl + 1 < 16:
            xn_rdy.append(xn_load(tl + 1, (tl + 1) % 2))
        qv = xn_rdy[tl]
        @block.tensor
        def _(t, bo=bo, qv=qv):
            xn_dma.wait_at(t, qv)
            PA.acq(t)
            for kt in range(16):
                ins = t.matmul(PA.buf[:, 0:512],
                               xnTd[:, bo + kt * 128:bo + (kt + 1) * 128],
                               w_qcb[:, kt * 512:(kt + 1) * 512],
                               start=(kt == 0), stop=(kt == 15))
            t_done.inc(ins)
        tv = t_done.n
        @block.vector
        def _(v, tv=tv):
            t_done.wait_at(v, tv)
            v.tensor_copy(out=evb[:, 0:512], in_=PA.buf[:, 0:512])
            v.drain()
            v_done.inc(v.nop())
        PA.rel(v_done)
        transp([(evb[:, j * 128:(j + 1) * 128], qcT[:, j * 128:(j + 1) * 128])
                for j in range(4)], PTa)
        vwq = v_done.n
        for ch in range(4):
            @block.tensor
            def _(t, ch=ch, vwq=vwq):
                v_done.wait_at(t, vwq)
                PB.acq(t)
                for j in range(4):
                    ins = t.matmul(PB.buf[:, 0:512],
                                   qcT[:, j * 128:(j + 1) * 128],
                                   w_qupb[:, j * 2048 + ch * 512:
                                          j * 2048 + (ch + 1) * 512],
                                   start=(j == 0), stop=(j == 3))
                t_done.inc(ins)
            tv = t_done.n
            @block.vector
            def _(v, ch=ch, tv=tv):
                t_done.wait_at(v, tv)
                v.tensor_copy(out=xt[:, ch * 512:(ch + 1) * 512],
                              in_=PB.buf[:, 0:512])
                v.drain()
                v_done.inc(v.nop())
            PB.rel(v_done)
        rope0(xt, cs0own, tl * 64, 16, 128, ev1)
        @block.vector
        def _(v):
            v.tensor_mul(xsq[:], xt[:], xt[:])
            v.drain()
            v.reduce_sum(ap2(sml, 0, [[16, 128], [1, 16]]),
                         ap2(xsq, 0, [[2048, 128], [128, 16], [1, 128]]),
                         axis=AX.X)
            v.drain()
            v.tensor_scalar(out=sml[:, 0:16], in0=sml[:, 0:16],
                            scalar1=1.0 / 128.0, scalar2=1e-6,
                            op0=OP.mult, op1=OP.add)
            v.drain()
            v_done.inc(v.nop())
        vv = v_done.n
        @block.scalar
        def _(s, vv=vv):
            v_done.wait_at(s, vv)
            s.activation(out=sml[:, 0:16], in_=sml[:, 0:16], func=AF.Ln)
            s.drain()
            s.activation(out=sml[:, 0:16], in_=sml[:, 0:16], func=AF.Exp,
                         scale=-0.5)
            s.drain()
            s_done.inc(s.nop())
        sv = s_done.n
        @block.vector
        def _(v, sv=sv):
            s_done.wait_at(v, sv)
            for hh in range(16):
                v.tensor_scalar(out=xsq[:, hh * 128:(hh + 1) * 128],
                                in0=xt[:, hh * 128:(hh + 1) * 128],
                                scalar1=sml[:, hh:hh + 1], scalar2=None,
                                op0=OP.mult)
            v.drain()
            v.tensor_mul(xnb[:], xsq[:], b_qnw[:])
            v.drain()
            v_done.inc(v.nop())
        for half in range(2):
            transp([(xnb[:, (half * 8 + j) * 128:(half * 8 + j + 1) * 128],
                     qT[:, (half * 8 + j) * 128:(half * 8 + j + 1) * 128])
                    for j in range(8)], PTa if half == 0 else PTb)
        vwh = v_done.n
        for hh in range(16):
            qTh = qT[:, hh * 128:(hh + 1) * 128]
            @block.tensor
            def _(t, qTh=qTh, vwh=vwh):
                v_done.wait_at(t, vwh)
                PC.acq(t)
                ins = t.matmul(PC.buf[:, 0:256], qTh, skvT_b[:, 0:256],
                               start=True, stop=True)
                t_done.inc(ins)
            tv = t_done.n
            @block.vector
            def _(v, tv=tv, tl=tl):
                t_done.wait_at(v, tv)
                v.scalar_tensor_tensor(out=ev1[:, 0:256],
                                       in0=PC.buf[:, 0:256], scalar=SC0,
                                       in1=selmask[:, tl * 256:(tl + 1) * 256],
                                       op0=OP.mult, op1=OP.add)
                v.drain()
                v.reduce_max(sml[:, 0:1], ev1[:, 0:256], axis=AX.X,
                             negate=True)
                v.drain()
                v_done.inc(v.nop())
            PC.rel(v_done)
            vv = v_done.n
            @block.scalar
            def _(s, vv=vv):
                v_done.wait_at(s, vv)
                s.activation(out=evb[:, 0:256], in_=ev1[:, 0:256], func=AF.Exp,
                             bias=sml[:, 0:1], accum_out=sml[:, 1:2])
                s.drain()
                s_done.inc(s.nop())
            sv = s_done.n
            @block.tensor
            def _(t, sv=sv):
                s_done.wait_at(t, sv)
                PTa.acq(t)
                ins = t.transpose(PTa.buf[:, 0:128], evb[:, 0:128], c_eye_b[:])
                ins = t.transpose(PTa.buf[:, 128:256], evb[:, 128:256],
                                  c_eye_b[:])
                t_done.inc(ins)
            tv = t_done.n
            @block.vector
            def _(v, tv=tv, sv=sv):
                t_done.wait_at(v, tv)
                s_done.wait_at(v, sv)
                v.tensor_copy(out=ptb[:, 0:256], in_=PTa.buf[:, 0:256])
                v.drain()
                v.reciprocal(sml[:, 2:3], sml[:, 1:2])
                v.drain()
                v.tensor_mul(sml[:, 3:4], sml[:, 2:3], spflag[:, tl:tl + 1])
                v.drain()
                v_done.inc(v.nop())
            PTa.rel(v_done)
            vv = v_done.n
            @block.tensor
            def _(t, vv=vv):
                v_done.wait_at(t, vv)
                PD.acq(t)
                ins = t.matmul(PD.buf[:, 0:128], ptb[:, 0:128],
                               skv_b[:, 0:128], start=True, stop=False)
                ins = t.matmul(PD.buf[:, 0:128], ptb[:, 128:256],
                               skv_b[:, 128:256], start=False, stop=True)
                t_done.inc(ins)
            tv = t_done.n
            @block.vector
            def _(v, tv=tv, hh=hh):
                t_done.wait_at(v, tv)
                v.tensor_scalar(out=attnac[:, hh * 128:(hh + 1) * 128],
                                in0=PD.buf[:, 0:128], scalar1=sml[:, 3:4],
                                scalar2=None, op0=OP.mult)
                v.drain()
                v_done.inc(v.nop())
            PD.rel(v_done)
            @block.tensor
            def _(t, qTh=qTh, tl=tl):
                PE2.acq(t)
                ins = t.matmul(PE2.buf[:, 0:128], qTh,
                               swkT_o[:, tl * 128:(tl + 1) * 128],
                               start=True, stop=True)
                ins = t.matmul(PE2.buf[:, 128:256], qTh,
                               swkT_p[:, tl * 128:(tl + 1) * 128],
                               start=True, stop=True)
                ins = t.matmul(PE2.buf[:, 256:384], qTh,
                               swkT_o[:, (tl + 1) * 128:(tl + 2) * 128],
                               start=True, stop=True)
                t_done.inc(ins)
            tv = t_done.n
            wmask = b_wm0 if tl == 0 else b_wmN
            @block.vector
            def _(v, tv=tv, wmask=wmask):
                t_done.wait_at(v, tv)
                v.scalar_tensor_tensor(out=ev1[:, 0:384],
                                       in0=PE2.buf[:, 0:384], scalar=SC0,
                                       in1=wmask[:, 0:384],
                                       op0=OP.mult, op1=OP.add)
                v.drain()
                v.reduce_max(sml[:, 4:5], ev1[:, 0:384], axis=AX.X,
                             negate=True)
                v.drain()
                v_done.inc(v.nop())
            PE2.rel(v_done)
            vv = v_done.n
            @block.scalar
            def _(s, vv=vv):
                v_done.wait_at(s, vv)
                s.activation(out=evb[:, 0:384], in_=ev1[:, 0:384], func=AF.Exp,
                             bias=sml[:, 4:5], accum_out=sml[:, 5:6])
                s.drain()
                s_done.inc(s.nop())
            sv = s_done.n
            @block.tensor
            def _(t, sv=sv):
                s_done.wait_at(t, sv)
                PTb.acq(t)
                for j in range(3):
                    ins = t.transpose(PTb.buf[:, j * 128:(j + 1) * 128],
                                      evb[:, j * 128:(j + 1) * 128], c_eye_b[:])
                t_done.inc(ins)
            tv = t_done.n
            @block.vector
            def _(v, tv=tv):
                t_done.wait_at(v, tv)
                v.tensor_copy(out=ptb[:, 0:384], in_=PTb.buf[:, 0:384])
                v.drain()
                v.reciprocal(sml[:, 6:7], sml[:, 5:6])
                v.drain()
                v_done.inc(v.nop())
            PTb.rel(v_done)
            vv = v_done.n
            @block.tensor
            def _(t, vv=vv, tl=tl):
                v_done.wait_at(t, vv)
                PF.acq(t)
                ins = t.matmul(PF.buf[:, 0:128], ptb[:, 0:128],
                               swv_o[:, tl * 128:(tl + 1) * 128],
                               start=True, stop=False)
                ins = t.matmul(PF.buf[:, 0:128], ptb[:, 128:256],
                               swv_p[:, tl * 128:(tl + 1) * 128],
                               start=False, stop=False)
                ins = t.matmul(PF.buf[:, 0:128], ptb[:, 256:384],
                               swv_o[:, (tl + 1) * 128:(tl + 2) * 128],
                               start=False, stop=True)
                t_done.inc(ins)
            tv = t_done.n
            @block.vector
            def _(v, tv=tv, hh=hh):
                t_done.wait_at(v, tv)
                v.scalar_tensor_tensor(
                    out=attnac[:, hh * 128:(hh + 1) * 128],
                    in0=PF.buf[:, 0:128], scalar=sml[:, 6:7],
                    in1=attnac[:, hh * 128:(hh + 1) * 128],
                    op0=OP.mult, op1=OP.add)
                v.drain()
                v_done.inc(v.nop())
            PF.rel(v_done)
        for gg in range(4):
            @block.vector
            def _(v, gg=gg):
                v.tensor_copy(out=evb[:, 0:512],
                              in_=attnac[:, gg * 512:(gg + 1) * 512])
                v.drain()
                v_done.inc(v.nop())
            transp([(evb[:, j * 128:(j + 1) * 128],
                     qcT[:, j * 128:(j + 1) * 128]) for j in range(4)], PTa)
            vv = v_done.n
            @block.tensor
            def _(t, gg=gg, vv=vv):
                v_done.wait_at(t, vv)
                PA.acq(t)
                for j in range(4):
                    ins = t.matmul(PA.buf[:, 0:512],
                                   qcT[:, j * 128:(j + 1) * 128],
                                   w_gwb[:, (4 * gg + j) * 512:
                                          (4 * gg + j + 1) * 512],
                                   start=(j == 0), stop=(j == 3))
                t_done.inc(ins)
            tv = t_done.n
            @block.vector
            def _(v, tv=tv):
                t_done.wait_at(v, tv)
                v.tensor_copy(out=evb[:, 0:512], in_=PA.buf[:, 0:512])
                v.drain()
                v_done.inc(v.nop())
            PA.rel(v_done)
            vv = v_done.n
            @block.tensor
            def _(t, vv=vv):
                v_done.wait_at(t, vv)
                PTb.acq(t)
                for j in range(4):
                    ins = t.transpose(PTb.buf[:, j * 128:(j + 1) * 128],
                                      evb[:, j * 128:(j + 1) * 128],
                                      c_eye_b[:])
                t_done.inc(ins)
            tv = t_done.n
            @block.vector
            def _(v, tv=tv, sw=st_dma.n):
                t_done.wait_at(v, tv)
                st_dma.wait_at(v, sw)
                v.tensor_copy(out=xnb[:, 0:512], in_=PTb.buf[:, 0:512])
                v.drain()
                v_done.inc(v.nop())
            PTb.rel(v_done)
            vv = v_done.n
            for j in range(4):
                storeg(ap2(d_oT, (gg * 4 + j) * 128 * 2048 + tl * 128,
                           [[2048, 128], [1, 128]]),
                       xnb[:, j * 128:(j + 1) * 128], wait=[(v_done, vv)])

    barrier()
    sbfree("attnac", "ptb", "qcT", "qT", "b_wmN", "b_wm0", "b_qnw",
           "w_gwb", "w_qupb", "w_qcb")
    sbfree("spm", "selmask", "b_spflag", "b_cs0own", "tau", "skv_b",
           "skvT_b", "icompT_b", "icompT", "kcompT", "swv_p", "swkT_p",
           "swv_o", "swkT_o")

    oTsb = sba("oTsb", [128, 16 * 2048], BF16)
    fwb = sba("fwb", [128, 16 * 512], BF16)

    for ft in range(16):
        loadq(oTsb[:, ft * 2048:(ft + 1) * 2048],
              ap2(d_oT, ft * 128 * 2048, [[2048, 128], [1, 2048]]))
    barrier()

    for ch in range(4):
        for kt in range(16):
            loadg(fwb[:, kt * 512:(kt + 1) * 512],
                  io["final_w"].ap()[kt * 128:(kt + 1) * 128,
                                     ch * 512:(ch + 1) * 512],
                  wait=[(t_done, t_done.n)])
        gv = g_dma.n
        for tl in range(16):
            @block.tensor
            def _(t, tl=tl, gv=gv):
                g_dma.wait_at(t, gv)
                PA.acq(t)
                for kt in range(16):
                    ins = t.matmul(PA.buf[:, 0:512],
                                   oTsb[:, kt * 2048 + tl * 128:
                                        kt * 2048 + (tl + 1) * 128],
                                   fwb[:, kt * 512:(kt + 1) * 512],
                                   start=(kt == 0), stop=(kt == 15))
                t_done.inc(ins)
            tv = t_done.n
            loadq(ev1[:, 0:512],
                  io["x_own"].ap()[tl * 128:(tl + 1) * 128,
                                   ch * 512:(ch + 1) * 512],
              wait=[(v_done, v_done.n)])
            qv = q_dma.n
            @block.vector
            def _(v, tv=tv, qv=qv, sw=st_dma.n):
                t_done.wait_at(v, tv)
                q_dma.wait_at(v, qv)
                st_dma.wait_at(v, sw)
                v.tensor_add(ev2[:, 0:512], PA.buf[:, 0:512], ev1[:, 0:512])
                v.drain()
                v_done.inc(v.nop())
            PA.rel(v_done)
            vv = v_done.n
            storeg(ap2(d_x1, tl * 128 * 2048 + ch * 512,
                       [[2048, 128], [1, 512]]),
                   ev2[:, 0:512], wait=[(v_done, vv)])

    barrier()
    sbfree("fwb", "oTsb")

    if stop == "x0a":
        dump_and_stop(d_x1, 2048, 2048)
        return nc, io, out

    xn0Th = sba("xn0Th", [128, 16 * 1024], BF16)
    hT = sba("hT", [128, 43 * 1024], BF16)

    @block.gpsimd
    def _(g):
        cc_sem.wait_at(g, 1)
        g_done.inc(g.nop())
    gv0 = g_done.n
    @block.sync
    def _(q):
        g_done.wait_at(q, gv0)

    def pack_g(shard, off, row0, nrows, col0, ncols, rowstride):
        return ap2(d_pack, shard * PACK + off + row0 * rowstride + col0,
                   [[rowstride, nrows], [1, ncols]])

    for half in range(2):
        for kt in range(16):
            loadq(xn0Th[:, kt * 1024:(kt + 1) * 1024],
                  ap2(d_xn0T, kt * 128 * 2048 + half * 1024,
                      [[2048, 128], [1, 1024]]),
                  wait=[(t_done, t_done.n)])
        with (
            nc.sbuf_tensor(f"wg_c{half}", [128, 16 * 512], BF16) as wg_c,
            nc.sbuf_tensor(f"wu_c{half}", [128, 16 * 512], BF16) as wu_c,
        ):
            for oc in range(11):
                ocn = 512 if oc < 10 else 384
                for s8 in range(8):
                    for r2 in range(2):
                        kt = s8 * 2 + r2
                        loadq(wg_c[:, kt * 512:kt * 512 + ocn],
                              pack_g(s8, OFF_G, r2 * 128, 128, oc * 512,
                                     ocn, 5504),
                              wait=[(t_done, t_done.n)])
                        loadq(wu_c[:, kt * 512:kt * 512 + ocn],
                              pack_g(s8, OFF_U, r2 * 128, 128, oc * 512,
                                     ocn, 5504))
                qv = q_dma.n
                for tt in range(8):
                    @block.tensor
                    def _(t, tt=tt, ocn=ocn, qv=qv):
                        q_dma.wait_at(t, qv)
                        PA.acq(t)
                        for kt in range(16):
                            ins = t.matmul(
                                PA.buf[:, 0:ocn],
                                xn0Th[:, kt * 1024 + tt * 128:
                                      kt * 1024 + (tt + 1) * 128],
                                wg_c[:, kt * 512:kt * 512 + ocn],
                                start=(kt == 0), stop=(kt == 15))
                        t_done.inc(ins)
                        PB.acq(t)
                        for kt in range(16):
                            ins = t.matmul(
                                PB.buf[:, 0:ocn],
                                xn0Th[:, kt * 1024 + tt * 128:
                                      kt * 1024 + (tt + 1) * 128],
                                wu_c[:, kt * 512:kt * 512 + ocn],
                                start=(kt == 0), stop=(kt == 15))
                        t_done.inc(ins)
                    tv = t_done.n
                    @block.scalar
                    def _(s, tv=tv, ocn=ocn):
                        t_done.wait_at(s, tv)
                        s.activation(out=ev1[:, 0:ocn], in_=PA.buf[:, 0:ocn],
                                     func=AF.Silu)
                        s.drain()
                        s_done.inc(s.nop())
                    sv = s_done.n
                    @block.vector
                    def _(v, sv=sv, ocn=ocn):
                        s_done.wait_at(v, sv)
                        v.tensor_mul(evb[:, 0:ocn], ev1[:, 0:ocn],
                                     PB.buf[:, 0:ocn])
                        v.drain()
                        v_done.inc(v.nop())
                    PA.rel(v_done)
                    PB.rel(v_done)
                    nj = ocn // 128
                    transp([(evb[:, j * 128:(j + 1) * 128],
                             hT[:, (oc * 4 + j) * 1024 + tt * 128:
                                (oc * 4 + j) * 1024 + (tt + 1) * 128])
                            for j in range(nj)], PTa)
        with nc.sbuf_tensor(f"wd_c{half}", [128, 43 * 512], BF16) as wd_c:
            for oc2 in range(4):
                for rt in range(43):
                    for s2 in range(2):
                        s8 = oc2 * 2 + s2
                        loadq(wd_c[:, rt * 512 + s2 * 256:
                                   rt * 512 + (s2 + 1) * 256],
                              pack_g(s8, OFF_D, rt * 128, 128, 0, 256, 256),
                              wait=[(t_done, t_done.n)])
                qv = q_dma.n
                vwd = v_done.n
                for tt in range(8):
                    @block.tensor
                    def _(t, tt=tt, qv=qv, vwd=vwd):
                        q_dma.wait_at(t, qv)
                        v_done.wait_at(t, vwd)
                        PC.acq(t)
                        for rt in range(43):
                            ins = t.matmul(
                                PC.buf[:, 0:512],
                                hT[:, rt * 1024 + tt * 128:
                                   rt * 1024 + (tt + 1) * 128],
                                wd_c[:, rt * 512:(rt + 1) * 512],
                                start=(rt == 0), stop=(rt == 42))
                        t_done.inc(ins)
                    tv = t_done.n
                    tl = half * 8 + tt
                    loadq(ev1[:, 0:512],
                          ap2(d_x1, tl * 128 * 2048 + oc2 * 512,
                              [[2048, 128], [1, 512]]),
                          wait=[(v_done, v_done.n)])
                    qv2 = q_dma.n
                    @block.vector
                    def _(v, tv=tv, qv2=qv2, sw=st_dma.n):
                        t_done.wait_at(v, tv)
                        q_dma.wait_at(v, qv2)
                        st_dma.wait_at(v, sw)
                        v.tensor_add(ev2[:, 0:512], PC.buf[:, 0:512],
                                     ev1[:, 0:512])
                        v.drain()
                        v_done.inc(v.nop())
                    PC.rel(v_done)
                    vv = v_done.n
                    storeg(ap2(d_x1, tl * 128 * 2048 + oc2 * 512,
                               [[2048, 128], [1, 512]]),
                           ev2[:, 0:512], wait=[(v_done, vv)])

    barrier()
    sbfree("hT", "xn0Th")


    if stop == "x1":
        dump_and_stop(d_x1, 2048, 2048)
        return nc, io, out

    w_kva = sba("w_kva", [128, 16 * 576], BF16)
    ckvb = sba("ckvb", [128, 576], F32)
    for kt in range(16):
        loadg(w_kva[:, kt * 576:(kt + 1) * 576],
              io["Wkva"].ap()[kt * 128:(kt + 1) * 128, :])
    barrier()

    for tl in range(16):
        loadq(xt[:], ap2(d_x1, tl * 128 * 2048, [[2048, 128], [1, 2048]]),
              wait=[(v_done, v_done.n)])
        layernorm(xt, 2048, xnb, xsq, sml, wait_dma=(q_dma, q_dma.n))
        for half in range(2):
            items = [(xnb[:, (half * 8 + j) * 128:(half * 8 + j + 1) * 128],
                      xnTtmp[:, (half * 8 + j) * 128:(half * 8 + j + 1) * 128])
                     for j in range(8)]
            transp(items, PTa if half == 0 else PTb)
        vv = v_done.n
        storeg(ap2(d_xn0T, tl * 128,
                   [[2048, 128], [128 * 2048, 16], [1, 128]]),
               ap2(xnTtmp, 0, [[2048, 128], [128, 16], [1, 128]]),
               wait=[(v_done, vv)])
        @block.tensor
        def _(t, vv=vv):
            v_done.wait_at(t, vv)
            PA.acq(t)
            for kt in range(16):
                ins = t.matmul(PA.buf[:, 0:512],
                               xnTtmp[:, kt * 128:(kt + 1) * 128],
                               w_kva[:, kt * 576:kt * 576 + 512],
                               start=(kt == 0), stop=(kt == 15))
            t_done.inc(ins)
            PB.acq(t)
            for kt in range(16):
                ins = t.matmul(PB.buf[:, 0:64],
                               xnTtmp[:, kt * 128:(kt + 1) * 128],
                               w_kva[:, kt * 576 + 512:(kt + 1) * 576],
                               start=(kt == 0), stop=(kt == 15))
            t_done.inc(ins)
        tv = t_done.n
        @block.vector
        def _(v, tv=tv, sw=st_dma.n):
            t_done.wait_at(v, tv)
            st_dma.wait_at(v, sw)
            v.tensor_copy(out=ckvb[:, 0:512], in_=PA.buf[:, 0:512])
            v.tensor_copy(out=ckvb[:, 512:576], in_=PB.buf[:, 0:64])
            v.drain()
            v_done.inc(v.nop())
        PA.rel(v_done)
        PB.rel(v_done)
        vv = v_done.n
        storeg(ap2(d_ckv_src, tl * 128 * 576, [[576, 128], [1, 576]]),
               ckvb[:, 0:576], wait=[(v_done, vv)])

    stv = st_dma.n
    @block.gpsimd
    def _(g):
        st_dma.wait_at(g, stv)
        ins = g.collective_compute(
            "AllGather", OP.bypass,
            replica_groups=[[0, 1], [2, 3], [4, 5], [6, 7]],
            ins=[d_ckv_src.ap().opt()], outs=[d_ckv.ap().opt()])
        cc_sem2.inc(ins)

    barrier()
    sbfree("w_kva", "ckvb")

    if stop == "ckv":
        dump_and_stop(d_ckv_src, 2048, 576)
        return nc, io, out

    w_qa = sba("w_qa", [128, 16 * 1536], BF16)
    qaln = sba("qaln", [128, 1536], F32)
    for kt in range(16):
        loadg(w_qa[:, kt * 1536:(kt + 1) * 1536],
              io["Wqa"].ap()[kt * 128:(kt + 1) * 128, :])
    loadq(qaln[:], io["qalnw_rep"].ap())
    barrier()

    xn_rdy = [xn_load(0, 0)]
    for tl in range(16):
        bo = (tl % 2) * 2048
        if tl + 1 < 16:
            xn_rdy.append(xn_load(tl + 1, (tl + 1) % 2))
        qv = xn_rdy[tl]
        for ch in range(3):
            P = (PA, PB, PC)[ch]
            @block.tensor
            def _(t, ch=ch, P=P, bo=bo, qv=qv):
                xn_dma.wait_at(t, qv)
                P.acq(t)
                for kt in range(16):
                    ins = t.matmul(P.buf[:, 0:512],
                                   xnTd[:, bo + kt * 128:
                                        bo + (kt + 1) * 128],
                                   w_qa[:, kt * 1536 + ch * 512:
                                        kt * 1536 + (ch + 1) * 512],
                                   start=(kt == 0), stop=(kt == 15))
                t_done.inc(ins)
            tv = t_done.n
            @block.vector
            def _(v, ch=ch, P=P, tv=tv):
                t_done.wait_at(v, tv)
                v.tensor_copy(out=xt[:, ch * 512:(ch + 1) * 512],
                              in_=P.buf[:, 0:512])
                v.drain()
                v_done.inc(v.nop())
            P.rel(v_done)
        rmsnorm_rows(xt, 1536, xnb, xsq, sml, wrep=qaln[:],
                     out_slice=xnb[:, 0:1536], x_slice=xt[:, 0:1536])
        transp([(xnb[:, j * 128:(j + 1) * 128],
                 xnTtmp[:, j * 128:(j + 1) * 128]) for j in range(8)], PTa)
        transp([(xnb[:, (8 + j) * 128:(9 + j) * 128],
                 xnTtmp[:, (8 + j) * 128:(9 + j) * 128]) for j in range(4)],
               PTb)
        vv = v_done.n
        storeg(ap2(d_qaT, tl * 128,
                   [[2048, 128], [128 * 2048, 12], [1, 128]]),
               ap2(xnTtmp, 0, [[2048, 128], [128, 12], [1, 128]]),
               wait=[(v_done, vv)])

    barrier()
    sbfree("w_qa", "qaln")

    w_qb = sba("w_qb", [128, 12 * 3072], BF16)
    cs1qb = sba("cs1qb", [128, 1024], F32)
    q3 = sba("q3", [128, 3072], F32)
    qb3 = sba("qb3", [128, 3072], BF16)
    qTn = sba("qTn", [128, 2048], BF16)
    qTp = sba("qTp", [64, 2048], BF16)
    for kt in range(12):
        loadg(w_qb[:, kt * 3072:(kt + 1) * 3072],
              io["Wqb"].ap()[kt * 128:(kt + 1) * 128, :])
    loadq(cs1qb[:], io["cs1q"].ap())
    barrier()

    def qa_load(tl, bi):
        tw_ = t_done.n
        @block.sync
        def _(q, tl=tl, bi=bi, tw_=tw_):
            t_done.wait_at(q, tw_)
            xn_dma.inc(q.dma_start(
                out=xnTd[:, bi * 2048:bi * 2048 + 1536],
                in_=ap2(d_qaT, tl * 128,
                        [[2048, 128], [128 * 2048, 12], [1, 128]])), 16)
        return xn_dma.n

    xn_rdy = [qa_load(0, 0)]
    for tl in range(16):
        bo = (tl % 2) * 2048
        if tl + 1 < 16:
            xn_rdy.append(qa_load(tl + 1, (tl + 1) % 2))
        qv = xn_rdy[tl]
        for ch in range(6):
            P = PA if ch % 2 == 0 else PB
            @block.tensor
            def _(t, ch=ch, P=P, bo=bo, qv=qv):
                xn_dma.wait_at(t, qv)
                P.acq(t)
                for j in range(12):
                    ins = t.matmul(P.buf[:, 0:512],
                                   xnTd[:, bo + j * 128:bo + (j + 1) * 128],
                                   w_qb[:, j * 3072 + ch * 512:
                                        j * 3072 + (ch + 1) * 512],
                                   start=(j == 0), stop=(j == 11))
                t_done.inc(ins)
            tv = t_done.n
            @block.vector
            def _(v, ch=ch, P=P, tv=tv):
                t_done.wait_at(v, tv)
                v.tensor_copy(out=q3[:, ch * 512:(ch + 1) * 512],
                              in_=P.buf[:, 0:512])
                v.drain()
                v_done.inc(v.nop())
            P.rel(v_done)
        rope0(q3, cs1qb, tl * 64, 16, 192, ev1, base=128)
        @block.vector
        def _(v):
            v.tensor_copy(out=qb3[:], in_=q3[:])
            v.drain()
            v_done.inc(v.nop())
        for half in range(2):
            transp([(qb3[:, (half * 8 + j) * 192:(half * 8 + j) * 192 + 128],
                     qTn[:, (half * 8 + j) * 128:(half * 8 + j + 1) * 128])
                    for j in range(8)], PTa)
            vvh = v_done.n
            @block.tensor
            def _(t, half=half, vvh=vvh):
                v_done.wait_at(t, vvh)
                PTb.acq(t)
                for j in range(8):
                    hh = half * 8 + j
                    ins = t.transpose(PTb.buf[0:64, j * 128:(j + 1) * 128],
                                      qb3[:, hh * 192 + 128:hh * 192 + 192],
                                      c_eye_b[:])
                t_done.inc(ins)
            tv = t_done.n
            @block.vector
            def _(v, half=half, tv=tv, sw=st_dma.n):
                t_done.wait_at(v, tv)
                st_dma.wait_at(v, sw)
                v.tensor_copy(out=qTp[:, half * 1024:(half + 1) * 1024],
                              in_=PTb.buf[0:64, 0:1024])
                v.drain()
                v_done.inc(v.nop())
            PTb.rel(v_done)
        vv = v_done.n
        for hh in range(16):
            storeg(ap2(d_qT, hh * 192 * 2048 + tl * 128,
                       [[2048, 128], [1, 128]]),
                   qTn[:, hh * 128:(hh + 1) * 128], wait=[(v_done, vv)])
            storeg(ap2(d_qT, hh * 192 * 2048 + 128 * 2048 + tl * 128,
                       [[2048, 64], [1, 128]]),
                   qTp[:, hh * 128:(hh + 1) * 128], wait=[(v_done, vv)])

    barrier()
    sbfree("w_qb", "cs1qb", "q3", "qb3", "qTn", "qTp")

    kpeT = sba("kpeT", [64, 4096], BF16)
    w_kvbn = sba("w_kvbn", [128, 4 * 2048], BF16)
    w_kvbv = sba("w_kvbv", [128, 4 * 2048], BF16)
    kvaln = sba("kvaln", [128, 512], F32)
    ckvnT = sba("ckvnT", [128, 4 * 512], BF16)
    csb = sba("csb", [128, 64], F32)
    for cc in range(4):
        loadg(w_kvbn[:, cc * 2048:(cc + 1) * 2048],
              io["Wkvb_n"].ap()[cc * 128:(cc + 1) * 128, :])
        loadg(w_kvbv[:, cc * 2048:(cc + 1) * 2048],
              io["Wkvb_v"].ap()[cc * 128:(cc + 1) * 128, :])
    loadq(kvaln[:], io["kvalnw_rep"].ap())
    @block.gpsimd
    def _(g):
        g.wait_ge(cc_sem2.h, 1)
        g_done.inc(g.nop())
    gv_ckv = g_done.n
    barrier()

    for g_grp in range(8):
        for gi in range(4):
            gg = g_grp * 4 + gi
            par, t_loc = gg % 2, gg // 2
            loadq(xt[:, 0:576],
                  ap2(d_ckv, par * 2048 * 576 + t_loc * 128 * 576,
                      [[576, 128], [1, 576]]),
                  wait=[(g_done, gv_ckv), (v_done, v_done.n)])
            loadq(csb[:], io["cs1k"].ap()[gg * 128:(gg + 1) * 128, :],
                  wait=[(v_done, v_done.n)])
            qv = q_dma.n
            @block.vector
            def _(v, qv=qv):
                q_dma.wait_at(v, qv)
            rope0(xt, csb, 0, 1, 0, ev1, base=512)
            @block.vector
            def _(v):
                v.tensor_copy(out=evb[:, 0:64], in_=xt[:, 512:576])
                v.drain()
                v_done.inc(v.nop())
            vv = v_done.n
            @block.tensor
            def _(t, vv=vv):
                v_done.wait_at(t, vv)
                PTa.acq(t)
                ins = t.transpose(PTa.buf[0:64, 0:128], evb[:, 0:64],
                                  c_eye_b[:])
                t_done.inc(ins)
            tv = t_done.n
            @block.vector
            def _(v, tv=tv, gg=gg):
                t_done.wait_at(v, tv)
                v.tensor_copy(out=kpeT[:, gg * 128:(gg + 1) * 128],
                              in_=PTa.buf[0:64, 0:128])
                v.drain()
                v_done.inc(v.nop())
            PTa.rel(v_done)
            rmsnorm_rows(xt, 512, xnb, xsq, sml, wrep=kvaln[:],
                         out_slice=xnb[:, 0:512], x_slice=xt[:, 0:512])
            transp([(xnb[:, c * 128:(c + 1) * 128],
                     ckvnT[:, c * 512 + gi * 128:c * 512 + gi * 128 + 128])
                    for c in range(4)], PTb)
        vwk = v_done.n
        for hh in range(16):
            @block.tensor
            def _(t, hh=hh, vwk=vwk):
                v_done.wait_at(t, vwk)
                PA.acq(t)
                for cc in range(4):
                    ins = t.matmul(PA.buf[:, 0:512],
                                   w_kvbn[:, cc * 2048 + hh * 128:
                                          cc * 2048 + (hh + 1) * 128],
                                   ckvnT[:, cc * 512:(cc + 1) * 512],
                                   start=(cc == 0), stop=(cc == 3))
                t_done.inc(ins)
            tv = t_done.n
            slot = hh % 2
            @block.scalar
            def _(s, tv=tv, slot=slot, sw=st_dma.n):
                t_done.wait_at(s, tv)
                st_dma.wait_at(s, sw)
                s.activation(out=xnb[:, slot * 512:(slot + 1) * 512],
                             in_=PA.buf[:, 0:512], func=AF.Copy)
                s.drain()
                s_done.inc(s.nop())
            PA.rel(s_done)
            sv = s_done.n
            storeg(ap2(d_kT, hh * 128 * 4096 + g_grp * 512,
                       [[4096, 128], [1, 512]]),
                   xnb[:, slot * 512:(slot + 1) * 512],
                   wait=[(s_done, sv)])
        for gi in range(4):
            for vg in range(4):
                @block.tensor
                def _(t, gi=gi, vg=vg, vwk=vwk):
                    v_done.wait_at(t, vwk)
                    PB.acq(t)
                    for cc in range(4):
                        ins = t.matmul(
                            PB.buf[:, 0:512],
                            ckvnT[:, cc * 512 + gi * 128:
                                  cc * 512 + gi * 128 + 128],
                            w_kvbv[:, cc * 2048 + vg * 512:
                                   cc * 2048 + (vg + 1) * 512],
                            start=(cc == 0), stop=(cc == 3))
                    t_done.inc(ins)
                tv = t_done.n
                slot = vg % 2
                @block.scalar
                def _(s, tv=tv, slot=slot, sw=st_dma.n):
                    t_done.wait_at(s, tv)
                    st_dma.wait_at(s, sw)
                    s.activation(out=xnb[:, 1024 + slot * 512:
                                         1024 + (slot + 1) * 512],
                                 in_=PB.buf[:, 0:512], func=AF.Copy)
                    s.drain()
                    s_done.inc(s.nop())
                PB.rel(s_done)
                sv = s_done.n
                gg = g_grp * 4 + gi
                for j in range(4):
                    hh = vg * 4 + j
                    storeg(ap2(d_V, hh * 4096 * 128 + gg * 128 * 128,
                               [[128, 128], [1, 128]]),
                           xnb[:, 1024 + slot * 512 + j * 128:
                               1024 + slot * 512 + (j + 1) * 128],
                           wait=[(s_done, sv)])

    barrier()
    sbfree("w_kvbn", "w_kvbv", "kvaln", "ckvnT", "csb")

    if stop == "kv":
        dump_and_stop(d_kT, 2048, 2048)
        return nc, io, out

    l1ma = sba("l1ma", [128, 512], F32)
    l1mb = sba("l1mb", [128, 512], F32)
    KTg = sba("KTg", [128, 4 * 4096], BF16)
    Vg = sba("Vg", [128, 4 * 4096], BF16)
    QTn10 = sba("QTn10", [128, 4 * 2048], BF16)
    QTp10 = sba("QTp10", [64, 4 * 2048], BF16)
    scrs = sba("scrs", [128, 4096], F32)
    pexp = sba("pexp", [128, 4096], BF16)
    pts = sba("pts", [128, 256], BF16)
    loadq(l1ma[:], io["l1mA"].ap())
    loadq(l1mb[:], io["l1mB"].ap())
    barrier()

    for hg in range(4):
        for hl in range(4):
            hh = hg * 4 + hl
            w8 = [(t_done, t_done.n)]
            loadq(KTg[:, hl * 4096:(hl + 1) * 4096],
                  ap2(d_kT, hh * 128 * 4096, [[4096, 128], [1, 4096]]),
                  wait=w8)
            loadq(Vg[:, hl * 4096:(hl + 1) * 4096],
                  ap2(d_V, hh * 4096 * 128,
                      [[128, 128], [16384, 32], [1, 128]]))
            loadq(QTn10[:, hl * 2048:(hl + 1) * 2048],
                  ap2(d_qT, hh * 192 * 2048, [[2048, 128], [1, 2048]]))
            loadq(QTp10[:, hl * 2048:(hl + 1) * 2048],
                  ap2(d_qT, hh * 192 * 2048 + 128 * 2048,
                      [[2048, 64], [1, 2048]]))
        qv_grp = q_dma.n
        for tl in range(16):
            nch = (2 * tl + 1) // 4 + 1
            NK = nch * 512
            for hl in range(4):
                hh = hg * 4 + hl
                for ch in range(nch):
                    P = PA if ch % 2 == 0 else PB
                    @block.tensor
                    def _(t, P=P, ch=ch, hl=hl, tl=tl, qv_grp=qv_grp):
                        q_dma.wait_at(t, qv_grp)
                        P.acq(t)
                        ins = t.matmul(P.buf[:, 0:512],
                                       QTn10[:, hl * 2048 + tl * 128:
                                             hl * 2048 + (tl + 1) * 128],
                                       KTg[:, hl * 4096 + ch * 512:
                                           hl * 4096 + (ch + 1) * 512],
                                       start=True, stop=False)
                        ins = t.matmul(P.buf[:, 0:512],
                                       QTp10[:, hl * 2048 + tl * 128:
                                             hl * 2048 + (tl + 1) * 128],
                                       kpeT[:, ch * 512:(ch + 1) * 512],
                                       start=False, stop=True)
                        t_done.inc(ins)
                    tv = t_done.n
                    if ch < nch - 1:
                        @block.scalar
                        def _(s, P=P, ch=ch, tv=tv):
                            t_done.wait_at(s, tv)
                            s.activation(out=scrs[:, ch * 512:(ch + 1) * 512],
                                         in_=P.buf[:, 0:512], func=AF.Copy,
                                         scale=SC1)
                            s.drain()
                            s_done.inc(s.nop())
                        P.rel(s_done)
                    else:
                        msk = l1ma if tl % 2 == 0 else l1mb
                        @block.vector
                        def _(v, P=P, ch=ch, tv=tv, msk=msk, sv=s_done.n):
                            t_done.wait_at(v, tv)
                            s_done.wait_at(v, sv)
                            v.scalar_tensor_tensor(
                                out=scrs[:, ch * 512:(ch + 1) * 512],
                                in0=P.buf[:, 0:512], scalar=SC1, in1=msk[:],
                                op0=OP.mult, op1=OP.add)
                            v.drain()
                            v_done.inc(v.nop())
                        P.rel(v_done)
                @block.vector
                def _(v, NK=NK, sv=s_done.n):
                    s_done.wait_at(v, sv)
                    v.reduce_max(sml[:, 0:1], scrs[:, 0:NK], axis=AX.X,
                                 negate=True)
                    v.drain()
                    v_done.inc(v.nop())
                vv = v_done.n
                @block.scalar
                def _(s, NK=NK, vv=vv, tw=t_done.n):
                    v_done.wait_at(s, vv)
                    t_done.wait_at(s, tw)
                    s.activation(out=pexp[:, 0:NK], in_=scrs[:, 0:NK],
                                 func=AF.Exp, bias=sml[:, 0:1],
                                 accum_out=sml[:, 1:2])
                    s.drain()
                    s_done.inc(s.nop())
                sv_exp = s_done.n
                @block.vector
                def _(v, sv_exp=sv_exp):
                    s_done.wait_at(v, sv_exp)
                    v.reciprocal(sml[:, 2:3], sml[:, 1:2])
                    v.drain()
                    v_done.inc(v.nop())
                vv_rcp = v_done.n
                nkt = 4 * nch
                pv_t = []
                for kt in range(nkt):
                    PT = PTa if kt % 2 == 0 else PTb
                    @block.tensor
                    def _(t, PT=PT, kt=kt, sv_exp=sv_exp):
                        if kt == 0:
                            s_done.wait_at(t, sv_exp)
                        PT.acq(t)
                        ins = t.transpose(PT.buf[:, 0:128],
                                          pexp[:, kt * 128:(kt + 1) * 128],
                                          c_eye_b[:])
                        t_done.inc(ins)
                    tv_tr = t_done.n
                    @block.scalar
                    def _(s, kt=kt, PT=PT, tv_tr=tv_tr,
                          pvw=(pv_t[kt - 2] if kt >= 2 else 0)):
                        t_done.wait_at(s, tv_tr)
                        if pvw:
                            t_done.wait_at(s, pvw)
                        s.activation(out=pts[:, (kt % 2) * 128:
                                             (kt % 2 + 1) * 128],
                                     in_=PT.buf[:, 0:128], func=AF.Copy)
                        s.drain()
                        s_done.inc(s.nop())
                    PT.rel(s_done)
                    sv_c = s_done.n
                    @block.tensor
                    def _(t, kt=kt, hl=hl, sv_c=sv_c, nkt=nkt):
                        s_done.wait_at(t, sv_c)
                        if kt == 0:
                            PC.acq(t)
                        ins = t.matmul(PC.buf[0:128, 0:128],
                                       pts[:, (kt % 2) * 128:
                                           (kt % 2 + 1) * 128],
                                       Vg[:, hl * 4096 + kt * 128:
                                          hl * 4096 + (kt + 1) * 128],
                                       start=(kt == 0), stop=(kt == nkt - 1))
                        t_done.inc(ins)
                    pv_t.append(t_done.n)
                @block.vector
                def _(v, tv=pv_t[-1], vv_rcp=vv_rcp):
                    t_done.wait_at(v, tv)
                    v.tensor_scalar(out=evb[:, 0:128],
                                    in0=PC.buf[0:128, 0:128],
                                    scalar1=sml[:, 2:3], scalar2=None,
                                    op0=OP.mult)
                    v.drain()
                    v_done.inc(v.nop())
                PC.rel(v_done)
                vv = v_done.n
                @block.tensor
                def _(t, vv=vv):
                    v_done.wait_at(t, vv)
                    PTa.acq(t)
                    ins = t.transpose(PTa.buf[:, 128:256], evb[:, 0:128],
                                      c_eye_b[:])
                    t_done.inc(ins)
                tv = t_done.n
                @block.vector
                def _(v, tv=tv, sw=st_dma.n):
                    t_done.wait_at(v, tv)
                    st_dma.wait_at(v, sw)
                    v.tensor_copy(out=evb[:, 128:256],
                                  in_=PTa.buf[:, 128:256])
                    v.drain()
                    v_done.inc(v.nop())
                PTa.rel(v_done)
                vv = v_done.n
                storeg(ap2(d_oT, hh * 128 * 2048 + tl * 128,
                           [[2048, 128], [1, 128]]),
                       evb[:, 128:256], wait=[(v_done, vv)])

    barrier()
    sbfree("l1ma", "l1mb", "KTg", "Vg", "QTn10", "QTp10", "scrs", "pexp",
           "pts")
    sbfree("kpeT")

    w_o = sba("w_o", [128, 16 * 2048], BF16)
    oTt = sba("oTt", [128, 2048], BF16)
    for kt in range(16):
        loadg(w_o[:, kt * 2048:(kt + 1) * 2048],
              io["Wo"].ap()[kt * 128:(kt + 1) * 128, :])
    barrier()

    for tl in range(16):
        for hh in range(16):
            loadq(oTt[:, hh * 128:(hh + 1) * 128],
                  ap2(d_oT, hh * 128 * 2048 + tl * 128,
                      [[2048, 128], [1, 128]]),
                  wait=([(t_done, t_done.n)] if hh == 0 else ()))
        qv = q_dma.n
        for oc in range(4):
            P = PA if oc % 2 == 0 else PB
            @block.tensor
            def _(t, P=P, oc=oc, qv=qv):
                q_dma.wait_at(t, qv)
                P.acq(t)
                for hh in range(16):
                    ins = t.matmul(P.buf[:, 0:512],
                                   oTt[:, hh * 128:(hh + 1) * 128],
                                   w_o[:, hh * 2048 + oc * 512:
                                       hh * 2048 + (oc + 1) * 512],
                                   start=(hh == 0), stop=(hh == 15))
                t_done.inc(ins)
            tv = t_done.n
            loadq(ev1[:, 0:512],
                  ap2(d_x1, tl * 128 * 2048 + oc * 512,
                      [[2048, 128], [1, 512]]),
                  wait=[(v_done, v_done.n)])
            qv2 = q_dma.n
            @block.vector
            def _(v, P=P, tv=tv, qv2=qv2, sw=st_dma.n):
                t_done.wait_at(v, tv)
                q_dma.wait_at(v, qv2)
                st_dma.wait_at(v, sw)
                v.tensor_add(ev2[:, 0:512], P.buf[:, 0:512], ev1[:, 0:512])
                v.drain()
                v_done.inc(v.nop())
            P.rel(v_done)
            vv = v_done.n
            storeg(ap2(d_x1, tl * 128 * 2048 + oc * 512,
                       [[2048, 128], [1, 512]]),
                   ev2[:, 0:512], wait=[(v_done, vv)])

    barrier()
    sbfree("w_o", "oTt")
    sbfree("xnTd")

    if stop == "xattn":
        dump_and_stop(d_x1, 2048, 2048)
        return nc, io, out

    SQ2048 = float(np.sqrt(2048.0))
    FLR2 = float(1e-5 * np.sqrt(4096.0) * 127.0 * 127.0)
    KAL = float(1.0 / (64.0 * 127.0 ** 3))
    mstat = sba("mstat", [128, 64], F32)
    y1 = sba("y1", [128, 8192], F32)
    y2 = sba("y2", [128, 8192], F32)
    qi32 = sba("qi32", [128, 8192], I32)
    sbc = sba("sbc", [128, 256], F32)
    xng = sba("xng", [128, 4096], BF16)
    xqg = sba("xqg", [128, 8192], BF16)

    def butterflies(nc_blocks, stages, width):
        @block.vector
        def _(v, sv=s_done.n):
            s_done.wait_at(v, sv)
            cur, oth = y1, y2
            for hstep in stages:
                for b0 in range(0, nc_blocks, 2 * hstep):
                    for k in range(hstep):
                        cA, cB = b0 + k, b0 + k + hstep
                        v.tensor_add(oth[:, cA * width:(cA + 1) * width],
                                     cur[:, cA * width:(cA + 1) * width],
                                     cur[:, cB * width:(cB + 1) * width])
                        v.tensor_sub(oth[:, cB * width:(cB + 1) * width],
                                     cur[:, cA * width:(cA + 1) * width],
                                     cur[:, cB * width:(cB + 1) * width])
                v.drain()
                cur, oth = oth, cur
            v_done.inc(v.nop())
        return y1 if len(stages) % 2 == 0 else y2

    def quant_group(g, nblk, ybuf, scr, flr_mode, width=256):
        @block.vector
        def _(v, g=g, nblk=nblk):
            half = nblk // 2

            def tree(op):
                for k in range(half):
                    v.tensor_tensor(out=scr[:, k * width:(k + 1) * width],
                                    in0=ybuf[:, k * width:(k + 1) * width],
                                    in1=ybuf[:, (k + half) * width:
                                             (k + half + 1) * width],
                                    op=op)
                v.drain()
                base, n = 0, half
                while n > 1:
                    for k in range(n // 2):
                        v.tensor_tensor(
                            out=scr[:, (base + n + k) * width:
                                    (base + n + k + 1) * width],
                            in0=scr[:, (base + k) * width:
                                    (base + k + 1) * width],
                            in1=scr[:, (base + k + n // 2) * width:
                                    (base + k + n // 2 + 1) * width],
                            op=op)
                    v.drain()
                    base += n
                    n //= 2
                return base
            v.drain()
            maxb = tree(OP.max)
            v.tensor_copy(out=ev2[:, 0:width],
                          in_=scr[:, maxb * width:(maxb + 1) * width])
            v.drain()
            minb = tree(OP.min)
            v.tensor_scalar(out=ev1[:, 0:width],
                            in0=scr[:, minb * width:(minb + 1) * width],
                            scalar1=-1.0, scalar2=None, op0=OP.mult)
            v.drain()
            v.tensor_tensor(out=ev2[:, 0:width], in0=ev1[:, 0:width],
                            in1=ev2[:, 0:width], op=OP.max)
            v.drain()
            v_done.inc(v.nop())
        vv = v_done.n
        @block.tensor
        def _(t, vv=vv):
            v_done.wait_at(t, vv)
            PC.acq(t)
            ins = t.transpose(PC.buf[:, 0:128], ev2[:, 0:128], c_eye_f[:])
            ins = t.transpose(PC.buf[:, 128:256], ev2[:, 128:256],
                              c_eye_f[:])
            t_done.inc(ins)
        tv = t_done.n
        @block.vector
        def _(v, g=g, tv=tv, flr_mode=flr_mode):
            t_done.wait_at(v, tv)
            for b in range(2):
                v.reduce_max(sml[:, 8 + b:9 + b],
                             PC.buf[:, b * 128:(b + 1) * 128], axis=AX.X)
            v.drain()
            for b in range(2):
                tile = 2 * g + b
                if flr_mode == 1:
                    v.tensor_scalar(out=mstat[:, tile:tile + 1],
                                    in0=sml[:, 8 + b:9 + b],
                                    scalar1=1.0 / SQ2048, scalar2=1e-5,
                                    op0=OP.mult, op1=OP.max)
                    v.drain()
                    v.reciprocal(sml[:, 10 + b:11 + b],
                                 mstat[:, tile:tile + 1])
                    v.drain()
                    v.tensor_scalar(out=sml[:, 12 + b:13 + b],
                                    in0=sml[:, 10 + b:11 + b],
                                    scalar1=127.0 / SQ2048, scalar2=None,
                                    op0=OP.mult)
                else:
                    v.tensor_scalar(out=sml[:, 10 + b:11 + b],
                                    in0=mstat[:, tile:tile + 1],
                                    scalar1=c_scales[:, 0:1], scalar2=None,
                                    op0=OP.mult)
                    v.drain()
                    v.tensor_mul(sml[:, 12 + b:13 + b],
                                 sml[:, 10 + b:11 + b],
                                 sml[:, 10 + b:11 + b])
                    v.drain()
                    v.reciprocal(sml[:, 10 + b:11 + b],
                                 sml[:, 12 + b:13 + b])
                    v.drain()
                    v.tensor_scalar(out=sml[:, 12 + b:13 + b],
                                    in0=sml[:, 10 + b:11 + b],
                                    scalar1=FLR2, scalar2=None, op0=OP.mult)
                    v.drain()
                    v.tensor_tensor(out=mstat[:, 16 + tile:17 + tile],
                                    in0=sml[:, 8 + b:9 + b],
                                    in1=sml[:, 12 + b:13 + b], op=OP.max)
                    v.drain()
                    v.reciprocal(sml[:, 10 + b:11 + b],
                                 mstat[:, 16 + tile:17 + tile])
                    v.drain()
                    v.tensor_scalar(out=sml[:, 12 + b:13 + b],
                                    in0=sml[:, 10 + b:11 + b],
                                    scalar1=127.0, scalar2=None, op0=OP.mult)
            v.drain()
            v_done.inc(v.nop())
        PC.rel(v_done)
        vv = v_done.n
        @block.tensor
        def _(t, vv=vv):
            v_done.wait_at(t, vv)
            PD.acq(t)
            ins = t.transpose(PD.buf[0:1, 0:128], sml[:, 12:13], c_eye_f[:])
            ins = t.transpose(PD.buf[0:1, 128:256], sml[:, 13:14],
                              c_eye_f[:])
            t_done.inc(ins)
        tv = t_done.n
        @block.vector
        def _(v, tv=tv):
            t_done.wait_at(v, tv)
            v.tensor_copy(out=ev1[0:1, 0:256], in_=PD.buf[0:1, 0:256])
            v.drain()
            v_done.inc(v.nop())
        PD.rel(v_done)
        vv = v_done.n
        @block.tensor
        def _(t, vv=vv):
            v_done.wait_at(t, vv)
            PE2.acq(t)
            ins = t.matmul(PE2.buf[:, 0:256], c_ones[0:1, 0:128],
                           ev1[0:1, 0:256], start=True, stop=True)
            t_done.inc(ins)
        tv = t_done.n
        @block.vector
        def _(v, tv=tv):
            t_done.wait_at(v, tv)
            v.tensor_copy(out=sbc[:], in_=PE2.buf[:, 0:256])
            v.drain()
            v_done.inc(v.nop())
        PE2.rel(v_done)

    def rne_to_bf16(nblk, ybuf, width=256):
        @block.vector
        def _(v, nblk=nblk, sw=st_dma.n):
            st_dma.wait_at(v, sw)
            for c in range(nblk):
                v.tensor_tensor(out=qi32[:, c * width:(c + 1) * width],
                                in0=ybuf[:, c * width:(c + 1) * width],
                                in1=sbc[:], op=OP.mult)
            v.drain()
            v.tensor_copy(out=y2[:, 0:nblk * width],
                          in_=qi32[:, 0:nblk * width])
            v.drain()
            v.tensor_copy(out=xqg[:, 0:nblk * width],
                          in_=y2[:, 0:nblk * width])
            v.drain()
            v_done.inc(v.nop())
        return v_done.n

    for g in range(8):
        loadq(xng[:], ap2(d_xn0T, g * 256,
                          [[2048, 128], [128 * 2048, 16], [1, 256]]),
              wait=[(t_done, t_done.n)])
        qv = q_dma.n
        for c in range(16):
            P = PA if c % 2 == 0 else PB
            @block.tensor
            def _(t, c=c, P=P, qv=qv):
                if c == 0:
                    q_dma.wait_at(t, qv)
                P.acq(t)
                ins = t.matmul(P.buf[:, 0:256], c_had_b[:],
                               xng[:, c * 256:(c + 1) * 256],
                               start=True, stop=True)
                t_done.inc(ins)
            tv = t_done.n
            @block.scalar
            def _(s, c=c, P=P, tv=tv, vw=v_done.n):
                t_done.wait_at(s, tv)
                if c == 0:
                    v_done.wait_at(s, vw)
                s.activation(out=y1[:, c * 256:(c + 1) * 256],
                             in_=P.buf[:, 0:256], func=AF.Copy)
                s.drain()
                s_done.inc(s.nop())
            P.rel(s_done)
        fin = butterflies(16, [1, 2, 4, 8], 256)
        assert fin is y1
        quant_group(g, 16, y1, y2, flr_mode=1)
        vv = rne_to_bf16(16, y1)
        storeg(ap2(d_xq1, g * 256,
                   [[2048, 128], [128 * 2048, 16], [1, 256]]),
               ap2(xqg, 0, [[8192, 128], [256, 16], [1, 256]]),
               wait=[(v_done, vv)])

    barrier()

    if stop == "xq1":
        dump_and_stop2(d_xq1, [[1, 2048 * 2048]])
        return nc, io, out

    wupb = sba("wupb", [128, 2 * 8192], BF16)
    xqs = sba("xqs", [128, 16 * 512], BF16)
    h2b = sba("h2b", [128, 1024], BF16)
    tdh = []
    for tg in range(4):
        loadq(xqs[:], ap2(d_xq1, tg * 512,
                          [[2048, 128], [128 * 2048, 16], [1, 512]]),
              wait=[(t_done, t_done.n)])
        qv_x = q_dma.n
        for iblk in range(8):
            dslot = iblk % 2
            wv = ([(t_done, tdh[-2])] if len(tdh) >= 2
                  else [(t_done, t_done.n)])
            for kt in range(16):
                s8, r2 = kt // 2, kt % 2
                loadq(wupb[:, dslot * 8192 + kt * 512:
                           dslot * 8192 + (kt + 1) * 512],
                      pack_g(s8, OFF_TU, r2 * 128, 128, iblk * 512, 512,
                             4096),
                      wait=(wv if kt == 0 else ()))
            qv = q_dma.n
            for isub in range(4):
                P = PA if isub % 2 == 0 else PB
                @block.tensor
                def _(t, P=P, isub=isub, dslot=dslot, qv=qv, qv_x=qv_x):
                    q_dma.wait_at(t, qv)
                    q_dma.wait_at(t, qv_x)
                    P.acq(t)
                    for kt in range(16):
                        ins = t.matmul(
                            P.buf[:, 0:512],
                            wupb[:, dslot * 8192 + kt * 512 + isub * 128:
                                 dslot * 8192 + kt * 512 + (isub + 1) * 128],
                            xqs[:, kt * 512:(kt + 1) * 512],
                            start=(kt == 0), stop=(kt == 15))
                    t_done.inc(ins)
                tv = t_done.n
                @block.scalar
                def _(s, P=P, tv=tv, vw=v_done.n):
                    t_done.wait_at(s, tv)
                    v_done.wait_at(s, vw)
                    s.activation(out=ev1[:, 0:512], in_=P.buf[:, 0:512],
                                 func=AF.Relu)
                    s.drain()
                    s_done.inc(s.nop())
                P.rel(s_done)
                sv = s_done.n
                slot = isub % 2
                @block.vector
                def _(v, sv=sv, slot=slot, sw=st_dma.n):
                    s_done.wait_at(v, sv)
                    st_dma.wait_at(v, sw)
                    v.tensor_mul(h2b[:, slot * 512:(slot + 1) * 512],
                                 ev1[:, 0:512], ev1[:, 0:512])
                    v.drain()
                    v_done.inc(v.nop())
                vv = v_done.n
                ic = iblk * 4 + isub
                storeg(ap2(d_h2T, ic * 128 * 2048 + tg * 512,
                           [[2048, 128], [1, 512]]),
                       h2b[:, slot * 512:(slot + 1) * 512],
                       wait=[(v_done, vv)])
            tdh.append(t_done.n)

    barrier()
    sbfree("wupb", "xqs", "h2b")

    if stop in ("h2a", "h2b"):
        off = 0 if stop == "h2a" else 2048 * 2048
        @block.gpsimd
        def _(g, off=off):
            st_dma.inc(g.dma_start(out=ap2(out, 0, [[1, 2048 * 2048]]),
                                   in_=ap2(d_h2T, off, [[1, 2048 * 2048]])),
                       16)
        barrier()
        return nc, io, out

    h2g = sba("h2g", [128, 8192], BF16)
    for g in range(8):
        loadq(h2g[:], ap2(d_h2T, g * 256,
                          [[2048, 128], [128 * 2048, 32], [1, 256]]),
              wait=[(t_done, t_done.n)])
        qv = q_dma.n
        for c in range(32):
            P = PA if c % 2 == 0 else PB
            @block.tensor
            def _(t, c=c, P=P, qv=qv):
                if c == 0:
                    q_dma.wait_at(t, qv)
                P.acq(t)
                ins = t.matmul(P.buf[:, 0:256], c_had_b[:],
                               h2g[:, c * 256:(c + 1) * 256],
                               start=True, stop=True)
                t_done.inc(ins)
            tv = t_done.n
            @block.scalar
            def _(s, c=c, P=P, tv=tv, vw=v_done.n):
                t_done.wait_at(s, tv)
                if c == 0:
                    v_done.wait_at(s, vw)
                s.activation(out=y1[:, c * 256:(c + 1) * 256],
                             in_=P.buf[:, 0:256], func=AF.Copy)
                s.drain()
                s_done.inc(s.nop())
            P.rel(s_done)
        fin = butterflies(32, [1, 2, 4, 8, 16], 256)
        assert fin is y2
        quant_group(g, 32, y2, y1, flr_mode=2)
        vv = rne_to_bf16(32, y2)
        storeg(ap2(d_xq2, g * 256,
                   [[2048, 128], [128 * 2048, 32], [1, 256]]),
               ap2(xqg, 0, [[8192, 128], [256, 32], [1, 256]]),
               wait=[(v_done, vv)])

    barrier()
    sbfree("y1", "y2", "qi32", "sbc", "xng", "xqg", "h2g")

    if stop in ("xq2a", "xq2b"):
        off = 0 if stop == "xq2a" else 2048 * 2048
        @block.gpsimd
        def _(g, off=off):
            st_dma.inc(g.dma_start(out=ap2(out, 0, [[1, 2048 * 2048]]),
                                   in_=ap2(d_xq2, off, [[1, 2048 * 2048]])),
                       16)
        barrier()
        return nc, io, out

    wdnb = sba("wdnb", [128, 32 * 512], BF16)
    xq2t = sba("xq2t", [128, 2 * 32 * 128], BF16)
    @block.vector
    def _(v):
        v.tensor_scalar(out=mstat[:, 32:48], in0=mstat[:, 0:16],
                        scalar1=c_scales[:, 0:1], scalar2=None, op0=OP.mult)
        v.drain()
        v.tensor_mul(mstat[:, 48:64], mstat[:, 32:48], mstat[:, 32:48])
        v.drain()
        v.tensor_mul(mstat[:, 32:48], mstat[:, 48:64], mstat[:, 16:32])
        v.drain()
        v.tensor_scalar(out=mstat[:, 48:64], in0=mstat[:, 32:48],
                        scalar1=c_scales[:, 1:2], scalar2=KAL,
                        op0=OP.mult, op1=OP.mult)
        v.drain()
        v_done.inc(v.nop())

    tdh2 = []
    for oc in range(4):
        for rt in range(32):
            loadq(wdnb[:, rt * 512:(rt + 1) * 512],
                  pack_g(rt // 4, OFF_TD, (rt % 4) * 128, 128, oc * 512,
                         512, 2048),
                  wait=([(t_done, t_done.n)] if rt == 0 else ()))
        for tl in range(16):
            xslot = tl % 2
            wvx = ([(t_done, tdh2[-2])] if len(tdh2) >= 2
                   else [(t_done, t_done.n)])
            loadq(xq2t[:, xslot * 4096:(xslot + 1) * 4096],
                  ap2(d_xq2, tl * 128,
                      [[2048, 128], [128 * 2048, 32], [1, 128]]),
                  wait=wvx)
            qv = q_dma.n
            P = PA if tl % 2 == 0 else PB
            @block.tensor
            def _(t, P=P, qv=qv, xslot=xslot):
                q_dma.wait_at(t, qv)
                P.acq(t)
                for rt in range(32):
                    ins = t.matmul(P.buf[:, 0:512],
                                   xq2t[:, xslot * 4096 + rt * 128:
                                        xslot * 4096 + (rt + 1) * 128],
                                   wdnb[:, rt * 512:(rt + 1) * 512],
                                   start=(rt == 0), stop=(rt == 31))
                t_done.inc(ins)
            tdh2.append(t_done.n)
            tv = t_done.n
            loadq(ev1[:, 0:512],
                  ap2(d_x1, tl * 128 * 2048 + oc * 512,
                      [[2048, 128], [1, 512]]),
                  wait=[(v_done, v_done.n)])
            qv2 = q_dma.n
            @block.vector
            def _(v, P=P, tl=tl, tv=tv, qv2=qv2, sw=st_dma.n):
                t_done.wait_at(v, tv)
                q_dma.wait_at(v, qv2)
                st_dma.wait_at(v, sw)
                v.scalar_tensor_tensor(out=ev2[:, 0:512],
                                       in0=P.buf[:, 0:512],
                                       scalar=mstat[:, 48 + tl:49 + tl],
                                       in1=ev1[:, 0:512],
                                       op0=OP.mult, op1=OP.add)
                v.drain()
                v_done.inc(v.nop())
            P.rel(v_done)
            vv = v_done.n
            storeg(out.ap()[tl * 128:(tl + 1) * 128,
                            oc * 512:(oc + 1) * 512],
                   ev2[:, 0:512], wait=[(v_done, vv)])

    barrier()
    return nc, io, out


def _host_consts():
    eye = np.eye(128, dtype=np.float32)
    had = np.empty((128, 128), np.float32)
    for i in range(128):
        for j in range(128):
            had[i, j] = 1.0 if bin(i & j).count("1") % 2 == 0 else -1.0
    inv = (1.0 / (10000.0 ** (np.arange(0, 64, 2, dtype=np.float32)
                              / np.float32(64.0)))).astype(np.float32)
    pos = np.arange(4096, dtype=np.float32)
    ang = pos[:, None] * inv[None, :]
    cosf = np.cos(ang).astype(np.float32)
    sinf = np.sin(ang).astype(np.float32)
    cs1k = np.concatenate([cosf, sinf], axis=1)
    return eye, had, cosf, sinf, cs1k


def _core_tables(h, cosf, sinf):
    NEGM = np.float32(-1.0e30)
    p = np.arange(128)
    cs_own = np.zeros((128, 1024), np.float32)
    cs_prv = np.zeros((128, 1024), np.float32)
    for t in range(16):
        g = 2 * t + h
        posn = g * 128 + p
        cs_own[:, t * 64:t * 64 + 32] = cosf[posn]
        cs_own[:, t * 64 + 32:t * 64 + 64] = sinf[posn]
        if g - 1 >= 0:
            posp = (g - 1) * 128 + p
            cs_prv[:, t * 64:t * 64 + 32] = cosf[posp]
            cs_prv[:, t * 64 + 32:t * 64 + 64] = sinf[posp]
    spmask = np.zeros((128, 4096), np.float32)
    spflag = np.zeros((128, 16), np.float32)
    blk_end = np.arange(256) * 16 + 15
    for t in range(16):
        posn = (2 * t + h) * 128 + p
        valid = blk_end[None, :] < posn[:, None]
        spmask[:, t * 256:(t + 1) * 256] = np.where(valid, 0.0, NEGM)
        spflag[:, t] = (posn >= 16).astype(np.float32)
    j = np.arange(384)
    base = (p[:, None] + 1 <= j[None, :]) & (j[None, :] <= p[:, None] + 256)
    wmN = np.where(base, 0.0, NEGM).astype(np.float32)
    wm0 = np.where(base & (j[None, :] >= 256 - 128 * h), 0.0,
                   NEGM).astype(np.float32)
    def l1mask(d0):
        m = np.zeros((128, 512), np.float32)
        for st in range(4):
            if st < d0:
                continue
            elif st == d0:
                cc = np.arange(128)
                m[:, st * 128:(st + 1) * 128] = np.where(
                    cc[None, :] <= p[:, None], 0.0, NEGM)
            else:
                m[:, st * 128:(st + 1) * 128] = NEGM
        return m
    return (cs_own, cs_prv, spmask, spflag, wm0, wmN,
            l1mask(h), l1mask(h + 2))


def kernel(**inputs):
    fi = {k: np.asarray(v) for k, v in inputs.items()}
    eye, had, cosf, sinf, cs1k = _host_consts()
    f32 = np.float32
    x = fi["L0_attn_h"].astype(f32, copy=False)
    lut = fi["L0_ffn_lut"].astype(f32, copy=False)
    lutcols = np.broadcast_to(lut[None, :], (128, 16)).copy()
    qnw = np.broadcast_to(np.tile(fi["L0_attn_q_norm_w"].astype(f32), 16)
                          [None, :], (128, 2048)).copy()
    knw_rep = np.broadcast_to(fi["L0_attn_k_norm_w"].astype(f32)[None, :],
                              (128, 128)).copy()
    knw_col = fi["L0_attn_k_norm_w"].astype(f32).reshape(128, 1).copy()
    qalnw = np.broadcast_to(fi["L1_attn_q_a_ln_w"].astype(f32)[None, :],
                            (128, 1536)).copy()
    kvalnw = np.broadcast_to(fi["L1_attn_kv_a_ln_w"].astype(f32)[None, :],
                             (128, 512)).copy()
    kvcb = np.tile(fi["L0_attn_kvc_bias"].astype(f32).T, (1, 8)).copy()
    idxb = np.tile(fi["L0_attn_idx_c_bias"].astype(f32).T, (1, 8)).copy()
    gidxT = np.ascontiguousarray(fi["L0_ffn_gate_idx"].astype(np.int32).T)
    uidxT = np.ascontiguousarray(fi["L0_ffn_up_idx"].astype(np.int32).T)
    didxT_full = np.ascontiguousarray(fi["L0_ffn_down_idx"]
                                      .astype(np.int32).T)
    wupT_full = np.ascontiguousarray(fi["L1_ffn_w_up"].astype(f32).T)
    wdnT_full = np.ascontiguousarray(fi["L1_ffn_w_down"].astype(f32).T)
    Wkvb = fi["L1_attn_Wkvb"].astype(f32).reshape(512, 16, 256)
    Wkvb_n = np.ascontiguousarray(Wkvb[:, :, :128].reshape(512, 2048))
    Wkvb_v = np.ascontiguousarray(Wkvb[:, :, 128:].reshape(512, 2048))
    group_w = np.ascontiguousarray(
        fi["L0_attn_group_w"].astype(f32).reshape(2048, 512))

    common = {
        "w_qc": fi["L0_attn_w_qc"].astype(f32, copy=False),
        "w_qup": fi["L0_attn_w_qup"].astype(f32, copy=False),
        "kvc_w": fi["L0_attn_kvc_w"].astype(f32, copy=False),
        "kvc_wz": fi["L0_attn_kvc_wz"].astype(f32, copy=False),
        "k_proj_w": fi["L0_attn_k_proj_w"].astype(f32, copy=False),
        "v_proj_w": fi["L0_attn_v_proj_w"].astype(f32, copy=False),
        "idx_c_w": fi["L0_attn_idx_c_w"].astype(f32, copy=False),
        "idx_c_wz": fi["L0_attn_idx_c_wz"].astype(f32, copy=False),
        "w_dqT": np.ascontiguousarray(fi["L0_attn_w_dq"].astype(f32).T),
        "w_iuq": fi["L0_attn_w_iuq"].astype(f32, copy=False),
        "w_w": fi["L0_attn_w_w"].astype(f32, copy=False),
        "group_w": group_w,
        "final_w": fi["L0_attn_final_w"].astype(f32, copy=False),
        "Wqa": fi["L1_attn_Wqa"].astype(f32, copy=False),
        "Wqb": fi["L1_attn_Wqb"].astype(f32, copy=False),
        "Wkva": fi["L1_attn_Wkva"].astype(f32, copy=False),
        "Wkvb_n": Wkvb_n,
        "Wkvb_v": Wkvb_v,
        "Wo": fi["L1_attn_Wo"].astype(f32, copy=False),
        "w_up": fi["L1_ffn_w_up"].astype(f32, copy=False),
        "w_down": fi["L1_ffn_w_down"].astype(f32, copy=False),
        "lutcols": lutcols,
        "eye_f": eye,
        "had_f": had,
        "qnw_rep": qnw,
        "knw_rep": knw_rep,
        "knw_col": knw_col,
        "qalnw_rep": qalnw,
        "kvalnw_rep": kvalnw,
        "kvcb_rep": kvcb,
        "idxb_rep": idxb,
        "cs1k": cs1k,
    }

    percore_h = {}
    for h in range(2):
        (cs_own, cs_prv, spmask, spflag, wm0, wmN, l1mA,
         l1mB) = _core_tables(h, cosf, sinf)
        percore_h[h] = {
            "cs0own": cs_own, "cs0prv": cs_prv, "cs1q": cs_own,
            "spmask": spmask, "spflag": spflag, "wm0": wm0, "wmN": wmN,
            "l1mA": l1mA, "l1mB": l1mB,
        }

    maps = []
    for c in range(8):
        s, h = c // 2, c % 2
        xs = x[s]
        own = np.concatenate(
            [xs[(2 * t + h) * 128:(2 * t + h + 1) * 128] for t in range(16)])
        prv_tiles = []
        for t in range(16):
            g = 2 * t + h - 1
            if g < 0:
                prv_tiles.append(np.zeros((128, 2048), f32))
            else:
                prv_tiles.append(xs[g * 128:(g + 1) * 128])
        prv = np.concatenate(prv_tiles)
        m = dict(common)
        m.update(percore_h[h])
        m.update({
            "x_own": np.ascontiguousarray(own),
            "x_prev": np.ascontiguousarray(prv),
            "x_full": np.ascontiguousarray(xs),
            "gidxT": np.ascontiguousarray(gidxT[c * 256:(c + 1) * 256]),
            "uidxT": np.ascontiguousarray(uidxT[c * 256:(c + 1) * 256]),
            "didxT": np.ascontiguousarray(
                didxT_full[:, c * 256:(c + 1) * 256]).reshape(688, 2048),
            "wupT": np.ascontiguousarray(wupT_full[c * 256:(c + 1) * 256]),
            "wdnT": np.ascontiguousarray(wdnT_full[c * 512:(c + 1) * 512]),
        })
        maps.append(m)

    nc, io, out = build_program(stop=_STOP)
    res = run_bass_kernel_spmd(nc, maps, core_ids=list(range(8)))

    y = np.zeros((4, 4096, 2048), np.float32)
    for c in range(8):
        s, h = c // 2, c % 2
        oc = res.results[c]["out"]
        for t in range(16):
            y[s, (2 * t + h) * 128:(2 * t + h + 1) * 128] = \
                oc[t * 128:(t + 1) * 128]
    if _STOP is not None:
        return [res.results[c]["out"] for c in range(8)]
    return y


_STOP = None


if __name__ == "__main__":
    nc, io, out = build_program()
    print("BUILD OK")
